# version 31
# baseline (speedup 1.0000x reference)
import contextlib
import numpy as np
import ml_dtypes
import concourse.bass as bass
import concourse.mybir as mybir
from concourse.bass import IndirectOffsetOnAxis
from concourse.bass_utils import run_bass_kernel_spmd

F32 = mybir.dt.float32
BF16 = mybir.dt.bfloat16
I32 = mybir.dt.int32
U8 = mybir.dt.uint8
AF = mybir.ActivationFunctionType
ALU = mybir.AluOpType
AX = mybir.AxisListType

E = 2048
NCH = 16
NTOK = 1024
H, D, L = 16, 128, 64
NT = 32
NEG = -1e30
EPS = 1e-5


DEBUG_DUMP = False

_UID = [0]


def uid():
    _UID[0] += 1
    return _UID[0]


class Sem:
    def __init__(self, h):
        self.h = h
        self.n = 0

    def inc(self, k=1):
        self.n += k
        return self.n



def arr_lhsT(w):
    kin, mout = w.shape
    return np.ascontiguousarray(
        w.reshape(kin // 128, 128, mout).transpose(1, 0, 2)
        .reshape(128, (kin // 128) * mout))


def arr_lhsT_q(w):
    kin, mout = w.shape
    nk = kin // 128
    mq = mout // 4
    a = w.reshape(nk, 128, 4, mq).transpose(1, 2, 0, 3)
    return np.ascontiguousarray(a.reshape(128, 4, nk * mq))


def owned_tiles(p):
    return [p + 4 * j for j in range(8)]


def owned_rows(p):
    return np.concatenate(
        [np.arange(128 * t, 128 * t + 128) for t in owned_tiles(p)])


def blockorder(p):
    return [r for r in range(4) if r != p] + [p]


def slot_of(p, g):
    return 4 * (g // 4) + blockorder(p).index(g % 4)


def make_masks(p):
    sp = np.arange(128)[:, None]
    qc = np.arange(128)[None, :]
    bandm = np.zeros((128, 4 * 256), np.uint8)
    fixm = np.zeros((128, 4 * 256), np.float32)
    for jj, r in enumerate(blockorder(p)):
        for w in range(2):
            dt = (p - r) if w == 0 else (p + 4 - r)
            diff = 128 * dt + qc - sp
            band = (diff >= 0) & (diff <= 64)
            cols = slice(256 * jj + 128 * w, 256 * jj + 128 * w + 128)
            bandm[:, cols] = band.astype(np.uint8)
            fixm[:, cols] = np.where(diff < 0, NEG, 0.0)
    return bandm, fixm.astype(ml_dtypes.bfloat16)


def make_maskneg(p):
    i = np.arange(128)[:, None]
    jc = np.arange(128)[None, :]
    mn = np.full((128, 2 * 256), NEG, np.float32)
    for slot in range(2):
        for w in range(2):
            if slot == 0 and p == 0 and w == 0:
                continue
            diff = (128 - 128 * w) + i - jc
            band = (diff >= 0) & (diff <= 64)
            cols = slice(256 * slot + 128 * w, 256 * slot + 128 * w + 128)
            mn[:, cols] = np.where(band, 0.0, NEG)
    return mn


def arrange_x(xb, p):
    xt = xb[owned_rows(p)].T
    return np.ascontiguousarray(
        xt.reshape(NCH, 128, NTOK).transpose(1, 0, 2).reshape(128, NCH * NTOK))


def unarrange_x(y):
    return np.ascontiguousarray(
        y.reshape(128, NCH, NTOK).transpose(1, 0, 2).reshape(E, NTOK).T)


def qlo_of(s):
    return 128 * (s // 4)


def pc_delta(i):
    qlo = qlo_of(i)
    return (512 * (i % 2) - qlo) if NTOK - qlo <= 512 else 0


def split_ranges(lo, hi, cuts):
    pts = sorted({lo, hi} | {c for c in cuts if lo < c < hi})
    return list(zip(pts[:-1], pts[1:]))



def load_fence(block, engines, sem, val):
    for name in engines:
        @getattr(block, name)
        def _(e):
            e.wait_ge(sem, val)


SEM_NAMES = ('ct', 'cs', 'cv', 'cg', 'dsy', 'dgp', 'dconst', 'dlat',
             'dw0', 'dw1', 'dvu0', 'dvu1', 'dx')


def make_sems(nc, e):
    hs = [e(nc.semaphore(n)) for n in SEM_NAMES]
    d = {n: Sem(h) for n, h in zip(SEM_NAMES, hs)}
    return d


def emit_ln(nc, block, bufs, S, src_chunks, dst, wait0=None, mrr=None):
    w0 = wait0 if isinstance(wait0, list) else \
        ([wait0] + [None] * (NCH - 1) if wait0 is not None else [None] * NCH)
    ct, cs, cv = S['ct'], S['cs'], S['cv']
    onesb = bufs['ones_bf']
    u = uid()
    with contextlib.ExitStack() as ps:
        p_sum = ps.enter_context(nc.psum_tensor(f"ln_sum{u}", [1, NTOK], F32))
        p_sq = ps.enter_context(nc.psum_tensor(f"ln_sq{u}", [1, NTOK], F32))
        p_rrep = ps.enter_context(nc.psum_tensor(f"ln_rrep{u}", [128, NTOK], F32))
        sq2 = [ps.enter_context(nc.sbuf_tensor(f"ln_sq2_{u}0", [128, NTOK],
                                               BF16))] * 2
        m_sb = ps.enter_context(nc.sbuf_tensor(f"ln_m{u}", [1, NTOK], F32))
        t2 = ps.enter_context(nc.sbuf_tensor(f"ln_t2{u}", [1, NTOK], F32))
        rstd = ps.enter_context(nc.sbuf_tensor(f"ln_r{u}", [1, NTOK], F32))
        sq_toks = []
        mm_toks = []
        dst_free = bufs.pop('ln_dst_free', None)
        for c in range(NCH):
            par = c % 2
            @block.scalar
            def _(s, c=c, par=par):
                if w0[c] is not None:
                    s.wait_ge(w0[c][0], w0[c][1])
                if c >= 1:
                    s.wait_ge(ct.h, mm_toks[c - 1])
                if dst_free is not None:
                    s.wait_ge(dst_free[0], dst_free[1])
                s.activation(dst[:, NTOK * c:NTOK * (c + 1)], src_chunks[c],
                             AF.Copy).then_inc(cs.h, 1)
                if w0[c] is not None:
                    s.wait_ge(w0[c][0], w0[c][1])
                if c >= 1:
                    s.wait_ge(ct.h, mm_toks[c - 1])
                s.activation(sq2[par][:, :], src_chunks[c],
                             AF.Square).then_inc(cs.h, 1)
            cs.inc(1)
            sq_toks.append(cs.inc())
            @block.tensor
            def _(t, c=c, par=par):
                t.wait_ge(cs.h, sq_toks[c])
                t.matmul(p_sum[:, 0:512], onesb[:, 0:1],
                         dst[:, NTOK * c:NTOK * c + 512],
                         start=(c == 0), stop=(c == NCH - 1))
                t.matmul(p_sum[:, 512:1024], onesb[:, 0:1],
                         dst[:, NTOK * c + 512:NTOK * (c + 1)],
                         start=(c == 0), stop=(c == NCH - 1))
                t.matmul(p_sq[:, 0:512], onesb[:, 0:1], sq2[par][:, 0:512],
                         start=(c == 0), stop=(c == NCH - 1))
                t.matmul(p_sq[:, 512:1024], onesb[:, 0:1],
                         sq2[par][:, 512:1024],
                         start=(c == 0), stop=(c == NCH - 1)).then_inc(ct.h, 1)
            mm_toks.append(ct.inc())
        @block.vector
        def _(v):
            v.wait_ge(ct.h, mm_toks[-1])
            v.tensor_scalar(m_sb[:, :], p_sum[:, :], 1.0 / E,
                            None, ALU.mult).then_inc(cv.h, 1)
            v.wait_ge(cv.h, cv.n + 1)
            v.tensor_tensor(t2[:, :], m_sb[:, :], m_sb[:, :],
                            ALU.mult).then_inc(cv.h, 1)
            v.wait_ge(cv.h, cv.n + 2)
            v.tensor_scalar(t2[:, :], t2[:, :], EPS, None,
                            ALU.subtract).then_inc(cv.h, 1)
            v.wait_ge(cv.h, cv.n + 3)
            v.scalar_tensor_tensor(t2[:, :], p_sq[:, :], 1.0 / E, t2[:, :],
                                   ALU.mult, ALU.subtract).then_inc(cv.h, 1)
        cv.inc(3)
        var_tok = cv.inc()
        @block.scalar
        def _(s):
            s.wait_ge(cv.h, var_tok)
            s.activation(rstd[:, :], t2[:, :], AF.Ln).then_inc(cs.h, 1)
            s.wait_ge(cs.h, cs.n + 1)
            s.activation(t2[:, :], rstd[:, :], AF.Exp,
                         scale=-0.5).then_inc(cs.h, 1)
        cs.inc(1)
        rstd_tok = cs.inc()
        @block.vector
        def _(v):
            v.wait_ge(cs.h, rstd_tok)
            v.scalar_tensor_tensor(mrr[:, :], m_sb[:, :], -1.0, t2[:, :],
                                   ALU.mult, ALU.mult).then_inc(cv.h, 1)
        mr_tok = cv.inc()
        @block.tensor
        def _(t):
            t.wait_ge(cs.h, rstd_tok)
            t.matmul(p_rrep[:, 0:512], bufs['onesr_f'][:, :], t2[:, 0:512],
                     start=True, stop=True)
            t.matmul(p_rrep[:, 512:1024], bufs['onesr_f'][:, :],
                     t2[:, 512:1024], start=True, stop=True).then_inc(ct.h, 1)
        rep_tok = ct.inc()
        last = None
        apply_toks = []
        for c in range(NCH):
            @block.vector
            def _(v, c=c):
                if c == 0:
                    v.wait_ge(ct.h, rep_tok)
                v.tensor_mul(dst[:, NTOK * c:NTOK * (c + 1)], src_chunks[c],
                             p_rrep[:, :]).then_inc(cv.h, 1)
            last = cv.inc()
            apply_toks.append(last)
        load_fence(block, ('tensor', 'scalar'), cv.h, last)
        return last, apply_toks, mr_tok


def emit_big_proj(nc, block, bufs, S, wq_dram, rhs_sb, out_cb,
                  extra_wait=None, rhs_ready=None, fixup=None,
                  colsum=None):
    ct, cs, cv = S['ct'], S['cs'], S['cv']
    dw = [S['dw0'], S['dw1']]
    wslot = bufs['wslot']
    slot_free = [None] * 4
    u = uid()
    with contextlib.ExitStack() as ps:
        P = [ps.enter_context(nc.psum_tensor(f"bp{u}_p{i}", [128, NTOK], F32))
             for i in range(2)]
        cs_ps = ps.enter_context(nc.psum_tensor(f"bp{u}_cs", [1, 512], F32)) \
            if colsum is not None else None
        qload = [None] * 4
        cons_toks = {}

        entry_ct = ct.n
        def emit_load(q):
            @block.gpsimd
            def _(g, q=q):
                if q >= 2:
                    g.wait_ge(ct.h, slot_free[q - 2])
                else:
                    g.wait_ge(ct.h, entry_ct)
                g.dma_start(out=wslot[q % 2][:, :],
                            in_=wq_dram[:, q, :]).then_inc(dw[q % 2].h, 16)
            qload[q] = (dw[q % 2].h, dw[q % 2].inc(16))

        emit_load(0)
        emit_load(1)
        for q in range(4):
            if colsum is not None:
                msum_sb, cs_out = colsum
                @block.tensor
                def _(t, q=q):
                    t.wait_ge(qload[q][0], qload[q][1])
                    wv = wslot[q % 2].rearrange("p (k m) -> p k m", k=NCH)
                    for k in range(NCH):
                        mi = t.matmul(cs_ps[:, :], bufs['ones_bf'][:, 0:1],
                                      wv[:, k, :], start=(k == 0),
                                      stop=(k == NCH - 1))
                    mi.then_inc(ct.h, 1)
                csq_tok = ct.inc()
                @block.vector
                def _(v, q=q, csq_tok=csq_tok, msum_sb=msum_sb):
                    v.wait_ge(ct.h, csq_tok)
                    v.tensor_copy(msum_sb[:, 512 * q:512 * q + 512],
                                  cs_ps[:, :]).then_inc(cv.h, 1)
                cs_out.append(cv.inc())
            for mc in range(4):
                m = 4 * q + mc
                par = m % 2
                @block.tensor
                def _(t, q=q, mc=mc, m=m, par=par):
                    if mc == 0:
                        t.wait_ge(qload[q][0], qload[q][1])
                        if extra_wait is not None:
                            t.wait_ge(extra_wait[0], extra_wait[1])
                        if m == 0 and fixup is not None:
                            t.wait_ge(cv.h, fixup[3])
                    if m >= 2:
                        t.wait_ge(cons_toks[m - 2][0], cons_toks[m - 2][1])
                    wv = wslot[q % 2].rearrange("p (k m) -> p k m", k=NCH)
                    for k in range(NCH):
                        if m == 0 and rhs_ready is not None:
                            t.wait_ge(rhs_ready[k][0], rhs_ready[k][1])
                        lhsT = wv[:, k, 128 * mc:128 * mc + 128]
                        for half in range(2):
                            mi = t.matmul(
                                P[par][:, 512 * half:512 * half + 512],
                                lhsT,
                                rhs_sb[:, NTOK * k + 512 * half:
                                       NTOK * k + 512 * half + 512],
                                start=(k == 0),
                                stop=(fixup is None and k == NCH - 1))
                    if fixup is not None:
                        msum_sb, ms_toks, mrr, _mt = fixup
                        if ms_toks is not None and mc == 0:
                            t.wait_ge(cv.h, ms_toks[q])
                        lhsT = msum_sb[:, 128 * m:128 * m + 128]
                        for half in range(2):
                            mi = t.matmul(
                                P[par][:, 512 * half:512 * half + 512],
                                lhsT,
                                mrr[:, 512 * half:512 * half + 512],
                                start=False, stop=True)
                    mi.then_inc(ct.h, 1)
                mm_tok = ct.inc()
                slot_free[q] = mm_tok
                cons_toks[m] = out_cb(m, P[par], mm_tok)
            if q + 2 <= 3:
                emit_load(q + 2)
        last = cons_toks[NCH - 1]
        load_fence(block, ('tensor', 'scalar'), last[0], last[1])
        return last, slot_free


def emit_ffn(nc, block, bufs, S, x_chunks, xhat, varr_dram, uarr_dram,
             rhs_ready=None, mrr=None, mr_tok=None, first=False, zps=None):
    ct, cs, cv = S['ct'], S['cs'], S['cv']
    dvu = [S['dvu0'], S['dvu1']]
    h_sb = bufs['hbuf']
    wslot = bufs['wslot']

    def h_cb(m, pap, mm_tok):
        @block.scalar
        def _(s, m=m, pap=pap, mm_tok=mm_tok):
            s.wait_ge(ct.h, mm_tok)
            s.activation(h_sb[:, NTOK * m:NTOK * (m + 1)], pap[:, :],
                         AF.Relu, bias=bufs['b1_sb'][:, m:m + 1]).then_inc(cs.h, 1)
        return (cs.h, cs.inc())

    ms_toks = [] if first else None
    wlast, slot_free = emit_big_proj(
        nc, block, bufs, S, bufs['w1q_dram'], xhat, h_cb,
        rhs_ready=rhs_ready,
        fixup=(bufs['msum_w1'], ms_toks, mrr, mr_tok),
        colsum=(bufs['msum_w1'], ms_toks) if first else None)
    @block.gpsimd
    def _(g):
        g.wait_ge(ct.h, slot_free[2])
        g.dma_start(out=wslot[0][:, :], in_=varr_dram[:, :]).then_inc(dvu[0].h, 16)
    v_tok = (dvu[0].h, dvu[0].inc(16))
    @block.gpsimd
    def _(g):
        g.wait_ge(ct.h, slot_free[3])
        g.dma_start(out=wslot[1][:, :], in_=uarr_dram[:, :]).then_inc(dvu[1].h, 16)
    u_tok = (dvu[1].h, dvu[1].inc(16))

    u = uid()
    with contextlib.ExitStack() as ps:
        P = [ps.enter_context(nc.psum_tensor(f"fe{u}_p{i}", [128, NTOK], F32))
             for i in range(2)]
        p_z = zps[0:8, :]
        p_ztr = zps[0:128, 0:64]
        p_a8 = ps.enter_context(nc.psum_tensor(f"ffn_a8{u}", [8, NTOK], BF16))
        z_sb = ps.enter_context(nc.sbuf_tensor(f"ffn_zsb{u}", [8, NTOK], F32))
        zt = ps.enter_context(nc.sbuf_tensor(f"ffn_zt{u}", [128, 64], F32))
        nmax = ps.enter_context(nc.sbuf_tensor(f"ffn_nm{u}", [128, 8], F32))
        ez = ps.enter_context(nc.sbuf_tensor(f"ffn_ez{u}", [128, 64], F32))
        ssum = ps.enter_context(nc.sbuf_tensor(f"ffn_ss{u}", [128, 8], F32))
        rcp = ps.enter_context(nc.sbuf_tensor(f"ffn_rc{u}", [128, 8], F32))
        alpha_tm = ps.enter_context(nc.sbuf_tensor(f"ffn_al{u}", [128, 64], BF16))
        a8 = ps.enter_context(nc.sbuf_tensor(f"ffn_a8s{u}", [8, NTOK], BF16))
        vx = ps.enter_context(nc.sbuf_tensor(f"ffn_vx{u}", [128, 4 * NTOK], BF16))
        @block.tensor
        def _(t, pzw=bufs.get('prev_zt', 0), pkv=bufs.get('prev_kvs', 0)):
            t.wait_ge(cv.h, pzw)
            t.wait_ge(cv.h, pkv)
            w2v = bufs['w2_sb'].rearrange("p (k m) -> p k m", k=NCH)
            for k in range(NCH):
                for half in range(2):
                    mi = t.matmul(p_z[:, 512 * half:512 * half + 512],
                                  w2v[:, k, :],
                                  h_sb[:, NTOK * k + 512 * half:
                                       NTOK * k + 512 * half + 512],
                                  start=(k == 0), stop=(k == NCH - 1))
            mi.then_inc(ct.h, 1)
        z_tok = ct.inc()
        @block.vector
        def _(v):
            v.wait_ge(ct.h, z_tok)
            v.tensor_scalar(z_sb[:, :], p_z[:, :], bufs['b2_sb'][:, 0:1],
                            None, ALU.add).then_inc(cv.h, 1)
        zsb_tok = cv.inc()
        zsb_sem = cv
        @block.tensor
        def _(t):
            t.wait_ge(cv.h, zsb_tok)
            for blk in range(8):
                mi = t.transpose(p_ztr[:, 8 * blk:8 * blk + 8],
                                 z_sb[:, 128 * blk:128 * blk + 128],
                                 bufs['ident_f'][0:8, 0:8])
            mi.then_inc(ct.h, 1)
        ztr_tok = ct.inc()
        @block.vector
        def _(v):
            v.wait_ge(ct.h, ztr_tok)
            v.tensor_copy(zt[:, :], p_ztr[:, :]).then_inc(cv.h, 1)
        zt_tok = cv.inc()
        bufs['prev_zt'] = zt_tok
        @block.vector
        def _(v):
            v.wait_ge(cv.h, zt_tok)
            v.wait_ge(cv.h, zt_tok)
            v.reduce_max(nmax.rearrange("p (b o) -> p b o", o=1)[:, :, :],
                         zt.rearrange("p (b k) -> p b k", k=8)[:, :, :],
                         axis=AX.X, negate=True).then_inc(cv.h, 1)
        nmax_tok = cv.inc()
        @block.scalar
        def _(s):
            s.wait_ge(cv.h, nmax_tok)
            for b in range(8):
                inst = s.activation(ez[:, 8 * b:8 * b + 8],
                                    zt[:, 8 * b:8 * b + 8],
                                    AF.Exp, bias=nmax[:, b:b + 1])
            inst.then_inc(cs.h, 1)
        ez_tok = cs.inc()
        @block.vector
        def _(v):
            v.wait_ge(cs.h, ez_tok)
            v.reduce_sum(ssum.rearrange("p (b o) -> p b o", o=1)[:, :, :],
                         ez.rearrange("p (b k) -> p b k", k=8)[:, :, :],
                         axis=AX.X).then_inc(cv.h, 1)
            v.wait_ge(cv.h, cv.n + 1)
            v.reciprocal(rcp[:, :], ssum[:, :]).then_inc(cv.h, 1)
            v.wait_ge(cv.h, cv.n + 2)
            v.tensor_scalar(rcp[:, :], rcp[:, :], 0.5, None,
                            ALU.mult).then_inc(cv.h, 1)
            v.wait_ge(cv.h, cv.n + 3)
            for b in range(8):
                inst = v.tensor_scalar(alpha_tm[:, 8 * b:8 * b + 8],
                                       ez[:, 8 * b:8 * b + 8], rcp[:, b:b + 1],
                                       None, ALU.mult)
            inst.then_inc(cv.h, 1)
        cv.inc(3)
        alpha_tok = cv.inc()
        @block.tensor
        def _(t):
            t.wait_ge(cv.h, alpha_tok)
            for blk in range(8):
                mi = t.transpose(p_a8[:, 128 * blk:128 * blk + 128],
                                 alpha_tm[:, 8 * blk:8 * blk + 8],
                                 bufs['ident_bf'][:, :])
            mi.then_inc(ct.h, 1)
        a8_tok = ct.inc()
        @block.vector
        def _(v):
            v.wait_ge(ct.h, a8_tok)
            v.tensor_copy(a8[:, :], p_a8[:, :]).then_inc(cv.h, 1)
        a8sb_tok = cv.inc()
        vx_toks = {}
        vav = wslot[0].rearrange("p (k m) -> p k m", k=NCH)
        vsum_tok = None
        if first:
            @block.tensor
            def _(t, wlast=wlast):
                t.wait_ge(v_tok[0], v_tok[1])
                t.wait_ge(wlast[0], wlast[1])
                for k in range(NCH):
                    mi = t.matmul(P[0][0:1, 0:512], bufs['ones_bf'][:, 0:1],
                                  vav[:, k, :], start=(k == 0),
                                  stop=(k == NCH - 1))
                mi.then_inc(ct.h, 1)
            vs_mm = ct.inc()
            @block.vector
            def _(v, vs_mm=vs_mm):
                v.wait_ge(ct.h, vs_mm)
                v.tensor_copy(bufs['vsum'][:, :],
                              P[0][0:1, 0:512]).then_inc(cv.h, 1)
            vsum_tok = cv.inc()
        for j in range(4):
            par = j % 2
            @block.tensor
            def _(t, j=j, par=par, vsum_tok=vsum_tok):
                if j == 0:
                    t.wait_ge(v_tok[0], v_tok[1])
                    t.wait_ge(cv.h, mr_tok)
                    if vsum_tok is not None:
                        t.wait_ge(cv.h, vsum_tok)
                if j >= 2:
                    t.wait_ge(cv.h, vx_toks[j - 2])
                for k in range(NCH):
                    lhsT = vav[:, k, 128 * j:128 * j + 128]
                    for half in range(2):
                        mi = t.matmul(
                            P[par][:, 512 * half:512 * half + 512], lhsT,
                            xhat[:, NTOK * k + 512 * half:
                                 NTOK * k + 512 * half + 512],
                            start=(k == 0), stop=False)
                lhsT = bufs['vsum'][:, 128 * j:128 * j + 128]
                for half in range(2):
                    mi = t.matmul(P[par][:, 512 * half:512 * half + 512],
                                  lhsT,
                                  mrr[:, 512 * half:512 * half + 512],
                                  start=False, stop=True)
                mi.then_inc(ct.h, 1)
            vmm_tok = ct.inc()
            @block.vector
            def _(v, j=j, par=par, vmm_tok=vmm_tok):
                v.wait_ge(ct.h, vmm_tok)
                v.tensor_copy(vx[:, NTOK * j:NTOK * (j + 1)],
                              P[par][:, :]).then_inc(cv.h, 1)
            vx_toks[j] = cv.inc()
        avx_toks = {}
        for j in range(4):
            par = j % 2
            @block.tensor
            def _(t, j=j, par=par):
                if j == 0:
                    t.wait_ge(cv.h, a8sb_tok)
                    t.wait_ge(cv.h, vx_toks[3])
                else:
                    t.wait_ge(cv.h, avx_toks[j - 1])
                for half in range(2):
                    mi = t.matmul(P[par][:, 512 * half:512 * half + 512],
                                  bufs['repsel_sb'][:, 128 * j:128 * j + 128],
                                  a8[:, 512 * half:512 * half + 512],
                                  start=True, stop=True)
                mi.then_inc(ct.h, 1)
            arep_tok = ct.inc()
            @block.vector
            def _(v, j=j, par=par, arep_tok=arep_tok):
                v.wait_ge(ct.h, arep_tok)
                v.wait_ge(cv.h, vx_toks[j])
                v.tensor_mul(vx[:, NTOK * j:NTOK * (j + 1)],
                             vx[:, NTOK * j:NTOK * (j + 1)],
                             P[par][:, :]).then_inc(cv.h, 1)
            avx_toks[j] = cv.inc()
        res_toks = {}
        uav = wslot[1].rearrange("p (j k m) -> p j k m", j=4, k=NCH)
        for m in range(NCH):
            par = m % 2
            @block.tensor
            def _(t, m=m, par=par):
                if m == 0:
                    t.wait_ge(cv.h, avx_toks[3])
                    t.wait_ge(u_tok[0], u_tok[1])
                if m >= 2:
                    t.wait_ge(cv.h, res_toks[m - 2])
                for j in range(4):
                    lhsT = uav[:, j, m, :]
                    for half in range(2):
                        mi = t.matmul(
                            P[par][:, 512 * half:512 * half + 512], lhsT,
                            vx[:, NTOK * j + 512 * half:
                               NTOK * j + 512 * half + 512],
                            start=(j == 0), stop=(j == 3))
                mi.then_inc(ct.h, 1)
            umm_tok = ct.inc()
            @block.vector
            def _(v, m=m, par=par, umm_tok=umm_tok):
                v.wait_ge(ct.h, umm_tok)
                v.tensor_add(x_chunks[m], x_chunks[m],
                             P[par][:, :]).then_inc(cv.h, 1)
            res_toks[m] = cv.inc()
        load_fence(block, ('tensor', 'scalar'), cv.h, res_toks[NCH - 1])
        return res_toks


def alloc_common(nc, e):
    b = {}
    b['w2_sb'] = e(nc.sbuf_tensor("w2_sb", [128, NCH * 8], BF16))
    b['b1_sb'] = e(nc.sbuf_tensor("b1_sb", [128, NCH], F32))
    b['b2_sb'] = e(nc.sbuf_tensor("b2_sb", [8, 1], F32))
    b['repsel_sb'] = e(nc.sbuf_tensor("repsel_sb", [8, 4 * 128], BF16))
    b['ident_f'] = e(nc.sbuf_tensor("ident_f", [128, 128], F32))
    b['ident_bf'] = e(nc.sbuf_tensor("ident_bf", [128, 128], BF16))
    b['ones_f'] = e(nc.sbuf_tensor("ones_f", [128, 1], F32))
    b['ones_bf'] = e(nc.sbuf_tensor("ones_bf", [128, 1], BF16))
    b['onesr_f'] = e(nc.sbuf_tensor("onesr_f", [1, 128], F32))
    b['onesr_bf'] = e(nc.sbuf_tensor("onesr_bf", [1, 128], BF16))
    b['consts'] = e(nc.sbuf_tensor("consts_sb", [1, 2], F32))
    return b


def emit_common_loads(nc, block, bufs, dr, S):
    dsy, dconst = S['dsy'], S['dconst']
    @block.sync
    def _(sy):
        sy.dma_start(out=bufs['b1_sb'][:, :], in_=dr['b1'][:, :]).then_inc(dsy.h, 16)
        sy.dma_start(out=bufs['b2_sb'][:, :], in_=dr['b2'][:, :]).then_inc(dsy.h, 16)
        sy.dma_start(out=bufs['ident_f'][:, :], in_=dr['ident'][:, :]).then_inc(dsy.h, 16)
        sy.dma_start(out=bufs['ones_f'][:, :], in_=dr['ones'][:, :]).then_inc(dsy.h, 16)
        sy.dma_start(out=bufs['onesr_f'][:, :], in_=dr['onesr'][:, :]).then_inc(dsy.h, 16)
        sy.dma_start(out=bufs['consts'][:, :], in_=dr['consts'][:, :]).then_inc(dsy.h, 16)
    dsy.inc(16 * 6)
    @block.gpsimd
    def _(g):
        g.dma_start(out=bufs['w2_sb'][:, :], in_=dr['w2'][:, :]).then_inc(dconst.h, 16)
        g.dma_start(out=bufs['repsel_sb'][:, :], in_=dr['repsel'][:, :]).then_inc(dconst.h, 16)
        g.dma_start(out=bufs['ident_bf'][:, :], in_=dr['ident'][:, :]).then_inc(dconst.h, 16)
        g.dma_start(out=bufs['ones_bf'][:, :], in_=dr['ones'][:, :]).then_inc(dconst.h, 16)
        g.dma_start(out=bufs['onesr_bf'][:, :], in_=dr['onesr'][:, :]).then_inc(dconst.h, 16)
    dconst.inc(16 * 5)



def preload_set6(block):
    @block.scalar
    def _(s):
        s.add_instruction(mybir.InstLoadActFuncSet(
            name=s.bass.get_next_instruction_name(),
            act_func_set_id=6, ins=[], outs=[]))



def build_fused():
    nc = bass.Bass("TRN2", target_bir_lowering=False)
    dr = {}
    def di(name, shape, dt=F32):
        dr[name] = nc.dram_tensor(name, shape, dt, kind="ExternalInput").ap()
    di('xall', [128, 4 * NCH * NTOK])
    di('wkv', [128, NCH * 128])
    di('wqq', [128, 4, NCH * 512])
    di('woq', [128, 4, NCH * 512])
    di('q2l', [128, H * 64])
    di('vup', [64, H * 128])
    di('w1q', [128, 4, NCH * 512])
    di('w2', [128, NCH * 8])
    di('varr', [128, NCH * 512])
    di('uarr', [128, 4 * NCH * 128])
    di('b1', [128, NCH])
    di('b2', [8, 1])
    di('repsel', [8, 4 * 128])
    di('ident', [128, 128])
    di('ones', [128, 1])
    di('onesr', [1, 128])
    di('consts', [1, 2])
    di('bandm', [128, 4 * 256], U8)
    di('fixm', [128, 4 * 256], BF16)
    di('maskneg', [128, 2 * 256])
    di('kwidx', [32, 1], I32)
    dr['yout'] = nc.dram_tensor("yout", [128, NCH * NTOK], F32,
                                kind="ExternalOutput").ap()
    if DEBUG_DUMP:
        dr['dbg1'] = nc.dram_tensor("dbg1", [128, NCH * NTOK], BF16,
                                    kind="ExternalOutput").ap()
        dr['dbg2'] = nc.dram_tensor("dbg2", [128, NT * 66 + NTOK], BF16,
                                    kind="ExternalOutput").ap()
        dr['dbg3'] = nc.dram_tensor("dbg3", [1, 2 * NTOK], F32,
                                    kind="ExternalOutput").ap()
    ktok = nc.dram_tensor("ktok", [NT, 66, 128], BF16, kind="Internal").ap()
    kwin_d = nc.dram_tensor("kwin_d", [32, 33 * 128], BF16, kind="Internal").ap()
    crow_d = nc.dram_tensor("crow_d", [H, NTOK], F32, kind="Internal").ap()

    with contextlib.ExitStack() as stx:
        e = stx.enter_context
        x_res = e(nc.sbuf_tensor("x_res", [128, NCH * NTOK], F32))
        bbuf = e(nc.sbuf_tensor("bbuf", [128, NCH * NTOK], BF16))
        kaug = e(nc.sbuf_tensor("kaug", [66, 4096], BF16))
        vt_sb = e(nc.sbuf_tensor("vt_sb", [64, 4096], BF16))
        bufs = alloc_common(nc, e)
        bufs['hbuf'] = bbuf
        bufs['w1q_dram'] = dr['w1q']

        S = make_sems(nc, e)
        block = e(nc.Block())
        preload_set6(block)
        ct, cs, cv = S['ct'], S['cs'], S['cv']
        cg, dsy, dgp = S['cg'], S['dsy'], S['dgp']
        dlat, dx = S['dlat'], S['dx']

        emit_common_loads(nc, block, bufs, dr, S)
        x_chunks = [x_res[:, NTOK * c:NTOK * (c + 1)] for c in range(NCH)]
        xall4 = dr['xall'].rearrange("p (j n) -> p j n", j=4)

        with contextlib.ExitStack() as ph1:
            xhat = ph1.enter_context(
                nc.sbuf_tensor("xhat", [128, NCH * NTOK], BF16))
            kvout = ph1.enter_context(
                nc.sbuf_tensor("kvout", [128, NTOK], BF16))
            wkv_sb = ph1.enter_context(
                nc.sbuf_tensor("wkv_sb", [128, NCH * 128], BF16))
            bufs['msum_w1'] = ph1.enter_context(
                nc.sbuf_tensor("msum_w1", [1, 2048], BF16))
            bufs['vsum'] = ph1.enter_context(
                nc.sbuf_tensor("vsum", [1, 512], BF16))
            wkvsum = ph1.enter_context(
                nc.sbuf_tensor("wkvsum", [1, 128], BF16))
            mrr1 = ph1.enter_context(nc.sbuf_tensor("mrr1", [1, NTOK], BF16))
            mrr2 = mrr1
            bufs['wslot'] = [ph1.enter_context(
                nc.sbuf_tensor(f"wslotP{i}", [128, NCH * 512], BF16))
                for i in range(2)]
            @block.gpsimd
            def _(g):
                g.dma_start(out=wkv_sb[:, :],
                            in_=dr['wkv'][:, :]).then_inc(S['dconst'].h, 16)
            S['dconst'].inc(16)
            load_fence(block, ('tensor', 'scalar', 'vector'), dsy.h, dsy.n)
            load_fence(block, ('tensor', 'scalar', 'vector'),
                       S['dconst'].h, S['dconst'].n)

            with nc.psum_tensor("wkvs_ps", [1, 128], F32) as wkvs_ps:
                @block.tensor
                def _(t):
                    wv = wkv_sb.rearrange("p (k m) -> p k m", k=NCH)
                    for k in range(NCH):
                        mi = t.matmul(wkvs_ps[:, :], bufs['ones_bf'][:, 0:1],
                                      wv[:, k, :], start=(k == 0),
                                      stop=(k == NCH - 1))
                    mi.then_inc(ct.h, 1)
                wks_mm = ct.inc()
                @block.vector
                def _(v, wks_mm=wks_mm):
                    v.wait_ge(ct.h, wks_mm)
                    v.tensor_copy(wkvsum[:, :],
                                  wkvs_ps[:, :]).then_inc(cv.h, 1)
                wkvsum_tok = cv.inc()

            kaug512 = kaug[0:64, :].rearrange("p (t c) -> p t c", c=512)
            vt512 = vt_sb.rearrange("p (t c) -> p t c", c=512)
            kv512 = kvout.rearrange("p (t c) -> p t c", c=128)
            kv_ps = ph1.enter_context(
                nc.psum_tensor("kv_ps", [128, NTOK], F32))
            ln2_q = None
            kvs_tok = None
            for j in range(4):
                xq_toks = []
                for qd in range(4):
                    if ln2_q is None:
                        rel = wkvsum_tok if qd == 0 else None
                    else:
                        rel = ln2_q[4 * qd + 3]
                    @block.sync
                    def _(sy, j=j, qd=qd, rel=rel, dxn=dx.n):
                        sy.wait_ge(dx.h, dxn)
                        if rel is not None:
                            sy.wait_ge(cv.h, rel)
                        sy.dma_start(
                            out=x_res[:, 4 * NTOK * qd:4 * NTOK * (qd + 1)],
                            in_=xall4[:, j, 4 * NTOK * qd:4 * NTOK * (qd + 1)]
                        ).then_inc(dx.h, 16)
                    xq_toks.append(dx.inc(16))
                w0 = [(dx.h, xq_toks[c // 4]) for c in range(NCH)]
                _, ln1_ap, mr1 = emit_ln(nc, block, bufs, S, x_chunks, xhat,
                                         wait0=w0, mrr=mrr1)
                res_toks = emit_ffn(nc, block, bufs, S, x_chunks, xhat,
                                    dr['varr'], dr['uarr'],
                                    rhs_ready=[(cv.h, t) for t in ln1_ap],
                                    mrr=mrr1, mr_tok=mr1,
                                    first=(j == 0), zps=kv_ps)
                w2l = [(cv.h, res_toks[c]) for c in range(NCH)]
                _, ln2_q, mr2 = emit_ln(nc, block, bufs, S, x_chunks, xhat,
                                        wait0=w2l, mrr=mrr2)
                if True:
                    @block.tensor
                    def _(t, mr2=mr2, ln2_q=ln2_q, pzw=bufs['prev_zt']):
                        t.wait_ge(cv.h, mr2)
                        t.wait_ge(cv.h, pzw)
                        wv = wkv_sb.rearrange("p (k m) -> p k m", k=NCH)
                        for k in range(NCH):
                            t.wait_ge(cv.h, ln2_q[k])
                            for half in range(2):
                                mi = t.matmul(
                                    kv_ps[:, 512 * half:512 * half + 512],
                                    wv[:, k, :],
                                    xhat[:, NTOK * k + 512 * half:
                                         NTOK * k + 512 * half + 512],
                                    start=(k == 0), stop=False)
                        for half in range(2):
                            mi = t.matmul(
                                kv_ps[:, 512 * half:512 * half + 512],
                                wkvsum[:, :],
                                mrr2[:, 512 * half:512 * half + 512],
                                start=False, stop=True)
                        mi.then_inc(ct.h, 1)
                    kv_tok = ct.inc()
                    @block.vector
                    def _(v, kv_tok=kv_tok, dn=dlat.n):
                        v.wait_ge(ct.h, kv_tok)
                        v.wait_ge(dlat.h, dn)
                        v.tensor_copy(kvout[:, :],
                                      kv_ps[:, :]).then_inc(cv.h, 1)
                    kvs_tok = cv.inc()
                    bufs['prev_kvs'] = kvs_tok
                @block.gpsimd
                def _(g, j=j, kvs_tok=kvs_tok):
                    g.wait_ge(cv.h, kvs_tok)
                    g.dma_start(out=kaug512[:, :, 128 * j:128 * j + 128],
                                in_=kv512[0:64, :, :]).then_inc(dlat.h, 16)
                    g.dma_start(out=vt512[:, :, 128 * j:128 * j + 128],
                                in_=kv512[64:128, :, :]).then_inc(dlat.h, 16)
                dlat.inc(32)
                bufs['ln_dst_free'] = (ct.h, kv_tok)

            def qt_cb(m, pap, mm_tok):
                @block.vector
                def _(v, m=m, pap=pap, mm_tok=mm_tok):
                    v.wait_ge(ct.h, mm_tok)
                    v.tensor_copy(bbuf[:, NTOK * m:NTOK * (m + 1)],
                                  pap[:, :]).then_inc(cv.h, 1)
                return (cv.h, cv.inc())

            ms_wq = []
            (_, wq_last), _sf = emit_big_proj(
                nc, block, bufs, S, dr['wqq'], xhat, qt_cb,
                rhs_ready=[(cv.h, ln2_q[k]) for k in range(NCH)],
                fixup=(bufs['msum_w1'], ms_wq, mrr2, mr2),
                colsum=(bufs['msum_w1'], ms_wq))

        @block.gpsimd
        def _(g):
            g.wait_ge(ct.h, ct.n)
            g.wait_ge(cs.h, cs.n)
        @block.sync
        def _(sy):
            sy.wait_ge(ct.h, ct.n)
            sy.wait_ge(cs.h, cs.n)

        with contextlib.ExitStack() as attn_stack:
            ea = attn_stack.enter_context
            kwin_sb = ea(nc.sbuf_tensor("kwin_sb", [66, 16 * 128], BF16))
            ksum_bf = ea(nc.sbuf_tensor("ksum_bf", [1, 4096], BF16))
            km2 = ea(nc.sbuf_tensor("km2", [1, 1], F32))
            kmt = ea(nc.sbuf_tensor("kmt", [1, 1], F32))
            hlk = ea(nc.sbuf_tensor("hlk", [128, 1], F32))
            idxt = ea(nc.sbuf_tensor("idxt", [32, 1], I32))
            bandm_sb = ea(nc.sbuf_tensor("bandm_sb", [128, 4 * 256], U8))
            fixm_sb = ea(nc.sbuf_tensor("fixm_sb", [128, 4 * 256], BF16))
            maskneg_sb = ea(nc.sbuf_tensor("maskneg_sb", [128, 2 * 256], F32))
            q2l_sb = ea(nc.sbuf_tensor("q2l_sb", [128, H * 64], BF16))
            vup_sb = ea(nc.sbuf_tensor("vup_sb", [64, H * 128], BF16))
            qsh_bf = ea(nc.sbuf_tensor("qsh_bf", [1, NTOK], BF16))
            ksumT = ea(nc.sbuf_tensor("ksumT", [128, NT], F32))
            ksT32 = ea(nc.sbuf_tensor("ksT32", [NT, 128], F32))
            ob_sb = [ea(nc.sbuf_tensor(f"ob{i}", [128, 256], BF16))
                     for i in range(2)]
            cb_sb = ea(nc.sbuf_tensor("cb_sb", [1, NTOK], F32))
            cfin = ea(nc.sbuf_tensor("cfin", [1, NTOK], F32))
            crow_bf = ea(nc.sbuf_tensor("crow_bf", [1, NTOK], BF16))
            bm_row = ea(nc.sbuf_tensor("bm_row", [1, NTOK], F32))
            sums_sb = ea(nc.sbuf_tensor("sums_sb", [1, NTOK], F32))
            rep_sb = ea(nc.sbuf_tensor("rep_sb", [128, NTOK], F32))
            rcp_row = ea(nc.sbuf_tensor("rcp_row", [1, NTOK], F32))
            bmaxT = ea(nc.sbuf_tensor("bmaxT", [128, 8], F32))
            bmax8 = ea(nc.sbuf_tensor("bmax8", [8, 128], F32))
            sqh = ea(nc.sbuf_tensor("sqh", [64, NTOK], BF16))
            qlat_b = [ea(nc.sbuf_tensor(f"qlat_h{i}", [66, NTOK], BF16))
                      for i in range(2)]
            qshrep_b = [ea(nc.sbuf_tensor(f"qshrepb{i}", [128, NTOK], BF16))
                        for i in range(2)]
            vtT = ea(nc.sbuf_tensor("vtT", [128, NT * 66], BF16))
            cv_sb = ea(nc.sbuf_tensor("cv_sb", [64, NTOK], BF16))
            probs = [ea(nc.sbuf_tensor(f"probs{i}", [128, NTOK], BF16))
                     for i in range(2)]
            mr_sb = [ea(nc.sbuf_tensor(f"mr{i}", [128, 256], BF16))
                     for i in range(4)]
            madd_sb = [ea(nc.sbuf_tensor(f"madd{i}", [128, 256], F32))
                       for i in range(2)]
            madd2_sb = [ea(nc.sbuf_tensor(f"madd2_{i}", [128, 256], F32))
                        for i in range(2)]

            @block.sync
            def _(sy):
                sy.dma_start(out=idxt[:, :], in_=dr['kwidx'][:, :]).then_inc(dsy.h, 16)
                sy.dma_start(out=fixm_sb[:, :], in_=dr['fixm'][:, :]).then_inc(dsy.h, 16)
                sy.dma_start(out=maskneg_sb[:, :], in_=dr['maskneg'][:, :]).then_inc(dsy.h, 16)
            dsy.inc(16 * 3)
            @block.gpsimd
            def _(g):
                g.dma_start(out=q2l_sb[:, :], in_=dr['q2l'][:, :]).then_inc(dlat.h, 16)
                g.dma_start(out=vup_sb[:, :], in_=dr['vup'][:, :]).then_inc(dlat.h, 16)
                g.dma_start(out=bandm_sb[:, :], in_=dr['bandm'][:, :]).then_inc(dlat.h, 16)
            lat_tok = dlat.inc(16 * 3)
            @block.gpsimd
            def _(g):
                g.wait_ge(dlat.h, lat_tok)
                g.memset(kaug[64:65, :], 1.0).then_inc(cg.h, 1)
            ones_row_tok = cg.inc()
            load_fence(block, ('tensor', 'scalar', 'vector'), dsy.h, dsy.n)
            load_fence(block, ('tensor', 'scalar', 'vector'), dlat.h, lat_tok)

            ksum_evac = {}
            knmax = {}
            with contextlib.ExitStack() as kst:
                ksq = kst.enter_context(
                    nc.sbuf_tensor("ksq", [64, 4096], BF16))
                ks_ps = kst.enter_context(nc.psum_tensor("ks_ps", [1, 512], F32))
                kn_ps = kst.enter_context(nc.psum_tensor("kn_ps", [1, 512], F32))
                hb_ps = kst.enter_context(nc.psum_tensor("hb_ps", [128, 1], F32))
                @block.scalar
                def _(s):
                    s.activation(ksq[:, :], kaug[0:64, :],
                                 AF.Square).then_inc(cs.h, 1)
                ksq_tok = cs.inc()
                for cch in range(8):
                    @block.tensor
                    def _(t, cch=cch):
                        if cch == 0:
                            t.wait_ge(cs.h, ksq_tok)
                        else:
                            t.wait_ge(cv.h, ksum_evac[cch - 1])
                            t.wait_ge(cv.h, knmax[cch - 1])
                        t.matmul(ks_ps[:, :], bufs['ones_bf'][0:64, 0:1],
                                 kaug[0:64, 512 * cch:512 * cch + 512],
                                 start=True, stop=True)
                        t.matmul(kn_ps[:, :], bufs['ones_bf'][0:64, 0:1],
                                 ksq[:, 512 * cch:512 * cch + 512],
                                 start=True, stop=True).then_inc(ct.h, 1)
                    kmm = ct.inc()
                    @block.vector
                    def _(v, cch=cch, kmm=kmm):
                        v.wait_ge(ct.h, kmm)
                        v.tensor_copy(ksum_bf[:, 512 * cch:512 * cch + 512],
                                      ks_ps[:, :]).then_inc(cv.h, 1)
                    ksum_evac[cch] = cv.inc()
                    @block.vector
                    def _(v, cch=cch, kmm=kmm):
                        v.wait_ge(ct.h, kmm)
                        v.reduce_max(kmt[:, :], kn_ps[:, :],
                                     axis=AX.X).then_inc(cv.h, 1)
                        v.wait_ge(cv.h, cv.n + 1)
                        if cch == 0:
                            inst = v.tensor_copy(km2[:, :], kmt[:, :])
                        else:
                            inst = v.tensor_max(km2[:, :], km2[:, :], kmt[:, :])
                        inst.then_inc(cv.h, 1)
                    cv.inc(1)
                    knmax[cch] = cv.inc()
                @block.scalar
                def _(s):
                    s.wait_ge(cv.h, knmax[7])
                    s.wait_ge(cv.h, ksum_evac[7])
                    s.activation(kmt[:, :], km2[:, :], AF.Ln).then_inc(cs.h, 1)
                lnk_tok = cs.inc()
                @block.tensor
                def _(t):
                    t.wait_ge(cs.h, lnk_tok)
                    t.matmul(hb_ps[:, :], bufs['onesr_f'][:, :], kmt[:, :],
                             start=True, stop=True).then_inc(ct.h, 1)
                hb_tok = ct.inc()
                @block.vector
                def _(v):
                    v.wait_ge(ct.h, hb_tok)
                    v.tensor_scalar(hlk[:, :], hb_ps[:, :], 0.5, None,
                                    ALU.mult).then_inc(cv.h, 1)
                hlk_tok = cv.inc()
            @block.sync
            def _(sy):
                sy.wait_ge(cs.h, lnk_tok)
                sy.wait_ge(cg.h, ones_row_tok)
                sy.dma_start(out=kaug[65:66, :],
                             in_=ksum_bf[:, :]).then_inc(dsy.h, 16)
            r65_tok = dsy.inc(16)
            @block.sync
            def _(sy):
                sy.wait_ge(dsy.h, r65_tok)
                sy.dma_start(out=ktok.rearrange("a b c -> b a c"),
                             in_=kaug.rearrange("p (a c) -> p a c", c=128)
                             ).then_inc(dsy.h, 16)
            ktok_tok = dsy.inc(16)
            @block.gpsimd
            def _(g):
                g.wait_ge(dsy.h, ktok_tok)
                g.dma_start(out=ksT32[:, :],
                            in_=ktok[:, 65, :]).then_inc(dlat.h, 16)
            kst_tok = dlat.inc(16)
            with nc.sbuf_tensor("kwtmp", [32, 33 * 128], BF16) as kwtmp:
                @block.gpsimd
                def _(g):
                    g.wait_ge(dsy.h, ktok_tok)
                    g.indirect_dma_start(
                        kwtmp[:, :], None,
                        ktok.rearrange("a (h bb) c -> (a h) (bb c)", h=2),
                        IndirectOffsetOnAxis(ap=idxt[:, :], axis=0)
                    ).then_inc(dgp.h, 16)
                gath_tok = dgp.inc(16)
                @block.sync
                def _(sy):
                    sy.wait_ge(dgp.h, gath_tok)
                    sy.dma_start(out=kwin_d[:, :],
                                 in_=kwtmp[:, :]).then_inc(dsy.h, 16)
            kwd_tok = dsy.inc(16)
            @block.sync
            def _(sy):
                sy.wait_ge(dsy.h, kwd_tok)
                sy.dma_start(
                    out=kwin_sb.rearrange("p (a c) -> p a c", c=128),
                    in_=kwin_d.rearrange("(a h) (bb c) -> (h bb) a c",
                                         h=2, c=128)
                ).then_inc(dsy.h, 16)
            kwin_tok = dsy.inc(16)

            vt_tok = lat_tok

            with contextlib.ExitStack() as ps2:
                PA = ps2.enter_context(nc.psum_tensor("PA", [128, NTOK], F32))
                CV = ps2.enter_context(nc.psum_tensor("CV", [128, NTOK], F32))
                PC = ps2.enter_context(nc.psum_tensor("PC", [128, NTOK], F32))
                @block.tensor
                def _(t):
                    t.wait_ge(dlat.h, kst_tok)
                    t.transpose(PA[:, 0:32], ksT32[:, :],
                                bufs['ident_f'][0:32, 0:32]).then_inc(ct.h, 1)
                kstr_tok = ct.inc()
                @block.vector
                def _(v, kstr_tok=kstr_tok):
                    v.wait_ge(ct.h, kstr_tok)
                    v.tensor_copy(ksumT[:, :], PA[:, 0:32]).then_inc(cv.h, 1)
                ksumT_cv = cv.inc()
                @block.vector
                def _(v):
                    for r in range(4):
                        v.tensor_copy(mr_sb[r][:, :],
                                      fixm_sb[:, 256 * r:256 * r + 256]
                                      ).then_inc(cv.h, 1)
                cv.inc(4)
                with nc.psum_tensor("vtp", [128, NT * 64], BF16) as vtp:
                    @block.tensor
                    def _(t):
                        t.wait_ge(dlat.h, vt_tok)
                        for s in range(NT):
                            mi = t.transpose(vtp[:, 64 * s:64 * s + 64],
                                             vt_sb[:, 128 * s:128 * s + 128],
                                             bufs['ident_bf'][0:64, 0:64])
                        mi.then_inc(ct.h, 1)
                    vtp_tok = ct.inc()
                    @block.vector
                    def _(v, vtp_tok=vtp_tok):
                        v.wait_ge(ct.h, vtp_tok)
                        for s in range(NT):
                            v.tensor_copy(vtT[:, 66 * s:66 * s + 64],
                                          vtp[:, 64 * s:64 * s + 64])
                            mi = v.tensor_copy(
                                vtT[:, 66 * s + 64:66 * s + 65],
                                bufs['ones_bf'][:, 0:1])
                        mi.then_inc(cv.h, 1)
                    vtT_tok = cv.inc()
                PD = ps2.enter_context(nc.psum_tensor("PD", [128, NTOK], F32))
                PF = {'pa': (cv.h, ksumT_cv), 'pb': (cv.h, 0),
                      'pc': (cv.h, 0), 'pdfree': 0, 'ctx_last': 0}
                sco_last = {-2: 0, -1: 0}
                A = {}
                ctxout_tok = 0
                ob64 = bufs['ones_bf'][0:64, 0:1]

                def emit_A_stage(h, k):
                    st = A.setdefault(h, {})
                    par = h % 2
                    ql = qlat_b[par]
                    pv = A.get(h - 1)
                    if k == 0:
                        ps = sco_last[h - 2]
                        pn = pv['nsq'] if pv else 0
                        st['ps'] = ps
                        @block.tensor
                        def _(t, h=h, paf=PF['pa']):
                            t.wait_ge(paf[0], paf[1])
                            if h == 0:
                                t.wait_ge(dsy.h, kwin_tok)
                            lhsT = q2l_sb[:, 64 * h:64 * h + 64]
                            t.matmul(PA[0:64, 0:512], lhsT,
                                     bbuf[:, NTOK * h:NTOK * h + 512],
                                     start=True, stop=True)
                            t.matmul(PA[0:64, 512:1024], lhsT,
                                     bbuf[:, NTOK * h + 512:NTOK * h + 1024],
                                     start=True, stop=True).then_inc(ct.h, 1)
                        qmm = ct.inc()
                        @block.vector
                        def _(v, qmm=qmm, ps=ps, pn=pn, ql=ql):
                            v.wait_ge(ct.h, qmm)
                            v.wait_ge(ct.h, ps)
                            v.wait_ge(ct.h, pn)
                            v.tensor_scalar(ql[0:64, :], PA[0:64, :], 0.125,
                                            None, ALU.mult).then_inc(cv.h, 1)
                            v.wait_ge(cv.h, cv.n + 1)
                            v.tensor_mul(sqh[:, :], ql[0:64, :],
                                         ql[0:64, :]).then_inc(cv.h, 1)
                        cv.inc(1)
                        st['qevac'] = cv.inc()
                        @block.gpsimd
                        def _(g, ps=ps, ql=ql):
                            g.wait_ge(ct.h, ps)
                            g.memset(ql[64:65, :], 0.0).then_inc(cg.h, 1)
                        st['z64'] = cg.inc()
                    elif k == 1:
                        @block.tensor
                        def _(t, qevac=st['qevac'], ql=ql):
                            t.wait_ge(cv.h, qevac)
                            t.matmul(PA[0:1, 0:512], ob64, ql[0:64, 0:512],
                                     start=True, stop=True)
                            t.matmul(PA[0:1, 512:1024], ob64,
                                     ql[0:64, 512:1024],
                                     start=True, stop=True).then_inc(ct.h, 1)
                        qs = ct.inc()
                        @block.vector
                        def _(v, qs=qs, pv=pv):
                            v.wait_ge(ct.h, qs)
                            if pv:
                                v.wait_ge(dsy.h, pv['r65'])
                            v.tensor_scalar(qsh_bf[:, :], PA[0:1, :], 0.5,
                                            None, ALU.mult).then_inc(cv.h, 1)
                        st['qshv'] = cv.inc()
                    elif k == 2:
                        @block.tensor
                        def _(t, qshv=st['qshv']):
                            t.wait_ge(cv.h, qshv)
                            t.matmul(PA[0:1, 0:512], ob64, sqh[:, 0:512],
                                     start=True, stop=True)
                            t.matmul(PA[0:1, 512:1024], ob64,
                                     sqh[:, 512:1024],
                                     start=True, stop=True).then_inc(ct.h, 1)
                        st['nsq'] = ct.inc()
                        @block.scalar
                        def _(s, h=h, nsq=st['nsq'], pv=pv):
                            s.wait_ge(ct.h, nsq)
                            if h == 0:
                                s.wait_ge(cv.h, hlk_tok)
                            if pv:
                                s.wait_ge(cv.h, pv['negc'])
                            s.activation(cfin[:, :], PA[0:1, :],
                                         AF.Ln).then_inc(cs.h, 1)
                            s.wait_ge(cs.h, cs.n + 1)
                            s.activation(cb_sb[:, :], cfin[:, :], AF.Exp,
                                         scale=0.5,
                                         bias=hlk[0:1, 0:1]).then_inc(cs.h, 1)
                        cs.inc(1)
                        st['cb'] = cs.inc()
                        @block.sync
                        def _(sy, qshv=st['qshv'], cbt=st['cb'], ql=ql):
                            sy.wait_ge(cv.h, qshv)
                            sy.wait_ge(cs.h, cbt)
                            sy.dma_start(out=ql[65:66, :],
                                         in_=qsh_bf[:, :]).then_inc(dsy.h, 16)
                        st['r65'] = dsy.inc(16)
                    elif k == 3:
                        @block.tensor
                        def _(t, cbt=st['cb']):
                            t.wait_ge(cs.h, cbt)
                            obr = bufs['onesr_bf'][:, :]
                            t.matmul(PA[:, 0:512], obr, qsh_bf[:, 0:512],
                                     start=True, stop=True)
                            t.matmul(PA[:, 512:1024], obr,
                                     qsh_bf[:, 512:1024],
                                     start=True, stop=True).then_inc(ct.h, 1)
                        qrep = ct.inc()
                        @block.vector
                        def _(v, qrep=qrep, par=par):
                            v.wait_ge(ct.h, qrep)
                            v.tensor_copy(qshrep_b[par][:, :],
                                          PA[:, :]).then_inc(cv.h, 1)
                        st['qrepv'] = cv.inc()
                        st['pp_evac'] = {}
                        st['ma'] = {}
                        st['bm'] = {}
                    elif 4 <= k <= 11:
                        j = k - 4
                        ppar = j % 2
                        pp_evac, ma_toks, bm_toks = \
                            st['pp_evac'], st['ma'], st['bm']
                        @block.tensor
                        def _(t, j=j, ppar=ppar, st=st, ql=ql):
                            if j == 0:
                                t.wait_ge(dsy.h, st['r65'])
                                t.wait_ge(cg.h, st['z64'])
                                t.wait_ge(cv.h, st['qrepv'])
                            if j >= 2:
                                t.wait_ge(cv.h, pp_evac[j - 2])
                            t.matmul(PA[:, 512 * ppar:512 * ppar + 256],
                                     ql[0:66, 128 * j:128 * j + 128],
                                     kwin_sb[0:66, 256 * j:256 * j + 256],
                                     start=True, stop=True).then_inc(ct.h, 1)
                        pp_tok = ct.inc()
                        slot = 0 if j == 0 else 1
                        @block.vector
                        def _(v, j=j, ppar=ppar, pp_tok=pp_tok, pv=pv,
                              slot=slot):
                            v.wait_ge(ct.h, pp_tok)
                            v.tensor_tensor(madd_sb[ppar][:, :],
                                            PA[:, 512 * ppar:512 * ppar + 256],
                                            maskneg_sb[:, 256 * slot:
                                                       256 * slot + 256],
                                            ALU.add).then_inc(cv.h, 1)
                        pp_evac[j] = cv.inc()
                        @block.vector
                        def _(v, j=j, ppar=ppar, pv=pv, pe=pp_evac[j]):
                            v.wait_ge(cv.h, pe)
                            if j < 2 and pv:
                                v.wait_ge(ct.h, pv['btr'])
                            v.reduce_max(bmaxT[:, j:j + 1],
                                         madd_sb[ppar][:, :],
                                         axis=AX.X).then_inc(cv.h, 1)
                        bm_toks[j] = cv.inc()
                    elif k == 12:
                        @block.tensor
                        def _(t, bm7=st['bm'][7]):
                            t.wait_ge(cv.h, bm7)
                            t.transpose(PA[0:8, 0:128], bmaxT[:, :],
                                        bufs['ident_f'][:, :]
                                        ).then_inc(ct.h, 1)
                        st['btr'] = ct.inc()
                        @block.vector
                        def _(v, btr=st['btr'], pv=pv):
                            v.wait_ge(ct.h, btr)
                            if pv:
                                v.wait_ge(dsy.h, pv['crow_out'])
                            v.tensor_copy(bmax8[:, :],
                                          PA[0:8, 0:128]).then_inc(cv.h, 1)
                        st['b8'] = cv.inc()
                        @block.sync
                        def _(sy, h=h, b8=st['b8']):
                            sy.wait_ge(cv.h, b8)
                            sy.dma_start(
                                out=crow_d[h:h + 1, :].rearrange(
                                    "o (a c) -> (o a) c", a=8),
                                in_=bmax8[:, :]).then_inc(dsy.h, 16)
                        st['crow_out'] = dsy.inc(16)
                        @block.sync
                        def _(sy, co=st['crow_out'], pv=pv):
                            sy.wait_ge(dsy.h, co)
                            if pv:
                                sy.wait_ge(cv.h, pv['negc'])
                            sy.dma_start(out=bm_row[:, :],
                                         in_=crow_d[h:h + 1, :]
                                         ).then_inc(dsy.h, 16)
                        st['bmrow'] = dsy.inc(16)
                    elif k == 13:
                        @block.vector
                        def _(v, bmrow=st['bmrow'], cbt=st['cb'], pv=pv):
                            v.wait_ge(dsy.h, bmrow)
                            v.wait_ge(cs.h, cbt)
                            if pv:
                                v.wait_ge(dsy.h, pv['r64'])
                            v.tensor_max(cfin[:, :], cb_sb[:, :],
                                         bm_row[:, :]).then_inc(cv.h, 1)
                            v.wait_ge(cv.h, cv.n + 1)
                            v.tensor_scalar(crow_bf[:, :], cfin[:, :],
                                            -1.004, -0.5, ALU.mult,
                                            ALU.add).then_inc(cv.h, 1)
                        cv.inc(1)
                        st['negc'] = cv.inc()
                        @block.sync
                        def _(sy, negc=st['negc'], ql=ql):
                            sy.wait_ge(cv.h, negc)
                            sy.dma_start(out=ql[64:65, :],
                                         in_=crow_bf[:, :]).then_inc(dsy.h, 16)
                        st['r64'] = dsy.inc(16)

                def emit_epi_sums(h, half, sc_tok):
                    lo, hi = 512 * half, 512 * half + 512
                    @block.vector
                    def _(v, sc_tok=sc_tok, lo=lo, hi=hi,
                          cl=PF['ctx_last']):
                        v.wait_ge(ct.h, sc_tok)
                        v.wait_ge(ct.h, cl)
                        v.tensor_copy(cv_sb[:, lo:hi],
                                      CV[0:64, lo:hi]).then_inc(cv.h, 1)
                        v.tensor_copy(sums_sb[:, lo:hi],
                                      CV[64:65, lo:hi]).then_inc(cv.h, 1)
                    cv.inc(1)
                    s_tok = cv.inc()
                    @block.scalar
                    def _(s, s_tok=s_tok, lo=lo, hi=hi):
                        s.wait_ge(cv.h, s_tok)
                        s.activation(rcp_row[:, lo:hi], sums_sb[:, lo:hi],
                                     AF.Ln).then_inc(cs.h, 1)
                        s.wait_ge(cs.h, cs.n + 1)
                        s.activation(rcp_row[:, lo:hi], rcp_row[:, lo:hi],
                                     AF.Exp, scale=-1.0).then_inc(cs.h, 1)
                    cs.inc(1)
                    return (s_tok, cs.inc())

                def emit_epi_rep(h, half, rcp_tok):
                    lo, hi = 512 * half, 512 * half + 512
                    nxt = A.get(h + 1)
                    ve_war = nxt['b8'] if nxt else 0
                    @block.tensor
                    def _(t, rcp_tok=rcp_tok, ve_war=ve_war, lo=lo, hi=hi):
                        t.wait_ge(cs.h, rcp_tok)
                        t.wait_ge(cv.h, ve_war)
                        t.matmul(PA[:, lo:hi], bufs['onesr_f'][:, :],
                                 rcp_row[:, lo:hi], start=True,
                                 stop=True).then_inc(ct.h, 1)
                    rep_tok = ct.inc()
                    @block.vector
                    def _(v, rep_tok=rep_tok, lo=lo, hi=hi):
                        v.wait_ge(ct.h, rep_tok)
                        v.tensor_copy(rep_sb[:, lo:hi],
                                      PA[:, lo:hi]).then_inc(cv.h, 1)
                    return (rep_tok, cv.inc())

                def emit_epi_out(h, half, rep, cvsb_tok, pd_free):
                    lo, hi = 512 * half, 512 * half + 512
                    @block.tensor
                    def _(t, h=h, cvsb_tok=cvsb_tok, pd_free=pd_free,
                          lo=lo, hi=hi):
                        t.wait_ge(cv.h, cvsb_tok)
                        t.wait_ge(cv.h, pd_free)
                        if h == 0 and lo == 0:
                            t.wait_ge(cv.h, vtT_tok)
                        t.matmul(PD[:, lo:hi],
                                 vup_sb[:, 128 * h:128 * h + 128],
                                 cv_sb[:, lo:hi], start=True,
                                 stop=True).then_inc(ct.h, 1)
                    ctx_tok = ct.inc()
                    PF['ctx_last'] = ctx_tok
                    @block.vector
                    def _(v, h=h, rep=rep, ctx_tok=ctx_tok, lo=lo, hi=hi):
                        v.wait_ge(cv.h, rep[1])
                        v.wait_ge(ct.h, ctx_tok)
                        v.tensor_mul(bbuf[:, NTOK * h + lo:NTOK * h + hi],
                                     rep_sb[:, lo:hi],
                                     PD[:, lo:hi]).then_inc(cv.h, 1)
                    return cv.inc()

                epi = {}
                for k in range(14):
                    emit_A_stage(0, k)
                PF['pa'] = (cv.h, A[0]['b8'])
                for h in range(H):
                    st = A[h]
                    par = h % 2
                    ql = qlat_b[par]
                    qrep_sb = qshrep_b[par]
                    os_toks = {}
                    sco_toks = {}
                    mr_toks = {}
                    madd_toks = {}
                    exp_toks = {}
                    sc_toks = {}

                    def emit_outer(i):
                        qlo = qlo_of(i)
                        lead = min(256, NTOK - qlo)
                        @block.vector
                        def _(v, i=i, qlo=qlo, lead=lead, qrep_sb=qrep_sb):
                            v.tensor_scalar(ob_sb[i % 2][:, 0:lead],
                                            qrep_sb[:, qlo:qlo + lead],
                                            ksumT[:, i:i + 1], None,
                                            ALU.mult).then_inc(cv.h, 1)
                        os_toks[i] = cv.inc()


                    def emit_select(i):
                        r = i % 4
                        qlo = qlo_of(i)
                        lead = min(256, NTOK - qlo)
                        @block.vector
                        def _(v, i=i, r=r, lead=lead):
                            v.wait_ge(cv.h, os_toks[i])
                            v.copy_predicated(
                                mr_sb[r][:, 0:lead],
                                bandm_sb[:, 256 * r:256 * r + lead],
                                ob_sb[i % 2][:, 0:lead]
                            ).then_inc(cv.h, 1)
                        mr_toks[i] = cv.inc()

                    def emit_madd(i):
                        qlo = qlo_of(i)
                        lead = min(256, NTOK - qlo)
                        dlt = pc_delta(i)
                        @block.scalar
                        def _(s, i=i, qlo=qlo, lead=lead, dlt=dlt):
                            s.wait_ge(ct.h, sco_toks[i])
                            s.activation(madd2_sb[i % 2][:, 0:lead],
                                         PC[:, qlo + dlt:qlo + dlt + lead],
                                         AF.Copy).then_inc(cs.h, 1)
                        cp_tok = cs.inc()
                        @block.vector
                        def _(v, i=i, lead=lead, cp_tok=cp_tok):
                            v.wait_ge(cs.h, cp_tok)
                            v.wait_ge(cv.h, mr_toks[i])
                            v.tensor_tensor(
                                madd2_sb[i % 2][:, 0:lead],
                                madd2_sb[i % 2][:, 0:lead],
                                mr_sb[i % 4][:, 0:lead],
                                ALU.add).then_inc(cv.h, 1)
                        madd_toks[i] = cv.inc()

                    def emit_exp(i):
                        qlo = qlo_of(i)
                        lead = min(256, NTOK - qlo)
                        dlt = pc_delta(i)
                        toks = {}
                        def reg(a, b, tok):
                            for bank in range((a + dlt) // 512,
                                              (b - 1 + dlt) // 512 + 1):
                                toks[bank] = tok
                        for (a, b) in split_ranges(qlo + lead, NTOK,
                                                   (512,) if dlt == 0 else ()):
                            @block.scalar
                            def _(s, i=i, a=a, b=b, dlt=dlt):
                                s.wait_ge(ct.h, sco_toks[i])
                                s.activation(probs[i % 2][:, a:b],
                                             PC[:, a + dlt:b + dlt],
                                             AF.Exp).then_inc(cs.h, 1)
                            reg(a, b, cs.inc())
                        @block.scalar
                        def _(s, i=i, qlo=qlo, lead=lead):
                            s.wait_ge(cv.h, madd_toks[i])
                            s.activation(probs[i % 2][:, qlo:qlo + lead],
                                         madd2_sb[i % 2][:, 0:lead],
                                         AF.Exp).then_inc(cs.h, 1)
                        reg(qlo, qlo + lead, cs.inc())
                        exp_toks[i] = toks

                    def emit_iter(i, first_waits):
                        fw = tuple(first_waits)
                        @block.tensor
                        def _(t, i=i, fw=fw, st=st, ql=ql, h=h,
                              pbf=PF['pb']):
                            if i >= 2:
                                j = i - 2
                                qlo = qlo_of(j)
                                dlt = pc_delta(j)
                                if j == 0:
                                    t.wait_ge(pbf[0], pbf[1])
                                    if h == 0:
                                        t.wait_ge(cv.h, vtT_tok)
                                for (a, b) in split_ranges(qlo, NTOK, (512,)):
                                    t.wait_ge(cs.h,
                                              exp_toks[j][(a + dlt) // 512])
                                    last = (j == NT - 1
                                            or qlo_of(j + 1) >= b)
                                    mi = t.matmul(
                                        CV[0:65, a:b],
                                        vtT[:, 66 * j:66 * j + 65],
                                        probs[j % 2][:, a:b],
                                        start=(j == 0), stop=last)
                                mi.then_inc(ct.h, 1)
                                sc_toks[j] = ct.inc()
                            if i < NT:
                                qlo = qlo_of(i)
                                dlt = pc_delta(i)
                                for sem, val in fw:
                                    t.wait_ge(sem, val)
                                pieces = (split_ranges(qlo, NTOK, (512,))
                                          if dlt == 0 else [(qlo, NTOK)])
                                for (a, b) in pieces:
                                    bank = (a + dlt) // 512
                                    for back in (1, 2):
                                        pe = exp_toks.get(i - back)
                                        if pe is not None and bank in pe:
                                            t.wait_ge(cs.h, pe[bank])
                                            break
                                    mi = t.matmul(
                                        PC[:, a + dlt:b + dlt],
                                        kaug[0:65, 128 * i:128 * i + 128],
                                        ql[0:65, a:b],
                                        start=True, stop=True)
                                mi.then_inc(ct.h, 1)
                                sco_toks[i] = ct.inc()

                    for i in range(NT + 2):
                        if i == 1 and epi.get('pend') is not None:
                            ph, psc = epi.pop('pend')
                            s1 = emit_epi_sums(ph, 1, psc)
                            r1 = emit_epi_rep(ph, 1, s1[1])
                            co1 = emit_epi_out(ph, 1, r1, s1[0],
                                               PF['pdfree'])
                            PF['pdfree'] = co1
                            PF['pb'] = (cv.h, s1[0])
                            PF['pa'] = (cv.h, r1[1])
                        if h + 1 < H and 2 <= i <= 15:
                            emit_A_stage(h + 1, i - 2)
                        if i == 22:
                            epi['sums0'] = emit_epi_sums(h, 0, sc_toks[19])
                        elif i == 24:
                            epi['rep0'] = emit_epi_rep(h, 0,
                                                       epi['sums0'][1])
                        elif i == 25:
                            out0 = emit_epi_out(h, 0, epi['rep0'],
                                                epi['sums0'][0],
                                                PF['pdfree'])
                            PF['pdfree'] = out0
                        if i < NT:
                            emit_outer(i)
                            emit_select(i)
                        fw = [(dsy.h, st['r64']), PF['pc']] if i == 0 else []
                        emit_iter(i, fw)
                        if i < NT:
                            emit_madd(i)
                            emit_exp(i)
                    sco_last[h] = sco_toks[NT - 1]
                    PF['pc'] = (cs.h, max(exp_toks[NT - 2][0],
                                          exp_toks[NT - 1][1]))
                    if h + 1 < H:
                        epi['pend'] = (h, sc_toks[NT - 1])
                    else:
                        s1 = emit_epi_sums(h, 1, sc_toks[NT - 1])
                        r1 = emit_epi_rep(h, 1, s1[1])
                        ctxout_tok = emit_epi_out(h, 1, r1, s1[0],
                                                  PF['pdfree'])
                if DEBUG_DUMP:
                    D2 = NT * 66
                    @block.sync
                    def _(sy):
                        sy.wait_ge(cv.h, ctxout_tok)
                        sy.dma_start(out=dr['dbg1'][:, :],
                                     in_=bbuf[:, :]).then_inc(dsy.h, 16)
                        sy.dma_start(out=dr['dbg2'][:, 0:D2],
                                     in_=vtT[:, :]).then_inc(dsy.h, 16)
                        sy.dma_start(out=dr['dbg2'][0:64, D2:D2 + NTOK],
                                     in_=cv_sb[:, :]).then_inc(dsy.h, 16)
                        sy.dma_start(out=dr['dbg3'][:, 0:NTOK],
                                     in_=sums_sb[:, :]).then_inc(dsy.h, 16)
                        sy.dma_start(out=dr['dbg3'][:, NTOK:2 * NTOK],
                                     in_=rcp_row[:, :]).then_inc(dsy.h, 16)
                        sy.wait_ge(dsy.h, dsy.n + 80)
                    dsy.inc(80)
                load_fence(block, ('tensor', 'scalar'), cv.h, ctxout_tok)

        @block.gpsimd
        def _(g):
            g.wait_ge(ct.h, ct.n)
            g.wait_ge(cs.h, cs.n)
            g.wait_ge(cv.h, cv.n)
        @block.sync
        def _(sy):
            sy.wait_ge(ct.h, ct.n)
            sy.wait_ge(cv.h, cv.n)

        with contextlib.ExitStack() as f2:
            bufs['wslot'] = [f2.enter_context(
                nc.sbuf_tensor(f"wslotF{i}", [128, NCH * 512], BF16))
                for i in range(2)]
            xhat3 = f2.enter_context(
                nc.sbuf_tensor("xhat3", [128, NCH * NTOK], BF16))
            bufs['msum_w1'] = f2.enter_context(
                nc.sbuf_tensor("msum_w1f", [1, 2048], BF16))
            bufs['vsum'] = f2.enter_context(
                nc.sbuf_tensor("vsumf", [1, 512], BF16))
            mrr3 = f2.enter_context(nc.sbuf_tensor("mrr3", [1, NTOK], BF16))

            wo_toks = {}

            def wo_cb(m, pap, mm_tok):
                @block.vector
                def _(v, m=m, pap=pap, mm_tok=mm_tok):
                    v.wait_ge(ct.h, mm_tok)
                    v.tensor_add(x_chunks[m], x_chunks[m],
                                 pap[:, :]).then_inc(cv.h, 1)
                wo_toks[m] = cv.inc()
                return (cv.h, wo_toks[m])

            (_, wo_last), _sf = emit_big_proj(
                nc, block, bufs, S, dr['woq'], bbuf, wo_cb,
                extra_wait=(cv.h, ctxout_tok))

            zps2 = f2.enter_context(
                nc.psum_tensor("zps2", [128, NTOK], F32))
            _, ln3_ap, mr3 = emit_ln(
                nc, block, bufs, S, x_chunks, xhat3,
                wait0=[(cv.h, wo_toks[c]) for c in range(NCH)], mrr=mrr3)
            ffn2_res = emit_ffn(nc, block, bufs, S, x_chunks, xhat3,
                                dr['varr'], dr['uarr'],
                                rhs_ready=[(cv.h, t) for t in ln3_ap],
                                mrr=mrr3, mr_tok=mr3, first=True, zps=zps2)
            for qd in range(4):
                @block.sync
                def _(sy, qd=qd):
                    sy.wait_ge(cv.h, ffn2_res[4 * qd + 3])
                    sy.dma_start(
                        out=dr['yout'][:, 4 * NTOK * qd:4 * NTOK * (qd + 1)],
                        in_=x_res[:, 4 * NTOK * qd:4 * NTOK * (qd + 1)]
                    ).then_inc(dsy.h, 16)
                dsy.inc(16)
            @block.sync
            def _(sy):
                sy.wait_ge(dsy.h, dsy.n)
    return nc



_CACHE = {}


def _get_program():
    if 'f' not in _CACHE:
        _CACHE['f'] = build_fused()
    return _CACHE['f']


def _prep_weights(Wq, Wk, Wv, q2l, vup, Wo, U, V, W1, b1, W2, b2):
    w = {}
    w['w1q'] = arr_lhsT_q(W1).reshape(128, 4, NCH * 512)
    w['w2'] = arr_lhsT(W2)
    w['varr'] = arr_lhsT(np.ascontiguousarray(
        V.transpose(2, 0, 1).reshape(E, 8 * 64)))
    w['uarr'] = np.ascontiguousarray(
        U.reshape(4, 2, NCH, 128, 64).transpose(1, 4, 0, 2, 3)
        .reshape(128, 4 * NCH * 128))
    w['b1'] = np.ascontiguousarray(b1.reshape(NCH, 128).T)
    w['b2'] = np.ascontiguousarray(b2.reshape(8, 1))
    w['wkv'] = arr_lhsT(np.ascontiguousarray(
        np.concatenate([Wk.T, Wv.T], axis=1)))
    w['wqq'] = arr_lhsT_q(np.ascontiguousarray(Wq.T)).reshape(128, 4, NCH * 512)
    w['woq'] = arr_lhsT_q(np.ascontiguousarray(Wo.T)).reshape(128, 4, NCH * 512)
    w['q2l'] = np.ascontiguousarray(q2l.transpose(1, 0, 2).reshape(128, H * 64))
    w['vup'] = np.ascontiguousarray(vup.transpose(1, 0, 2).reshape(64, H * 128))
    k = np.arange(8)[:, None]
    jm = np.arange(4 * 128)[None, :]
    w['repsel'] = (k == 2 * (jm // 128) + (jm % 128) // 64).astype(np.float32)
    w['ident'] = np.eye(128, dtype=np.float32)
    w['ones'] = np.ones((128, 1), np.float32)
    w['onesr'] = np.ones((1, 128), np.float32)
    w['consts'] = np.array([[-0.5, 0.0]], np.float32)
    return w


def kernel(L0_attn_hidden_states, L0_attn_Wq, L0_attn_Wk, L0_attn_Wv,
           L0_attn_q_to_latent, L0_attn_v_up, L0_attn_Wo,
           L0_ffn_U, L0_ffn_V, L0_ffn_W1, L0_ffn_b1, L0_ffn_W2, L0_ffn_b2):
    x = np.asarray(L0_attn_hidden_states, np.float32)
    w = _prep_weights(
        np.asarray(L0_attn_Wq, np.float32), np.asarray(L0_attn_Wk, np.float32),
        np.asarray(L0_attn_Wv, np.float32),
        np.asarray(L0_attn_q_to_latent, np.float32),
        np.asarray(L0_attn_v_up, np.float32), np.asarray(L0_attn_Wo, np.float32),
        np.asarray(L0_ffn_U, np.float32), np.asarray(L0_ffn_V, np.float32),
        np.asarray(L0_ffn_W1, np.float32), np.asarray(L0_ffn_b1, np.float32),
        np.asarray(L0_ffn_W2, np.float32), np.asarray(L0_ffn_b2, np.float32))

    ncf = _get_program()
    shared = {k: w[k] for k in ('wqq', 'woq', 'q2l', 'vup', 'w1q', 'w2',
                                'varr', 'uarr', 'b1', 'b2', 'wkv', 'repsel',
                                'ident', 'ones', 'onesr', 'consts')}
    in_maps = []
    for c in range(8):
        b, p = divmod(c, 4)
        bandm, fixm = make_masks(p)
        maskneg = make_maskneg(p)
        kwidx = np.zeros((32, 1), np.int32)
        for j, T in enumerate(owned_tiles(p)):
            s0 = slot_of(p, max(0, T - 1))
            s1 = slot_of(p, T)
            kwidx[4 * j:4 * j + 4, 0] = (2 * s0, 2 * s0 + 1,
                                         2 * s1, 2 * s1 + 1)
        xall = np.concatenate([arrange_x(x[b], r) for r in blockorder(p)],
                              axis=1)
        m = dict(shared)
        m.update(xall=xall, bandm=bandm, fixm=fixm, maskneg=maskneg,
                 kwidx=kwidx)
        in_maps.append(m)
    res = run_bass_kernel_spmd(ncf, in_maps, core_ids=list(range(8)))

    out = np.zeros_like(x)
    for c in range(8):
        b, p = divmod(c, 4)
        y = np.asarray(res.results[c]['yout'], np.float32)
        out[b][owned_rows(p)] = unarrange_x(y)
    return out


# version 32
# speedup vs baseline: 1.1109x; 1.1109x over previous
import contextlib
import numpy as np
import ml_dtypes
import concourse.bass as bass
import concourse.mybir as mybir
from concourse.bass import IndirectOffsetOnAxis
from concourse.bass_utils import run_bass_kernel_spmd

F32 = mybir.dt.float32
BF16 = mybir.dt.bfloat16
I32 = mybir.dt.int32
U8 = mybir.dt.uint8
AF = mybir.ActivationFunctionType
ALU = mybir.AluOpType
AX = mybir.AxisListType

E = 2048
NCH = 16
NTOK = 1024
H, D, L = 16, 128, 64
NT = 32
NEG = -1e30
EPS = 1e-5


DEBUG_DUMP = False

_UID = [0]


def uid():
    _UID[0] += 1
    return _UID[0]


class Sem:
    def __init__(self, h):
        self.h = h
        self.n = 0

    def inc(self, k=1):
        self.n += k
        return self.n



def arr_lhsT(w):
    kin, mout = w.shape
    return np.ascontiguousarray(
        w.reshape(kin // 128, 128, mout).transpose(1, 0, 2)
        .reshape(128, (kin // 128) * mout))


def arr_lhsT_q(w):
    kin, mout = w.shape
    nk = kin // 128
    mq = mout // 4
    a = w.reshape(nk, 128, 4, mq).transpose(1, 2, 0, 3)
    return np.ascontiguousarray(a.reshape(128, 4, nk * mq))


def owned_tiles(p):
    return [p + 4 * j for j in range(8)]


def owned_rows(p):
    return np.concatenate(
        [np.arange(128 * t, 128 * t + 128) for t in owned_tiles(p)])


def blockorder(p):
    return [r for r in range(4) if r != p] + [p]


def slot_of(p, g):
    return 4 * (g // 4) + blockorder(p).index(g % 4)


def make_masks(p):
    sp = np.arange(128)[:, None]
    qc = np.arange(128)[None, :]
    bandm = np.zeros((128, 4 * 256), np.uint8)
    fixm = np.zeros((128, 4 * 256), np.float32)
    for jj, r in enumerate(blockorder(p)):
        for w in range(2):
            dt = (p - r) if w == 0 else (p + 4 - r)
            diff = 128 * dt + qc - sp
            band = (diff >= 0) & (diff <= 64)
            cols = slice(256 * jj + 128 * w, 256 * jj + 128 * w + 128)
            bandm[:, cols] = band.astype(np.uint8)
            fixm[:, cols] = np.where(diff < 0, NEG, 0.0)
    return bandm, fixm.astype(ml_dtypes.bfloat16)


def make_maskneg(p):
    i = np.arange(128)[:, None]
    jc = np.arange(128)[None, :]
    mn = np.full((128, 2 * 256), NEG, np.float32)
    for slot in range(2):
        for w in range(2):
            if slot == 0 and p == 0 and w == 0:
                continue
            diff = (128 - 128 * w) + i - jc
            band = (diff >= 0) & (diff <= 64)
            cols = slice(256 * slot + 128 * w, 256 * slot + 128 * w + 128)
            mn[:, cols] = np.where(band, 0.0, NEG)
    return mn


def arrange_x(xb, p):
    xt = xb[owned_rows(p)].T
    return np.ascontiguousarray(
        xt.reshape(NCH, 128, NTOK).transpose(1, 0, 2).reshape(128, NCH * NTOK))


def unarrange_x(y):
    return np.ascontiguousarray(
        y.reshape(128, NCH, NTOK).transpose(1, 0, 2).reshape(E, NTOK).T)


def qlo_of(s):
    return 128 * (s // 4)


def pc_delta(i):
    return 1024 * (i % 2) - qlo_of(i)


def split_ranges(lo, hi, cuts):
    pts = sorted({lo, hi} | {c for c in cuts if lo < c < hi})
    return list(zip(pts[:-1], pts[1:]))



def load_fence(block, engines, sem, val):
    for name in engines:
        @getattr(block, name)
        def _(e):
            e.wait_ge(sem, val)


SEM_NAMES = ('ct', 'cs', 'cv', 'cg', 'dsy', 'dgp', 'dconst', 'dlat',
             'dw0', 'dw1', 'dvu0', 'dvu1', 'dx')


def make_sems(nc, e):
    hs = [e(nc.semaphore(n)) for n in SEM_NAMES]
    d = {n: Sem(h) for n, h in zip(SEM_NAMES, hs)}
    return d


def emit_ln(nc, block, bufs, S, src_chunks, dst, wait0=None, mrr=None):
    w0 = wait0 if isinstance(wait0, list) else \
        ([wait0] + [None] * (NCH - 1) if wait0 is not None else [None] * NCH)
    ct, cs, cv = S['ct'], S['cs'], S['cv']
    onesb = bufs['ones_bf']
    u = uid()
    with contextlib.ExitStack() as ps:
        p_sum = ps.enter_context(nc.psum_tensor(f"ln_sum{u}", [1, NTOK], F32))
        p_sq = ps.enter_context(nc.psum_tensor(f"ln_sq{u}", [1, NTOK], F32))
        p_rrep = ps.enter_context(nc.psum_tensor(f"ln_rrep{u}", [128, NTOK], F32))
        sq2 = [ps.enter_context(nc.sbuf_tensor(f"ln_sq2_{u}0", [128, NTOK],
                                               BF16))] * 2
        m_sb = ps.enter_context(nc.sbuf_tensor(f"ln_m{u}", [1, NTOK], F32))
        t2 = ps.enter_context(nc.sbuf_tensor(f"ln_t2{u}", [1, NTOK], F32))
        rstd = ps.enter_context(nc.sbuf_tensor(f"ln_r{u}", [1, NTOK], F32))
        sq_toks = []
        mm_toks = []
        for c in range(NCH):
            par = c % 2
            @block.scalar
            def _(s, c=c, par=par):
                if w0[c] is not None:
                    s.wait_ge(w0[c][0], w0[c][1])
                if c >= 1:
                    s.wait_ge(ct.h, mm_toks[c - 1])
                s.activation(sq2[par][:, :], src_chunks[c],
                             AF.Square).then_inc(cs.h, 1)
            sq_toks.append(cs.inc())
            @block.tensor
            def _(t, c=c, par=par):
                if w0[c] is not None:
                    t.wait_ge(w0[c][0], w0[c][1])
                t.matmul(p_sum[:, 0:512], bufs['ones_f'][:, 0:1],
                         src_chunks[c][:, 0:512],
                         start=(c == 0), stop=(c == NCH - 1))
                t.matmul(p_sum[:, 512:1024], bufs['ones_f'][:, 0:1],
                         src_chunks[c][:, 512:1024],
                         start=(c == 0), stop=(c == NCH - 1))
                t.wait_ge(cs.h, sq_toks[c])
                t.matmul(p_sq[:, 0:512], onesb[:, 0:1], sq2[par][:, 0:512],
                         start=(c == 0), stop=(c == NCH - 1))
                t.matmul(p_sq[:, 512:1024], onesb[:, 0:1],
                         sq2[par][:, 512:1024],
                         start=(c == 0), stop=(c == NCH - 1)).then_inc(ct.h, 1)
            mm_toks.append(ct.inc())
        @block.vector
        def _(v):
            v.wait_ge(ct.h, mm_toks[-1])
            v.tensor_scalar(m_sb[:, :], p_sum[:, :], 1.0 / E,
                            None, ALU.mult).then_inc(cv.h, 1)
            v.wait_ge(cv.h, cv.n + 1)
            v.tensor_tensor(t2[:, :], m_sb[:, :], m_sb[:, :],
                            ALU.mult).then_inc(cv.h, 1)
            v.wait_ge(cv.h, cv.n + 2)
            v.tensor_scalar(t2[:, :], t2[:, :], EPS, None,
                            ALU.subtract).then_inc(cv.h, 1)
            v.wait_ge(cv.h, cv.n + 3)
            v.scalar_tensor_tensor(t2[:, :], p_sq[:, :], 1.0 / E, t2[:, :],
                                   ALU.mult, ALU.subtract).then_inc(cv.h, 1)
        cv.inc(3)
        var_tok = cv.inc()
        @block.scalar
        def _(s):
            s.wait_ge(cv.h, var_tok)
            s.activation(rstd[:, :], t2[:, :], AF.Ln).then_inc(cs.h, 1)
            s.wait_ge(cs.h, cs.n + 1)
            s.activation(t2[:, :], rstd[:, :], AF.Exp,
                         scale=-0.5).then_inc(cs.h, 1)
        cs.inc(1)
        rstd_tok = cs.inc()
        @block.vector
        def _(v):
            v.wait_ge(cs.h, rstd_tok)
            v.scalar_tensor_tensor(mrr[:, :], m_sb[:, :], -1.0, t2[:, :],
                                   ALU.mult, ALU.mult).then_inc(cv.h, 1)
        mr_tok = cv.inc()
        @block.tensor
        def _(t):
            t.wait_ge(cs.h, rstd_tok)
            t.matmul(p_rrep[:, 0:512], bufs['onesr_f'][:, :], t2[:, 0:512],
                     start=True, stop=True)
            t.matmul(p_rrep[:, 512:1024], bufs['onesr_f'][:, :],
                     t2[:, 512:1024], start=True, stop=True).then_inc(ct.h, 1)
        rep_tok = ct.inc()
        last = None
        apply_toks = []
        for c in range(NCH):
            @block.vector
            def _(v, c=c):
                if c == 0:
                    v.wait_ge(ct.h, rep_tok)
                v.tensor_mul(dst[:, NTOK * c:NTOK * (c + 1)], src_chunks[c],
                             p_rrep[:, :]).then_inc(cv.h, 1)
            last = cv.inc()
            apply_toks.append(last)
        load_fence(block, ('tensor', 'scalar'), cv.h, last)
        return last, apply_toks, mr_tok


def emit_big_proj(nc, block, bufs, S, wq_dram, rhs_sb, out_cb,
                  extra_wait=None, rhs_ready=None, fixup=None,
                  colsum=None):
    ct, cs, cv = S['ct'], S['cs'], S['cv']
    dw = [S['dw0'], S['dw1']]
    wslot = bufs['wslot']
    slot_free = [None] * 4
    u = uid()
    with contextlib.ExitStack() as ps:
        P = [ps.enter_context(nc.psum_tensor(f"bp{u}_p{i}", [128, NTOK], F32))
             for i in range(2)]
        cs_ps = ps.enter_context(nc.psum_tensor(f"bp{u}_cs", [1, 512], F32)) \
            if colsum is not None else None
        qload = [None] * 4
        cons_toks = {}

        entry_ct = ct.n
        def emit_load(q):
            @block.gpsimd
            def _(g, q=q):
                if q >= 2:
                    g.wait_ge(ct.h, slot_free[q - 2])
                else:
                    g.wait_ge(ct.h, entry_ct)
                g.dma_start(out=wslot[q % 2][:, :],
                            in_=wq_dram[:, q, :]).then_inc(dw[q % 2].h, 16)
            qload[q] = (dw[q % 2].h, dw[q % 2].inc(16))

        emit_load(0)
        emit_load(1)
        for q in range(4):
            if colsum is not None:
                msum_sb, cs_out = colsum
                @block.tensor
                def _(t, q=q):
                    t.wait_ge(qload[q][0], qload[q][1])
                    wv = wslot[q % 2].rearrange("p (k m) -> p k m", k=NCH)
                    for k in range(NCH):
                        mi = t.matmul(cs_ps[:, :], bufs['ones_bf'][:, 0:1],
                                      wv[:, k, :], start=(k == 0),
                                      stop=(k == NCH - 1))
                    mi.then_inc(ct.h, 1)
                csq_tok = ct.inc()
                @block.vector
                def _(v, q=q, csq_tok=csq_tok, msum_sb=msum_sb):
                    v.wait_ge(ct.h, csq_tok)
                    v.tensor_copy(msum_sb[:, 512 * q:512 * q + 512],
                                  cs_ps[:, :]).then_inc(cv.h, 1)
                cs_out.append(cv.inc())
            for mc in range(4):
                m = 4 * q + mc
                par = m % 2
                @block.tensor
                def _(t, q=q, mc=mc, m=m, par=par):
                    if mc == 0:
                        t.wait_ge(qload[q][0], qload[q][1])
                        if extra_wait is not None:
                            t.wait_ge(extra_wait[0], extra_wait[1])
                        if m == 0 and fixup is not None:
                            t.wait_ge(cv.h, fixup[3])
                    if m >= 2:
                        t.wait_ge(cons_toks[m - 2][0], cons_toks[m - 2][1])
                    wv = wslot[q % 2].rearrange("p (k m) -> p k m", k=NCH)
                    for k in range(NCH):
                        if m == 0 and rhs_ready is not None:
                            t.wait_ge(rhs_ready[k][0], rhs_ready[k][1])
                        lhsT = wv[:, k, 128 * mc:128 * mc + 128]
                        for half in range(2):
                            mi = t.matmul(
                                P[par][:, 512 * half:512 * half + 512],
                                lhsT,
                                rhs_sb[:, NTOK * k + 512 * half:
                                       NTOK * k + 512 * half + 512],
                                start=(k == 0),
                                stop=(fixup is None and k == NCH - 1))
                    if fixup is not None:
                        msum_sb, ms_toks, mrr, _mt = fixup
                        if ms_toks is not None and mc == 0:
                            t.wait_ge(cv.h, ms_toks[q])
                        lhsT = msum_sb[:, 128 * m:128 * m + 128]
                        for half in range(2):
                            mi = t.matmul(
                                P[par][:, 512 * half:512 * half + 512],
                                lhsT,
                                mrr[:, 512 * half:512 * half + 512],
                                start=False, stop=True)
                    mi.then_inc(ct.h, 1)
                mm_tok = ct.inc()
                slot_free[q] = mm_tok
                cons_toks[m] = out_cb(m, P[par], mm_tok)
            if q + 2 <= 3:
                emit_load(q + 2)
        last = cons_toks[NCH - 1]
        load_fence(block, ('tensor', 'scalar'), last[0], last[1])
        return last, slot_free


def emit_ffn(nc, block, bufs, S, x_chunks, xhat, varr_dram, uarr_dram,
             rhs_ready=None, mrr=None, mr_tok=None, first=False, zps=None):
    ct, cs, cv = S['ct'], S['cs'], S['cv']
    dvu = [S['dvu0'], S['dvu1']]
    h_sb = bufs['hbuf']
    wslot = bufs['wslot']

    def h_cb(m, pap, mm_tok):
        @block.scalar
        def _(s, m=m, pap=pap, mm_tok=mm_tok):
            s.wait_ge(ct.h, mm_tok)
            s.activation(h_sb[:, NTOK * m:NTOK * (m + 1)], pap[:, :],
                         AF.Relu, bias=bufs['b1_sb'][:, m:m + 1]).then_inc(cs.h, 1)
        return (cs.h, cs.inc())

    ms_toks = [] if first else None
    wlast, slot_free = emit_big_proj(
        nc, block, bufs, S, bufs['w1q_dram'], xhat, h_cb,
        rhs_ready=rhs_ready,
        fixup=(bufs['msum_w1'], ms_toks, mrr, mr_tok),
        colsum=(bufs['msum_w1'], ms_toks) if first else None)
    @block.gpsimd
    def _(g):
        g.wait_ge(ct.h, slot_free[2])
        g.dma_start(out=wslot[0][:, :], in_=varr_dram[:, :]).then_inc(dvu[0].h, 16)
    v_tok = (dvu[0].h, dvu[0].inc(16))
    @block.gpsimd
    def _(g):
        g.wait_ge(ct.h, slot_free[3])
        g.dma_start(out=wslot[1][:, :], in_=uarr_dram[:, :]).then_inc(dvu[1].h, 16)
    u_tok = (dvu[1].h, dvu[1].inc(16))

    u = uid()
    with contextlib.ExitStack() as ps:
        P = [ps.enter_context(nc.psum_tensor(f"fe{u}_p{i}", [128, NTOK], F32))
             for i in range(2)]
        p_z = zps[0:8, :]
        p_ztr = zps[0:128, 0:64]
        p_a8 = ps.enter_context(nc.psum_tensor(f"ffn_a8{u}", [8, NTOK], BF16))
        z_sb = ps.enter_context(nc.sbuf_tensor(f"ffn_zsb{u}", [8, NTOK], F32))
        zt = ps.enter_context(nc.sbuf_tensor(f"ffn_zt{u}", [128, 64], F32))
        nmax = ps.enter_context(nc.sbuf_tensor(f"ffn_nm{u}", [128, 8], F32))
        ez = ps.enter_context(nc.sbuf_tensor(f"ffn_ez{u}", [128, 64], F32))
        ssum = ps.enter_context(nc.sbuf_tensor(f"ffn_ss{u}", [128, 8], F32))
        rcp = ps.enter_context(nc.sbuf_tensor(f"ffn_rc{u}", [128, 8], F32))
        alpha_tm = ps.enter_context(nc.sbuf_tensor(f"ffn_al{u}", [128, 64], BF16))
        a8 = ps.enter_context(nc.sbuf_tensor(f"ffn_a8s{u}", [8, NTOK], BF16))
        vx = ps.enter_context(nc.sbuf_tensor(f"ffn_vx{u}", [128, 4 * NTOK], BF16))
        @block.tensor
        def _(t, pzw=bufs.get('prev_zt', 0), pkv=bufs.get('prev_kvs', 0)):
            t.wait_ge(cv.h, pzw)
            t.wait_ge(cv.h, pkv)
            w2v = bufs['w2_sb'].rearrange("p (k m) -> p k m", k=NCH)
            for k in range(NCH):
                for half in range(2):
                    mi = t.matmul(p_z[:, 512 * half:512 * half + 512],
                                  w2v[:, k, :],
                                  h_sb[:, NTOK * k + 512 * half:
                                       NTOK * k + 512 * half + 512],
                                  start=(k == 0), stop=(k == NCH - 1))
            mi.then_inc(ct.h, 1)
        z_tok = ct.inc()
        @block.vector
        def _(v):
            v.wait_ge(ct.h, z_tok)
            v.tensor_scalar(z_sb[:, :], p_z[:, :], bufs['b2_sb'][:, 0:1],
                            None, ALU.add).then_inc(cv.h, 1)
        zsb_tok = cv.inc()
        zsb_sem = cv
        @block.tensor
        def _(t):
            t.wait_ge(cv.h, zsb_tok)
            for blk in range(8):
                mi = t.transpose(p_ztr[:, 8 * blk:8 * blk + 8],
                                 z_sb[:, 128 * blk:128 * blk + 128],
                                 bufs['ident_f'][0:8, 0:8])
            mi.then_inc(ct.h, 1)
        ztr_tok = ct.inc()
        @block.vector
        def _(v):
            v.wait_ge(ct.h, ztr_tok)
            v.tensor_copy(zt[:, :], p_ztr[:, :]).then_inc(cv.h, 1)
        zt_tok = cv.inc()
        bufs['prev_zt'] = zt_tok
        @block.vector
        def _(v):
            v.wait_ge(cv.h, zt_tok)
            v.wait_ge(cv.h, zt_tok)
            v.reduce_max(nmax.rearrange("p (b o) -> p b o", o=1)[:, :, :],
                         zt.rearrange("p (b k) -> p b k", k=8)[:, :, :],
                         axis=AX.X, negate=True).then_inc(cv.h, 1)
        nmax_tok = cv.inc()
        @block.scalar
        def _(s):
            s.wait_ge(cv.h, nmax_tok)
            for b in range(8):
                inst = s.activation(ez[:, 8 * b:8 * b + 8],
                                    zt[:, 8 * b:8 * b + 8],
                                    AF.Exp, bias=nmax[:, b:b + 1])
            inst.then_inc(cs.h, 1)
        ez_tok = cs.inc()
        @block.vector
        def _(v):
            v.wait_ge(cs.h, ez_tok)
            v.reduce_sum(ssum.rearrange("p (b o) -> p b o", o=1)[:, :, :],
                         ez.rearrange("p (b k) -> p b k", k=8)[:, :, :],
                         axis=AX.X).then_inc(cv.h, 1)
            v.wait_ge(cv.h, cv.n + 1)
            v.reciprocal(rcp[:, :], ssum[:, :]).then_inc(cv.h, 1)
            v.wait_ge(cv.h, cv.n + 2)
            v.tensor_scalar(rcp[:, :], rcp[:, :], 0.5, None,
                            ALU.mult).then_inc(cv.h, 1)
            v.wait_ge(cv.h, cv.n + 3)
            for b in range(8):
                inst = v.tensor_scalar(alpha_tm[:, 8 * b:8 * b + 8],
                                       ez[:, 8 * b:8 * b + 8], rcp[:, b:b + 1],
                                       None, ALU.mult)
            inst.then_inc(cv.h, 1)
        cv.inc(3)
        alpha_tok = cv.inc()
        @block.tensor
        def _(t):
            t.wait_ge(cv.h, alpha_tok)
            for blk in range(8):
                mi = t.transpose(p_a8[:, 128 * blk:128 * blk + 128],
                                 alpha_tm[:, 8 * blk:8 * blk + 8],
                                 bufs['ident_bf'][:, :])
            mi.then_inc(ct.h, 1)
        a8_tok = ct.inc()
        @block.vector
        def _(v):
            v.wait_ge(ct.h, a8_tok)
            v.tensor_copy(a8[:, :], p_a8[:, :]).then_inc(cv.h, 1)
        a8sb_tok = cv.inc()
        vx_toks = {}
        vav = wslot[0].rearrange("p (k m) -> p k m", k=NCH)
        vsum_tok = None
        if first:
            @block.tensor
            def _(t, wlast=wlast):
                t.wait_ge(v_tok[0], v_tok[1])
                t.wait_ge(wlast[0], wlast[1])
                for k in range(NCH):
                    mi = t.matmul(P[0][0:1, 0:512], bufs['ones_bf'][:, 0:1],
                                  vav[:, k, :], start=(k == 0),
                                  stop=(k == NCH - 1))
                mi.then_inc(ct.h, 1)
            vs_mm = ct.inc()
            @block.vector
            def _(v, vs_mm=vs_mm):
                v.wait_ge(ct.h, vs_mm)
                v.tensor_copy(bufs['vsum'][:, :],
                              P[0][0:1, 0:512]).then_inc(cv.h, 1)
            vsum_tok = cv.inc()
        for j in range(4):
            par = j % 2
            @block.tensor
            def _(t, j=j, par=par, vsum_tok=vsum_tok):
                if j == 0:
                    t.wait_ge(v_tok[0], v_tok[1])
                    t.wait_ge(cv.h, mr_tok)
                    if vsum_tok is not None:
                        t.wait_ge(cv.h, vsum_tok)
                if j >= 2:
                    t.wait_ge(cv.h, vx_toks[j - 2])
                for k in range(NCH):
                    lhsT = vav[:, k, 128 * j:128 * j + 128]
                    for half in range(2):
                        mi = t.matmul(
                            P[par][:, 512 * half:512 * half + 512], lhsT,
                            xhat[:, NTOK * k + 512 * half:
                                 NTOK * k + 512 * half + 512],
                            start=(k == 0), stop=False)
                lhsT = bufs['vsum'][:, 128 * j:128 * j + 128]
                for half in range(2):
                    mi = t.matmul(P[par][:, 512 * half:512 * half + 512],
                                  lhsT,
                                  mrr[:, 512 * half:512 * half + 512],
                                  start=False, stop=True)
                mi.then_inc(ct.h, 1)
            vmm_tok = ct.inc()
            @block.vector
            def _(v, j=j, par=par, vmm_tok=vmm_tok):
                v.wait_ge(ct.h, vmm_tok)
                v.tensor_copy(vx[:, NTOK * j:NTOK * (j + 1)],
                              P[par][:, :]).then_inc(cv.h, 1)
            vx_toks[j] = cv.inc()
        avx_toks = {}
        for j in range(4):
            par = j % 2
            @block.tensor
            def _(t, j=j, par=par):
                if j == 0:
                    t.wait_ge(cv.h, a8sb_tok)
                    t.wait_ge(cv.h, vx_toks[3])
                else:
                    t.wait_ge(cv.h, avx_toks[j - 1])
                for half in range(2):
                    mi = t.matmul(P[par][:, 512 * half:512 * half + 512],
                                  bufs['repsel_sb'][:, 128 * j:128 * j + 128],
                                  a8[:, 512 * half:512 * half + 512],
                                  start=True, stop=True)
                mi.then_inc(ct.h, 1)
            arep_tok = ct.inc()
            @block.vector
            def _(v, j=j, par=par, arep_tok=arep_tok):
                v.wait_ge(ct.h, arep_tok)
                v.wait_ge(cv.h, vx_toks[j])
                v.tensor_mul(vx[:, NTOK * j:NTOK * (j + 1)],
                             vx[:, NTOK * j:NTOK * (j + 1)],
                             P[par][:, :]).then_inc(cv.h, 1)
            avx_toks[j] = cv.inc()
        res_toks = {}
        uav = wslot[1].rearrange("p (j k m) -> p j k m", j=4, k=NCH)
        for m in range(NCH):
            par = m % 2
            @block.tensor
            def _(t, m=m, par=par):
                if m == 0:
                    t.wait_ge(cv.h, avx_toks[3])
                    t.wait_ge(u_tok[0], u_tok[1])
                if m >= 2:
                    t.wait_ge(cv.h, res_toks[m - 2])
                for j in range(4):
                    lhsT = uav[:, j, m, :]
                    for half in range(2):
                        mi = t.matmul(
                            P[par][:, 512 * half:512 * half + 512], lhsT,
                            vx[:, NTOK * j + 512 * half:
                               NTOK * j + 512 * half + 512],
                            start=(j == 0), stop=(j == 3))
                mi.then_inc(ct.h, 1)
            umm_tok = ct.inc()
            @block.vector
            def _(v, m=m, par=par, umm_tok=umm_tok):
                v.wait_ge(ct.h, umm_tok)
                v.tensor_add(x_chunks[m], x_chunks[m],
                             P[par][:, :]).then_inc(cv.h, 1)
            res_toks[m] = cv.inc()
        load_fence(block, ('tensor', 'scalar'), cv.h, res_toks[NCH - 1])
        return res_toks


def alloc_common(nc, e):
    b = {}
    b['w2_sb'] = e(nc.sbuf_tensor("w2_sb", [128, NCH * 8], BF16))
    b['b1_sb'] = e(nc.sbuf_tensor("b1_sb", [128, NCH], F32))
    b['b2_sb'] = e(nc.sbuf_tensor("b2_sb", [8, 1], F32))
    b['repsel_sb'] = e(nc.sbuf_tensor("repsel_sb", [8, 4 * 128], BF16))
    b['ident_f'] = e(nc.sbuf_tensor("ident_f", [128, 128], F32))
    b['ident_bf'] = e(nc.sbuf_tensor("ident_bf", [128, 128], BF16))
    b['ones_f'] = e(nc.sbuf_tensor("ones_f", [128, 1], F32))
    b['ones_bf'] = e(nc.sbuf_tensor("ones_bf", [128, 1], BF16))
    b['onesr_f'] = e(nc.sbuf_tensor("onesr_f", [1, 128], F32))
    b['onesr_bf'] = e(nc.sbuf_tensor("onesr_bf", [1, 128], BF16))
    b['consts'] = e(nc.sbuf_tensor("consts_sb", [1, 2], F32))
    return b


def emit_common_loads(nc, block, bufs, dr, S):
    dsy, dconst = S['dsy'], S['dconst']
    @block.sync
    def _(sy):
        sy.dma_start(out=bufs['b1_sb'][:, :], in_=dr['b1'][:, :]).then_inc(dsy.h, 16)
        sy.dma_start(out=bufs['b2_sb'][:, :], in_=dr['b2'][:, :]).then_inc(dsy.h, 16)
        sy.dma_start(out=bufs['ident_f'][:, :], in_=dr['ident'][:, :]).then_inc(dsy.h, 16)
        sy.dma_start(out=bufs['ones_f'][:, :], in_=dr['ones'][:, :]).then_inc(dsy.h, 16)
        sy.dma_start(out=bufs['onesr_f'][:, :], in_=dr['onesr'][:, :]).then_inc(dsy.h, 16)
        sy.dma_start(out=bufs['consts'][:, :], in_=dr['consts'][:, :]).then_inc(dsy.h, 16)
    dsy.inc(16 * 6)
    @block.gpsimd
    def _(g):
        g.dma_start(out=bufs['w2_sb'][:, :], in_=dr['w2'][:, :]).then_inc(dconst.h, 16)
        g.dma_start(out=bufs['repsel_sb'][:, :], in_=dr['repsel'][:, :]).then_inc(dconst.h, 16)
        g.dma_start(out=bufs['ident_bf'][:, :], in_=dr['ident'][:, :]).then_inc(dconst.h, 16)
        g.dma_start(out=bufs['ones_bf'][:, :], in_=dr['ones'][:, :]).then_inc(dconst.h, 16)
        g.dma_start(out=bufs['onesr_bf'][:, :], in_=dr['onesr'][:, :]).then_inc(dconst.h, 16)
    dconst.inc(16 * 5)



def preload_set6(block):
    @block.scalar
    def _(s):
        s.add_instruction(mybir.InstLoadActFuncSet(
            name=s.bass.get_next_instruction_name(),
            act_func_set_id=6, ins=[], outs=[]))



def build_fused():
    nc = bass.Bass("TRN2", target_bir_lowering=False)
    dr = {}
    def di(name, shape, dt=F32):
        dr[name] = nc.dram_tensor(name, shape, dt, kind="ExternalInput").ap()
    di('xall', [128, 4 * NCH * NTOK])
    di('wkv', [128, NCH * 128])
    di('wqq', [128, 4, NCH * 512])
    di('woq', [128, 4, NCH * 512])
    di('q2l', [128, H * 64])
    di('vup', [64, H * 128])
    di('w1q', [128, 4, NCH * 512])
    di('w2', [128, NCH * 8])
    di('varr', [128, NCH * 512])
    di('uarr', [128, 4 * NCH * 128])
    di('b1', [128, NCH])
    di('b2', [8, 1])
    di('repsel', [8, 4 * 128])
    di('ident', [128, 128])
    di('ones', [128, 1])
    di('onesr', [1, 128])
    di('consts', [1, 2])
    di('bandm', [128, 4 * 256], U8)
    di('fixm', [128, 4 * 256], BF16)
    di('maskneg', [128, 2 * 256])
    di('kwidx', [32, 1], I32)
    dr['yout'] = nc.dram_tensor("yout", [128, NCH * NTOK], F32,
                                kind="ExternalOutput").ap()
    if DEBUG_DUMP:
        dr['dbg1'] = nc.dram_tensor("dbg1", [128, NCH * NTOK], BF16,
                                    kind="ExternalOutput").ap()
        dr['dbg2'] = nc.dram_tensor("dbg2", [128, NT * 66 + NTOK], BF16,
                                    kind="ExternalOutput").ap()
        dr['dbg3'] = nc.dram_tensor("dbg3", [1, 2 * NTOK], F32,
                                    kind="ExternalOutput").ap()
    ktok = nc.dram_tensor("ktok", [NT, 66, 128], BF16, kind="Internal").ap()
    kwin_d = nc.dram_tensor("kwin_d", [32, 33 * 128], BF16, kind="Internal").ap()
    crow_d = nc.dram_tensor("crow_d", [H, NTOK], F32, kind="Internal").ap()

    with contextlib.ExitStack() as stx:
        e = stx.enter_context
        x_res = e(nc.sbuf_tensor("x_res", [128, NCH * NTOK], F32))
        bbuf = e(nc.sbuf_tensor("bbuf", [128, NCH * NTOK], BF16))
        kaug = e(nc.sbuf_tensor("kaug", [66, 4096], BF16))
        vt_sb = e(nc.sbuf_tensor("vt_sb", [64, 4096], BF16))
        bufs = alloc_common(nc, e)
        bufs['hbuf'] = bbuf
        bufs['w1q_dram'] = dr['w1q']

        S = make_sems(nc, e)
        block = e(nc.Block())
        preload_set6(block)
        ct, cs, cv = S['ct'], S['cs'], S['cv']
        cg, dsy, dgp = S['cg'], S['dsy'], S['dgp']
        dlat, dx = S['dlat'], S['dx']

        emit_common_loads(nc, block, bufs, dr, S)
        x_chunks = [x_res[:, NTOK * c:NTOK * (c + 1)] for c in range(NCH)]
        xall4 = dr['xall'].rearrange("p (j n) -> p j n", j=4)

        with contextlib.ExitStack() as ph1:
            xhat = ph1.enter_context(
                nc.sbuf_tensor("xhat", [128, NCH * NTOK], BF16))
            kvout = ph1.enter_context(
                nc.sbuf_tensor("kvout", [128, NTOK], BF16))
            wkv_sb = ph1.enter_context(
                nc.sbuf_tensor("wkv_sb", [128, NCH * 128], BF16))
            bufs['msum_w1'] = ph1.enter_context(
                nc.sbuf_tensor("msum_w1", [1, 2048], BF16))
            bufs['vsum'] = ph1.enter_context(
                nc.sbuf_tensor("vsum", [1, 512], BF16))
            wkvsum = ph1.enter_context(
                nc.sbuf_tensor("wkvsum", [1, 128], BF16))
            mrr1 = ph1.enter_context(nc.sbuf_tensor("mrr1", [1, NTOK], BF16))
            mrr2 = mrr1
            bufs['wslot'] = [ph1.enter_context(
                nc.sbuf_tensor(f"wslotP{i}", [128, NCH * 512], BF16))
                for i in range(2)]
            @block.gpsimd
            def _(g):
                g.dma_start(out=wkv_sb[:, :],
                            in_=dr['wkv'][:, :]).then_inc(S['dconst'].h, 16)
            S['dconst'].inc(16)
            load_fence(block, ('tensor', 'scalar', 'vector'), dsy.h, dsy.n)
            load_fence(block, ('tensor', 'scalar', 'vector'),
                       S['dconst'].h, S['dconst'].n)

            with nc.psum_tensor("wkvs_ps", [1, 128], F32) as wkvs_ps:
                @block.tensor
                def _(t):
                    wv = wkv_sb.rearrange("p (k m) -> p k m", k=NCH)
                    for k in range(NCH):
                        mi = t.matmul(wkvs_ps[:, :], bufs['ones_bf'][:, 0:1],
                                      wv[:, k, :], start=(k == 0),
                                      stop=(k == NCH - 1))
                    mi.then_inc(ct.h, 1)
                wks_mm = ct.inc()
                @block.vector
                def _(v, wks_mm=wks_mm):
                    v.wait_ge(ct.h, wks_mm)
                    v.tensor_copy(wkvsum[:, :],
                                  wkvs_ps[:, :]).then_inc(cv.h, 1)
                wkvsum_tok = cv.inc()

            kaug512 = kaug[0:64, :].rearrange("p (t c) -> p t c", c=512)
            vt512 = vt_sb.rearrange("p (t c) -> p t c", c=512)
            kv512 = kvout.rearrange("p (t c) -> p t c", c=128)
            kv_ps = ph1.enter_context(
                nc.psum_tensor("kv_ps", [128, NTOK], F32))
            ln2_q = None
            kvs_tok = None
            for j in range(4):
                xq_toks = []
                for qd in range(4):
                    if ln2_q is None:
                        rel = wkvsum_tok if qd == 0 else None
                    else:
                        rel = ln2_q[4 * qd + 3]
                    @block.sync
                    def _(sy, j=j, qd=qd, rel=rel, dxn=dx.n):
                        sy.wait_ge(dx.h, dxn)
                        if rel is not None:
                            sy.wait_ge(cv.h, rel)
                        sy.dma_start(
                            out=x_res[:, 4 * NTOK * qd:4 * NTOK * (qd + 1)],
                            in_=xall4[:, j, 4 * NTOK * qd:4 * NTOK * (qd + 1)]
                        ).then_inc(dx.h, 16)
                    xq_toks.append(dx.inc(16))
                w0 = [(dx.h, xq_toks[c // 4]) for c in range(NCH)]
                _, ln1_ap, mr1 = emit_ln(nc, block, bufs, S, x_chunks, xhat,
                                         wait0=w0, mrr=mrr1)
                res_toks = emit_ffn(nc, block, bufs, S, x_chunks, xhat,
                                    dr['varr'], dr['uarr'],
                                    rhs_ready=[(cv.h, t) for t in ln1_ap],
                                    mrr=mrr1, mr_tok=mr1,
                                    first=(j == 0), zps=kv_ps)
                w2l = [(cv.h, res_toks[c]) for c in range(NCH)]
                _, ln2_q, mr2 = emit_ln(nc, block, bufs, S, x_chunks, xhat,
                                        wait0=w2l, mrr=mrr2)
                if True:
                    @block.tensor
                    def _(t, mr2=mr2, ln2_q=ln2_q, pzw=bufs['prev_zt']):
                        t.wait_ge(cv.h, mr2)
                        t.wait_ge(cv.h, pzw)
                        wv = wkv_sb.rearrange("p (k m) -> p k m", k=NCH)
                        for k in range(NCH):
                            t.wait_ge(cv.h, ln2_q[k])
                            for half in range(2):
                                mi = t.matmul(
                                    kv_ps[:, 512 * half:512 * half + 512],
                                    wv[:, k, :],
                                    xhat[:, NTOK * k + 512 * half:
                                         NTOK * k + 512 * half + 512],
                                    start=(k == 0), stop=False)
                        for half in range(2):
                            mi = t.matmul(
                                kv_ps[:, 512 * half:512 * half + 512],
                                wkvsum[:, :],
                                mrr2[:, 512 * half:512 * half + 512],
                                start=False, stop=True)
                        mi.then_inc(ct.h, 1)
                    kv_tok = ct.inc()
                    @block.vector
                    def _(v, kv_tok=kv_tok, dn=dlat.n):
                        v.wait_ge(ct.h, kv_tok)
                        v.wait_ge(dlat.h, dn)
                        v.tensor_copy(kvout[:, :],
                                      kv_ps[:, :]).then_inc(cv.h, 1)
                    kvs_tok = cv.inc()
                    bufs['prev_kvs'] = kvs_tok
                @block.gpsimd
                def _(g, j=j, kvs_tok=kvs_tok):
                    g.wait_ge(cv.h, kvs_tok)
                    g.dma_start(out=kaug512[:, :, 128 * j:128 * j + 128],
                                in_=kv512[0:64, :, :]).then_inc(dlat.h, 16)
                    g.dma_start(out=vt512[:, :, 128 * j:128 * j + 128],
                                in_=kv512[64:128, :, :]).then_inc(dlat.h, 16)
                dlat.inc(32)

            def qt_cb(m, pap, mm_tok):
                @block.vector
                def _(v, m=m, pap=pap, mm_tok=mm_tok):
                    v.wait_ge(ct.h, mm_tok)
                    v.tensor_copy(bbuf[:, NTOK * m:NTOK * (m + 1)],
                                  pap[:, :]).then_inc(cv.h, 1)
                return (cv.h, cv.inc())

            ms_wq = []
            (_, wq_last), _sf = emit_big_proj(
                nc, block, bufs, S, dr['wqq'], xhat, qt_cb,
                rhs_ready=[(cv.h, ln2_q[k]) for k in range(NCH)],
                fixup=(bufs['msum_w1'], ms_wq, mrr2, mr2),
                colsum=(bufs['msum_w1'], ms_wq))

        @block.gpsimd
        def _(g):
            g.wait_ge(ct.h, ct.n)
            g.wait_ge(cs.h, cs.n)
        @block.sync
        def _(sy):
            sy.wait_ge(ct.h, ct.n)
            sy.wait_ge(cs.h, cs.n)

        with contextlib.ExitStack() as attn_stack:
            ea = attn_stack.enter_context
            kwin_sb = ea(nc.sbuf_tensor("kwin_sb", [66, 16 * 128], BF16))
            ksum_bf = ea(nc.sbuf_tensor("ksum_bf", [1, 4096], BF16))
            km2 = ea(nc.sbuf_tensor("km2", [1, 1], F32))
            kmt = ea(nc.sbuf_tensor("kmt", [1, 1], F32))
            hlk = ea(nc.sbuf_tensor("hlk", [128, 1], F32))
            idxt = ea(nc.sbuf_tensor("idxt", [32, 1], I32))
            bandm_sb = ea(nc.sbuf_tensor("bandm_sb", [128, 4 * 256], U8))
            fixm_sb = ea(nc.sbuf_tensor("fixm_sb", [128, 4 * 256], BF16))
            maskneg_sb = ea(nc.sbuf_tensor("maskneg_sb", [128, 2 * 256], F32))
            q2l_sb = ea(nc.sbuf_tensor("q2l_sb", [128, H * 64], BF16))
            vup_sb = ea(nc.sbuf_tensor("vup_sb", [64, H * 128], BF16))
            qsh_bf = ea(nc.sbuf_tensor("qsh_bf", [1, NTOK], BF16))
            ksumT = ea(nc.sbuf_tensor("ksumT", [128, NT], F32))
            ksT32 = ea(nc.sbuf_tensor("ksT32", [NT, 128], F32))
            ob_sb = [ea(nc.sbuf_tensor(f"ob{i}", [128, 256], BF16))
                     for i in range(2)]
            cb_sb = ea(nc.sbuf_tensor("cb_sb", [1, NTOK], F32))
            cfin = ea(nc.sbuf_tensor("cfin", [1, NTOK], F32))
            crow_bf = ea(nc.sbuf_tensor("crow_bf", [1, NTOK], BF16))
            bm_row = ea(nc.sbuf_tensor("bm_row", [1, NTOK], F32))
            sums_sb = ea(nc.sbuf_tensor("sums_sb", [1, NTOK], F32))
            rep_sb = ea(nc.sbuf_tensor("rep_sb", [128, NTOK], F32))
            rcp_row = ea(nc.sbuf_tensor("rcp_row", [1, NTOK], F32))
            bmaxT = ea(nc.sbuf_tensor("bmaxT", [128, 8], F32))
            bmax8 = ea(nc.sbuf_tensor("bmax8", [8, 128], F32))
            sqh = ea(nc.sbuf_tensor("sqh", [64, NTOK], BF16))
            qlat_b = [ea(nc.sbuf_tensor(f"qlat_h{i}", [66, NTOK], BF16))
                      for i in range(2)]
            qshrep_b = [ea(nc.sbuf_tensor(f"qshrepb{i}", [128, NTOK], BF16))
                        for i in range(2)]
            vtT = ea(nc.sbuf_tensor("vtT", [128, NT * 66], BF16))
            cv_sb = ea(nc.sbuf_tensor("cv_sb", [64, NTOK], BF16))
            probs = [ea(nc.sbuf_tensor(f"probs{i}", [128, NTOK], BF16))
                     for i in range(2)]
            mr_sb = [ea(nc.sbuf_tensor(f"mr{i}", [128, 256], BF16))
                     for i in range(4)]
            madd_sb = [ea(nc.sbuf_tensor(f"madd{i}", [128, 256], F32))
                       for i in range(2)]
            madd2_sb = [ea(nc.sbuf_tensor(f"madd2_{i}", [128, 256], F32))
                        for i in range(2)]

            @block.sync
            def _(sy):
                sy.dma_start(out=idxt[:, :], in_=dr['kwidx'][:, :]).then_inc(dsy.h, 16)
                sy.dma_start(out=fixm_sb[:, :], in_=dr['fixm'][:, :]).then_inc(dsy.h, 16)
                sy.dma_start(out=maskneg_sb[:, :], in_=dr['maskneg'][:, :]).then_inc(dsy.h, 16)
            dsy.inc(16 * 3)
            @block.gpsimd
            def _(g):
                g.dma_start(out=q2l_sb[:, :], in_=dr['q2l'][:, :]).then_inc(dlat.h, 16)
                g.dma_start(out=vup_sb[:, :], in_=dr['vup'][:, :]).then_inc(dlat.h, 16)
                g.dma_start(out=bandm_sb[:, :], in_=dr['bandm'][:, :]).then_inc(dlat.h, 16)
            lat_tok = dlat.inc(16 * 3)
            @block.gpsimd
            def _(g):
                g.wait_ge(dlat.h, lat_tok)
                g.memset(kaug[64:65, :], 1.0).then_inc(cg.h, 1)
            ones_row_tok = cg.inc()
            load_fence(block, ('tensor', 'scalar', 'vector'), dsy.h, dsy.n)
            load_fence(block, ('tensor', 'scalar', 'vector'), dlat.h, lat_tok)

            ksum_evac = {}
            knmax = {}
            with contextlib.ExitStack() as kst:
                ksq = kst.enter_context(
                    nc.sbuf_tensor("ksq", [64, 4096], BF16))
                ks_ps = kst.enter_context(nc.psum_tensor("ks_ps", [1, 512], F32))
                kn_ps = kst.enter_context(nc.psum_tensor("kn_ps", [1, 512], F32))
                hb_ps = kst.enter_context(nc.psum_tensor("hb_ps", [128, 1], F32))
                @block.scalar
                def _(s):
                    s.activation(ksq[:, :], kaug[0:64, :],
                                 AF.Square).then_inc(cs.h, 1)
                ksq_tok = cs.inc()
                for cch in range(8):
                    @block.tensor
                    def _(t, cch=cch):
                        if cch == 0:
                            t.wait_ge(cs.h, ksq_tok)
                        else:
                            t.wait_ge(cv.h, ksum_evac[cch - 1])
                            t.wait_ge(cv.h, knmax[cch - 1])
                        t.matmul(ks_ps[:, :], bufs['ones_bf'][0:64, 0:1],
                                 kaug[0:64, 512 * cch:512 * cch + 512],
                                 start=True, stop=True)
                        t.matmul(kn_ps[:, :], bufs['ones_bf'][0:64, 0:1],
                                 ksq[:, 512 * cch:512 * cch + 512],
                                 start=True, stop=True).then_inc(ct.h, 1)
                    kmm = ct.inc()
                    @block.vector
                    def _(v, cch=cch, kmm=kmm):
                        v.wait_ge(ct.h, kmm)
                        v.tensor_copy(ksum_bf[:, 512 * cch:512 * cch + 512],
                                      ks_ps[:, :]).then_inc(cv.h, 1)
                    ksum_evac[cch] = cv.inc()
                    @block.vector
                    def _(v, cch=cch, kmm=kmm):
                        v.wait_ge(ct.h, kmm)
                        v.reduce_max(kmt[:, :], kn_ps[:, :],
                                     axis=AX.X).then_inc(cv.h, 1)
                        v.wait_ge(cv.h, cv.n + 1)
                        if cch == 0:
                            inst = v.tensor_copy(km2[:, :], kmt[:, :])
                        else:
                            inst = v.tensor_max(km2[:, :], km2[:, :], kmt[:, :])
                        inst.then_inc(cv.h, 1)
                    cv.inc(1)
                    knmax[cch] = cv.inc()
                @block.scalar
                def _(s):
                    s.wait_ge(cv.h, knmax[7])
                    s.wait_ge(cv.h, ksum_evac[7])
                    s.activation(kmt[:, :], km2[:, :], AF.Ln).then_inc(cs.h, 1)
                lnk_tok = cs.inc()
                @block.tensor
                def _(t):
                    t.wait_ge(cs.h, lnk_tok)
                    t.matmul(hb_ps[:, :], bufs['onesr_f'][:, :], kmt[:, :],
                             start=True, stop=True).then_inc(ct.h, 1)
                hb_tok = ct.inc()
                @block.vector
                def _(v):
                    v.wait_ge(ct.h, hb_tok)
                    v.tensor_scalar(hlk[:, :], hb_ps[:, :], 0.5, None,
                                    ALU.mult).then_inc(cv.h, 1)
                hlk_tok = cv.inc()
            @block.sync
            def _(sy):
                sy.wait_ge(cs.h, lnk_tok)
                sy.wait_ge(cg.h, ones_row_tok)
                sy.dma_start(out=kaug[65:66, :],
                             in_=ksum_bf[:, :]).then_inc(dsy.h, 16)
            r65_tok = dsy.inc(16)
            @block.sync
            def _(sy):
                sy.wait_ge(dsy.h, r65_tok)
                sy.dma_start(out=ktok.rearrange("a b c -> b a c"),
                             in_=kaug.rearrange("p (a c) -> p a c", c=128)
                             ).then_inc(dsy.h, 16)
            ktok_tok = dsy.inc(16)
            @block.gpsimd
            def _(g):
                g.wait_ge(dsy.h, ktok_tok)
                g.dma_start(out=ksT32[:, :],
                            in_=ktok[:, 65, :]).then_inc(dlat.h, 16)
            kst_tok = dlat.inc(16)
            with nc.sbuf_tensor("kwtmp", [32, 33 * 128], BF16) as kwtmp:
                @block.gpsimd
                def _(g):
                    g.wait_ge(dsy.h, ktok_tok)
                    g.indirect_dma_start(
                        kwtmp[:, :], None,
                        ktok.rearrange("a (h bb) c -> (a h) (bb c)", h=2),
                        IndirectOffsetOnAxis(ap=idxt[:, :], axis=0)
                    ).then_inc(dgp.h, 16)
                gath_tok = dgp.inc(16)
                @block.sync
                def _(sy):
                    sy.wait_ge(dgp.h, gath_tok)
                    sy.dma_start(out=kwin_d[:, :],
                                 in_=kwtmp[:, :]).then_inc(dsy.h, 16)
            kwd_tok = dsy.inc(16)
            @block.sync
            def _(sy):
                sy.wait_ge(dsy.h, kwd_tok)
                sy.dma_start(
                    out=kwin_sb.rearrange("p (a c) -> p a c", c=128),
                    in_=kwin_d.rearrange("(a h) (bb c) -> (h bb) a c",
                                         h=2, c=128)
                ).then_inc(dsy.h, 16)
            kwin_tok = dsy.inc(16)

            vt_tok = lat_tok

            with contextlib.ExitStack() as ps2:
                PA = ps2.enter_context(nc.psum_tensor("PA", [128, NTOK], F32))
                CV = ps2.enter_context(nc.psum_tensor("CV", [128, NTOK], F32))
                @block.tensor
                def _(t):
                    t.wait_ge(dlat.h, kst_tok)
                    t.transpose(PA[:, 0:32], ksT32[:, :],
                                bufs['ident_f'][0:32, 0:32]).then_inc(ct.h, 1)
                kstr_tok = ct.inc()
                @block.vector
                def _(v, kstr_tok=kstr_tok):
                    v.wait_ge(ct.h, kstr_tok)
                    v.tensor_copy(ksumT[:, :], PA[:, 0:32]).then_inc(cv.h, 1)
                ksumT_cv = cv.inc()
                @block.vector
                def _(v):
                    for r in range(4):
                        v.tensor_copy(mr_sb[r][:, :],
                                      fixm_sb[:, 256 * r:256 * r + 256]
                                      ).then_inc(cv.h, 1)
                cv.inc(4)
                with nc.psum_tensor("vtp", [128, NT * 64], BF16) as vtp:
                    @block.tensor
                    def _(t):
                        t.wait_ge(dlat.h, vt_tok)
                        for s in range(NT):
                            mi = t.transpose(vtp[:, 64 * s:64 * s + 64],
                                             vt_sb[:, 128 * s:128 * s + 128],
                                             bufs['ident_bf'][0:64, 0:64])
                        mi.then_inc(ct.h, 1)
                    vtp_tok = ct.inc()
                    @block.vector
                    def _(v, vtp_tok=vtp_tok):
                        v.wait_ge(ct.h, vtp_tok)
                        for s in range(NT):
                            v.tensor_copy(vtT[:, 66 * s:66 * s + 64],
                                          vtp[:, 64 * s:64 * s + 64])
                            mi = v.tensor_copy(
                                vtT[:, 66 * s + 64:66 * s + 65],
                                bufs['ones_bf'][:, 0:1])
                        mi.then_inc(cv.h, 1)
                    vtT_tok = cv.inc()
                PC = ps2.enter_context(
                    nc.psum_tensor("PC", [128, 2 * NTOK], F32))
                PF = {'pa': (cv.h, ksumT_cv), 'pb': (cv.h, 0),
                      'pc': (cv.h, 0), 'pdfree': 0, 'ctx_last': 0}
                sco_last = {-2: 0, -1: 0}
                A = {}
                ctxout_tok = 0
                ob64 = bufs['ones_bf'][0:64, 0:1]

                def emit_A_stage(h, k):
                    st = A.setdefault(h, {})
                    par = h % 2
                    ql = qlat_b[par]
                    pv = A.get(h - 1)
                    if k == 0:
                        ps = sco_last[h - 2]
                        pn = pv['nsq'] if pv else 0
                        st['ps'] = ps
                        @block.tensor
                        def _(t, h=h, paf=PF['pa']):
                            t.wait_ge(paf[0], paf[1])
                            if h == 0:
                                t.wait_ge(dsy.h, kwin_tok)
                            lhsT = q2l_sb[:, 64 * h:64 * h + 64]
                            t.matmul(PA[0:64, 0:512], lhsT,
                                     bbuf[:, NTOK * h:NTOK * h + 512],
                                     start=True, stop=True)
                            t.matmul(PA[0:64, 512:1024], lhsT,
                                     bbuf[:, NTOK * h + 512:NTOK * h + 1024],
                                     start=True, stop=True).then_inc(ct.h, 1)
                        qmm = ct.inc()
                        @block.vector
                        def _(v, qmm=qmm, ps=ps, pn=pn, ql=ql):
                            v.wait_ge(ct.h, qmm)
                            v.wait_ge(ct.h, ps)
                            v.wait_ge(ct.h, pn)
                            v.tensor_scalar(ql[0:64, :], PA[0:64, :], 0.125,
                                            None, ALU.mult).then_inc(cv.h, 1)
                            v.wait_ge(cv.h, cv.n + 1)
                            v.tensor_mul(sqh[:, :], ql[0:64, :],
                                         ql[0:64, :]).then_inc(cv.h, 1)
                        cv.inc(1)
                        st['qevac'] = cv.inc()
                        @block.gpsimd
                        def _(g, ps=ps, ql=ql):
                            g.wait_ge(ct.h, ps)
                            g.memset(ql[64:65, :], 0.0).then_inc(cg.h, 1)
                        st['z64'] = cg.inc()
                    elif k == 1:
                        @block.tensor
                        def _(t, qevac=st['qevac'], ql=ql):
                            t.wait_ge(cv.h, qevac)
                            t.matmul(PA[0:1, 0:512], ob64, ql[0:64, 0:512],
                                     start=True, stop=True)
                            t.matmul(PA[0:1, 512:1024], ob64,
                                     ql[0:64, 512:1024],
                                     start=True, stop=True).then_inc(ct.h, 1)
                        qs = ct.inc()
                        @block.vector
                        def _(v, qs=qs, pv=pv):
                            v.wait_ge(ct.h, qs)
                            if pv:
                                v.wait_ge(dsy.h, pv['r65'])
                            v.tensor_scalar(qsh_bf[:, :], PA[0:1, :], 0.5,
                                            None, ALU.mult).then_inc(cv.h, 1)
                        st['qshv'] = cv.inc()
                    elif k == 2:
                        @block.tensor
                        def _(t, qshv=st['qshv']):
                            t.wait_ge(cv.h, qshv)
                            t.matmul(PA[0:1, 0:512], ob64, sqh[:, 0:512],
                                     start=True, stop=True)
                            t.matmul(PA[0:1, 512:1024], ob64,
                                     sqh[:, 512:1024],
                                     start=True, stop=True).then_inc(ct.h, 1)
                        st['nsq'] = ct.inc()
                        @block.scalar
                        def _(s, h=h, nsq=st['nsq'], pv=pv):
                            s.wait_ge(ct.h, nsq)
                            if h == 0:
                                s.wait_ge(cv.h, hlk_tok)
                            if pv:
                                s.wait_ge(cv.h, pv['negc'])
                            s.activation(cfin[:, :], PA[0:1, :],
                                         AF.Ln).then_inc(cs.h, 1)
                            s.wait_ge(cs.h, cs.n + 1)
                            s.activation(cb_sb[:, :], cfin[:, :], AF.Exp,
                                         scale=0.5,
                                         bias=hlk[0:1, 0:1]).then_inc(cs.h, 1)
                        cs.inc(1)
                        st['cb'] = cs.inc()
                        @block.sync
                        def _(sy, qshv=st['qshv'], cbt=st['cb'], ql=ql):
                            sy.wait_ge(cv.h, qshv)
                            sy.wait_ge(cs.h, cbt)
                            sy.dma_start(out=ql[65:66, :],
                                         in_=qsh_bf[:, :]).then_inc(dsy.h, 16)
                        st['r65'] = dsy.inc(16)
                    elif k == 3:
                        @block.tensor
                        def _(t, cbt=st['cb']):
                            t.wait_ge(cs.h, cbt)
                            obr = bufs['onesr_bf'][:, :]
                            t.matmul(PA[:, 0:512], obr, qsh_bf[:, 0:512],
                                     start=True, stop=True)
                            t.matmul(PA[:, 512:1024], obr,
                                     qsh_bf[:, 512:1024],
                                     start=True, stop=True).then_inc(ct.h, 1)
                        qrep = ct.inc()
                        @block.vector
                        def _(v, qrep=qrep, par=par):
                            v.wait_ge(ct.h, qrep)
                            v.tensor_copy(qshrep_b[par][:, :],
                                          PA[:, :]).then_inc(cv.h, 1)
                        st['qrepv'] = cv.inc()
                        st['pp_evac'] = {}
                        st['ma'] = {}
                        st['bm'] = {}
                    elif 4 <= k <= 11:
                        j = k - 4
                        ppar = j % 2
                        pp_evac, ma_toks, bm_toks = \
                            st['pp_evac'], st['ma'], st['bm']
                        @block.tensor
                        def _(t, j=j, ppar=ppar, st=st, ql=ql):
                            if j == 0:
                                t.wait_ge(dsy.h, st['r65'])
                                t.wait_ge(cg.h, st['z64'])
                                t.wait_ge(cv.h, st['qrepv'])
                            if j >= 2:
                                t.wait_ge(cv.h, pp_evac[j - 2])
                            t.matmul(PA[:, 512 * ppar:512 * ppar + 256],
                                     ql[0:66, 128 * j:128 * j + 128],
                                     kwin_sb[0:66, 256 * j:256 * j + 256],
                                     start=True, stop=True).then_inc(ct.h, 1)
                        pp_tok = ct.inc()
                        slot = 0 if j == 0 else 1
                        @block.vector
                        def _(v, j=j, ppar=ppar, pp_tok=pp_tok, pv=pv,
                              slot=slot):
                            v.wait_ge(ct.h, pp_tok)
                            v.tensor_tensor(madd_sb[ppar][:, :],
                                            PA[:, 512 * ppar:512 * ppar + 256],
                                            maskneg_sb[:, 256 * slot:
                                                       256 * slot + 256],
                                            ALU.add).then_inc(cv.h, 1)
                        pp_evac[j] = cv.inc()
                        @block.vector
                        def _(v, j=j, ppar=ppar, pv=pv, pe=pp_evac[j]):
                            v.wait_ge(cv.h, pe)
                            if j < 2 and pv:
                                v.wait_ge(ct.h, pv['btr'])
                            v.reduce_max(bmaxT[:, j:j + 1],
                                         madd_sb[ppar][:, :],
                                         axis=AX.X).then_inc(cv.h, 1)
                        bm_toks[j] = cv.inc()
                    elif k == 12:
                        @block.tensor
                        def _(t, bm7=st['bm'][7]):
                            t.wait_ge(cv.h, bm7)
                            t.transpose(PA[0:8, 0:128], bmaxT[:, :],
                                        bufs['ident_f'][:, :]
                                        ).then_inc(ct.h, 1)
                        st['btr'] = ct.inc()
                        @block.vector
                        def _(v, btr=st['btr'], pv=pv):
                            v.wait_ge(ct.h, btr)
                            if pv:
                                v.wait_ge(dsy.h, pv['crow_out'])
                            v.tensor_copy(bmax8[:, :],
                                          PA[0:8, 0:128]).then_inc(cv.h, 1)
                        st['b8'] = cv.inc()
                        @block.sync
                        def _(sy, h=h, b8=st['b8']):
                            sy.wait_ge(cv.h, b8)
                            sy.dma_start(
                                out=crow_d[h:h + 1, :].rearrange(
                                    "o (a c) -> (o a) c", a=8),
                                in_=bmax8[:, :]).then_inc(dsy.h, 16)
                        st['crow_out'] = dsy.inc(16)
                        @block.sync
                        def _(sy, co=st['crow_out'], pv=pv):
                            sy.wait_ge(dsy.h, co)
                            if pv:
                                sy.wait_ge(cv.h, pv['negc'])
                            sy.dma_start(out=bm_row[:, :],
                                         in_=crow_d[h:h + 1, :]
                                         ).then_inc(dsy.h, 16)
                        st['bmrow'] = dsy.inc(16)
                    elif k == 13:
                        @block.vector
                        def _(v, bmrow=st['bmrow'], cbt=st['cb'], pv=pv):
                            v.wait_ge(dsy.h, bmrow)
                            v.wait_ge(cs.h, cbt)
                            if pv:
                                v.wait_ge(dsy.h, pv['r64'])
                            v.tensor_max(cfin[:, :], cb_sb[:, :],
                                         bm_row[:, :]).then_inc(cv.h, 1)
                            v.wait_ge(cv.h, cv.n + 1)
                            v.tensor_scalar(crow_bf[:, :], cfin[:, :],
                                            -1.004, -0.5, ALU.mult,
                                            ALU.add).then_inc(cv.h, 1)
                        cv.inc(1)
                        st['negc'] = cv.inc()
                        @block.sync
                        def _(sy, negc=st['negc'], ql=ql):
                            sy.wait_ge(cv.h, negc)
                            sy.dma_start(out=ql[64:65, :],
                                         in_=crow_bf[:, :]).then_inc(dsy.h, 16)
                        st['r64'] = dsy.inc(16)

                def emit_epi_sums(h, half, sc_tok):
                    lo, hi = 512 * half, 512 * half + 512
                    @block.vector
                    def _(v, sc_tok=sc_tok, lo=lo, hi=hi,
                          cl=PF['ctx_last']):
                        v.wait_ge(ct.h, sc_tok)
                        v.wait_ge(ct.h, cl)
                        v.tensor_copy(cv_sb[:, lo:hi],
                                      CV[0:64, lo:hi]).then_inc(cv.h, 1)
                        v.tensor_copy(sums_sb[:, lo:hi],
                                      CV[64:65, lo:hi]).then_inc(cv.h, 1)
                    cv.inc(1)
                    s_tok = cv.inc()
                    @block.scalar
                    def _(s, s_tok=s_tok, lo=lo, hi=hi):
                        s.wait_ge(cv.h, s_tok)
                        s.activation(rcp_row[:, lo:hi], sums_sb[:, lo:hi],
                                     AF.Ln).then_inc(cs.h, 1)
                        s.wait_ge(cs.h, cs.n + 1)
                        s.activation(rcp_row[:, lo:hi], rcp_row[:, lo:hi],
                                     AF.Exp, scale=-1.0).then_inc(cs.h, 1)
                    cs.inc(1)
                    return (s_tok, cs.inc())

                def emit_epi_rep(h, half, rcp_tok):
                    lo, hi = 512 * half, 512 * half + 512
                    nxt = A.get(h + 1)
                    ve_war = nxt['b8'] if nxt else 0
                    @block.tensor
                    def _(t, rcp_tok=rcp_tok, ve_war=ve_war, lo=lo, hi=hi,
                          pdf=PF['pdfree']):
                        t.wait_ge(cs.h, rcp_tok)
                        t.wait_ge(cv.h, ve_war)
                        t.wait_ge(cv.h, pdf)
                        t.matmul(PA[:, lo:hi], bufs['onesr_f'][:, :],
                                 rcp_row[:, lo:hi], start=True,
                                 stop=True).then_inc(ct.h, 1)
                    rep_tok = ct.inc()
                    @block.vector
                    def _(v, rep_tok=rep_tok, lo=lo, hi=hi):
                        v.wait_ge(ct.h, rep_tok)
                        v.tensor_copy(rep_sb[:, lo:hi],
                                      PA[:, lo:hi]).then_inc(cv.h, 1)
                    return (rep_tok, cv.inc())

                def emit_epi_out(h, half, rep, cvsb_tok, pd_free):
                    lo, hi = 512 * half, 512 * half + 512
                    lo2 = 512 * (1 - half)
                    nxt = A.get(h + 1)
                    b8w = nxt['b8'] if (nxt and half == 0) else 0
                    @block.tensor
                    def _(t, h=h, cvsb_tok=cvsb_tok, pd_free=pd_free,
                          lo=lo, hi=hi, lo2=lo2, b8w=b8w):
                        t.wait_ge(cv.h, cvsb_tok)
                        t.wait_ge(cv.h, pd_free)
                        t.wait_ge(cv.h, b8w)
                        t.matmul(PA[:, lo2:lo2 + 512],
                                 vup_sb[:, 128 * h:128 * h + 128],
                                 cv_sb[:, lo:hi], start=True,
                                 stop=True).then_inc(ct.h, 1)
                    ctx_tok = ct.inc()
                    PF['ctx_last'] = ctx_tok
                    @block.vector
                    def _(v, h=h, rep=rep, ctx_tok=ctx_tok, lo=lo, hi=hi,
                          lo2=lo2):
                        v.wait_ge(cv.h, rep[1])
                        v.wait_ge(ct.h, ctx_tok)
                        v.tensor_mul(bbuf[:, NTOK * h + lo:NTOK * h + hi],
                                     rep_sb[:, lo:hi],
                                     PA[:, lo2:lo2 + 512]).then_inc(cv.h, 1)
                    return cv.inc()

                epi = {}
                for k in range(14):
                    emit_A_stage(0, k)
                PF['pa'] = (cv.h, A[0]['b8'])
                for h in range(H):
                    st = A[h]
                    par = h % 2
                    ql = qlat_b[par]
                    qrep_sb = qshrep_b[par]
                    os_toks = {}
                    sco_toks = {}
                    mr_toks = {}
                    madd_toks = {}
                    exp_toks = {}
                    sc_toks = {}

                    def emit_outer(i):
                        qlo = qlo_of(i)
                        lead = min(256, NTOK - qlo)
                        @block.vector
                        def _(v, i=i, qlo=qlo, lead=lead, qrep_sb=qrep_sb):
                            v.tensor_scalar(ob_sb[i % 2][:, 0:lead],
                                            qrep_sb[:, qlo:qlo + lead],
                                            ksumT[:, i:i + 1], None,
                                            ALU.mult).then_inc(cv.h, 1)
                        os_toks[i] = cv.inc()


                    def emit_select(i):
                        r = i % 4
                        qlo = qlo_of(i)
                        lead = min(256, NTOK - qlo)
                        @block.vector
                        def _(v, i=i, r=r, lead=lead):
                            v.wait_ge(cv.h, os_toks[i])
                            v.copy_predicated(
                                mr_sb[r][:, 0:lead],
                                bandm_sb[:, 256 * r:256 * r + lead],
                                ob_sb[i % 2][:, 0:lead]
                            ).then_inc(cv.h, 1)
                        mr_toks[i] = cv.inc()

                    def emit_madd(i):
                        qlo = qlo_of(i)
                        lead = min(256, NTOK - qlo)
                        dlt = pc_delta(i)
                        @block.scalar
                        def _(s, i=i, qlo=qlo, lead=lead, dlt=dlt):
                            s.wait_ge(ct.h, sco_toks[i])
                            s.activation(madd2_sb[i % 2][:, 0:lead],
                                         PC[:, qlo + dlt:qlo + dlt + lead],
                                         AF.Copy).then_inc(cs.h, 1)
                        cp_tok = cs.inc()
                        @block.vector
                        def _(v, i=i, lead=lead, cp_tok=cp_tok):
                            v.wait_ge(cs.h, cp_tok)
                            v.wait_ge(cv.h, mr_toks[i])
                            v.tensor_tensor(
                                madd2_sb[i % 2][:, 0:lead],
                                madd2_sb[i % 2][:, 0:lead],
                                mr_sb[i % 4][:, 0:lead],
                                ALU.add).then_inc(cv.h, 1)
                        madd_toks[i] = cv.inc()

                    def emit_exp(i):
                        qlo = qlo_of(i)
                        lead = min(256, NTOK - qlo)
                        dlt = pc_delta(i)
                        toks = {}
                        def reg(a, b, tok):
                            for bank in range((a + dlt) // 512,
                                              (b - 1 + dlt) // 512 + 1):
                                toks[bank] = tok
                        for (a, b) in split_ranges(qlo + lead, NTOK,
                                                   (qlo + 512,)):
                            @block.scalar
                            def _(s, i=i, a=a, b=b, dlt=dlt):
                                s.wait_ge(ct.h, sco_toks[i])
                                s.activation(probs[i % 2][:, a:b],
                                             PC[:, a + dlt:b + dlt],
                                             AF.Exp).then_inc(cs.h, 1)
                            reg(a, b, cs.inc())
                        @block.scalar
                        def _(s, i=i, qlo=qlo, lead=lead):
                            s.wait_ge(cv.h, madd_toks[i])
                            s.activation(probs[i % 2][:, qlo:qlo + lead],
                                         madd2_sb[i % 2][:, 0:lead],
                                         AF.Exp).then_inc(cs.h, 1)
                        reg(qlo, qlo + lead, cs.inc())
                        exp_toks[i] = toks

                    def emit_iter(i, first_waits):
                        fw = tuple(first_waits)
                        @block.tensor
                        def _(t, i=i, fw=fw, st=st, ql=ql, h=h,
                              pbf=PF['pb']):
                            if i >= 2:
                                j = i - 2
                                qlo = qlo_of(j)
                                dlt = pc_delta(j)
                                if j == 0:
                                    t.wait_ge(pbf[0], pbf[1])
                                    if h == 0:
                                        t.wait_ge(cv.h, vtT_tok)
                                for (a, b) in split_ranges(qlo, NTOK, (512,)):
                                    t.wait_ge(cs.h,
                                              exp_toks[j][(a + dlt) // 512])
                                    last = (j == NT - 1
                                            or qlo_of(j + 1) >= b)
                                    mi = t.matmul(
                                        CV[0:65, a:b],
                                        vtT[:, 66 * j:66 * j + 65],
                                        probs[j % 2][:, a:b],
                                        start=(j == 0), stop=last)
                                mi.then_inc(ct.h, 1)
                                sc_toks[j] = ct.inc()
                            if i < NT:
                                qlo = qlo_of(i)
                                dlt = pc_delta(i)
                                for sem, val in fw:
                                    t.wait_ge(sem, val)
                                pieces = split_ranges(qlo, NTOK,
                                                      (qlo + 512,))
                                for (a, b) in pieces:
                                    bank = (a + dlt) // 512
                                    for back in (1, 2):
                                        pe = exp_toks.get(i - back)
                                        if pe is not None and bank in pe:
                                            t.wait_ge(cs.h, pe[bank])
                                            break
                                    mi = t.matmul(
                                        PC[:, a + dlt:b + dlt],
                                        kaug[0:65, 128 * i:128 * i + 128],
                                        ql[0:65, a:b],
                                        start=True, stop=True)
                                mi.then_inc(ct.h, 1)
                                sco_toks[i] = ct.inc()

                    for i in range(NT + 2):
                        if i == 1 and epi.get('pend') is not None:
                            ph, psc = epi.pop('pend')
                            s1 = emit_epi_sums(ph, 1, psc)
                            r1 = emit_epi_rep(ph, 1, s1[1])
                            co1 = emit_epi_out(ph, 1, r1, s1[0],
                                               PF['pdfree'])
                            PF['pdfree'] = co1
                            PF['pb'] = (cv.h, s1[0])
                            PF['pa'] = (cv.h, co1)
                        if h + 1 < H and 2 <= i <= 15:
                            emit_A_stage(h + 1, i - 2)
                        if i == 22:
                            epi['sums0'] = emit_epi_sums(h, 0, sc_toks[19])
                        elif i == 24:
                            epi['rep0'] = emit_epi_rep(h, 0,
                                                       epi['sums0'][1])
                        elif i == 25:
                            out0 = emit_epi_out(h, 0, epi['rep0'],
                                                epi['sums0'][0],
                                                PF['pdfree'])
                            PF['pdfree'] = out0
                        if i < NT:
                            emit_outer(i)
                            emit_select(i)
                        fw = ([(dsy.h, st['r64']), PF['pc']]
                              + ([(cv.h, vtT_tok)] if h == 0 else [])
                              ) if i == 0 else []
                        emit_iter(i, fw)
                        if i < NT:
                            emit_madd(i)
                            emit_exp(i)
                    sco_last[h] = sco_toks[NT - 1]
                    PF['pc'] = (cs.h, max(*exp_toks[NT - 2].values(),
                                          *exp_toks[NT - 1].values()))
                    if h + 1 < H:
                        epi['pend'] = (h, sc_toks[NT - 1])
                    else:
                        s1 = emit_epi_sums(h, 1, sc_toks[NT - 1])
                        r1 = emit_epi_rep(h, 1, s1[1])
                        ctxout_tok = emit_epi_out(h, 1, r1, s1[0],
                                                  PF['pdfree'])
                if DEBUG_DUMP:
                    D2 = NT * 66
                    @block.sync
                    def _(sy):
                        sy.wait_ge(cv.h, ctxout_tok)
                        sy.dma_start(out=dr['dbg1'][:, :],
                                     in_=bbuf[:, :]).then_inc(dsy.h, 16)
                        sy.dma_start(out=dr['dbg2'][:, 0:D2],
                                     in_=vtT[:, :]).then_inc(dsy.h, 16)
                        sy.dma_start(out=dr['dbg2'][0:64, D2:D2 + NTOK],
                                     in_=cv_sb[:, :]).then_inc(dsy.h, 16)
                        sy.dma_start(out=dr['dbg3'][:, 0:NTOK],
                                     in_=sums_sb[:, :]).then_inc(dsy.h, 16)
                        sy.dma_start(out=dr['dbg3'][:, NTOK:2 * NTOK],
                                     in_=rcp_row[:, :]).then_inc(dsy.h, 16)
                        sy.wait_ge(dsy.h, dsy.n + 80)
                    dsy.inc(80)
                load_fence(block, ('tensor', 'scalar'), cv.h, ctxout_tok)

        @block.gpsimd
        def _(g):
            g.wait_ge(ct.h, ct.n)
            g.wait_ge(cs.h, cs.n)
            g.wait_ge(cv.h, cv.n)
        @block.sync
        def _(sy):
            sy.wait_ge(ct.h, ct.n)
            sy.wait_ge(cv.h, cv.n)

        with contextlib.ExitStack() as f2:
            bufs['wslot'] = [f2.enter_context(
                nc.sbuf_tensor(f"wslotF{i}", [128, NCH * 512], BF16))
                for i in range(2)]
            xhat3 = f2.enter_context(
                nc.sbuf_tensor("xhat3", [128, NCH * NTOK], BF16))
            bufs['msum_w1'] = f2.enter_context(
                nc.sbuf_tensor("msum_w1f", [1, 2048], BF16))
            bufs['vsum'] = f2.enter_context(
                nc.sbuf_tensor("vsumf", [1, 512], BF16))
            mrr3 = f2.enter_context(nc.sbuf_tensor("mrr3", [1, NTOK], BF16))

            wo_toks = {}

            def wo_cb(m, pap, mm_tok):
                @block.vector
                def _(v, m=m, pap=pap, mm_tok=mm_tok):
                    v.wait_ge(ct.h, mm_tok)
                    v.tensor_add(x_chunks[m], x_chunks[m],
                                 pap[:, :]).then_inc(cv.h, 1)
                wo_toks[m] = cv.inc()
                return (cv.h, wo_toks[m])

            (_, wo_last), _sf = emit_big_proj(
                nc, block, bufs, S, dr['woq'], bbuf, wo_cb,
                extra_wait=(cv.h, ctxout_tok))

            zps2 = f2.enter_context(
                nc.psum_tensor("zps2", [128, NTOK], F32))
            _, ln3_ap, mr3 = emit_ln(
                nc, block, bufs, S, x_chunks, xhat3,
                wait0=[(cv.h, wo_toks[c]) for c in range(NCH)], mrr=mrr3)
            ffn2_res = emit_ffn(nc, block, bufs, S, x_chunks, xhat3,
                                dr['varr'], dr['uarr'],
                                rhs_ready=[(cv.h, t) for t in ln3_ap],
                                mrr=mrr3, mr_tok=mr3, first=True, zps=zps2)
            for qd in range(4):
                @block.sync
                def _(sy, qd=qd):
                    sy.wait_ge(cv.h, ffn2_res[4 * qd + 3])
                    sy.dma_start(
                        out=dr['yout'][:, 4 * NTOK * qd:4 * NTOK * (qd + 1)],
                        in_=x_res[:, 4 * NTOK * qd:4 * NTOK * (qd + 1)]
                    ).then_inc(dsy.h, 16)
                dsy.inc(16)
            @block.sync
            def _(sy):
                sy.wait_ge(dsy.h, dsy.n)
    return nc



_CACHE = {}


def _get_program():
    if 'f' not in _CACHE:
        _CACHE['f'] = build_fused()
    return _CACHE['f']


def _prep_weights(Wq, Wk, Wv, q2l, vup, Wo, U, V, W1, b1, W2, b2):
    w = {}
    w['w1q'] = arr_lhsT_q(W1).reshape(128, 4, NCH * 512)
    w['w2'] = arr_lhsT(W2)
    w['varr'] = arr_lhsT(np.ascontiguousarray(
        V.transpose(2, 0, 1).reshape(E, 8 * 64)))
    w['uarr'] = np.ascontiguousarray(
        U.reshape(4, 2, NCH, 128, 64).transpose(1, 4, 0, 2, 3)
        .reshape(128, 4 * NCH * 128))
    w['b1'] = np.ascontiguousarray(b1.reshape(NCH, 128).T)
    w['b2'] = np.ascontiguousarray(b2.reshape(8, 1))
    w['wkv'] = arr_lhsT(np.ascontiguousarray(
        np.concatenate([Wk.T, Wv.T], axis=1)))
    w['wqq'] = arr_lhsT_q(np.ascontiguousarray(Wq.T)).reshape(128, 4, NCH * 512)
    w['woq'] = arr_lhsT_q(np.ascontiguousarray(Wo.T)).reshape(128, 4, NCH * 512)
    w['q2l'] = np.ascontiguousarray(q2l.transpose(1, 0, 2).reshape(128, H * 64))
    w['vup'] = np.ascontiguousarray(vup.transpose(1, 0, 2).reshape(64, H * 128))
    k = np.arange(8)[:, None]
    jm = np.arange(4 * 128)[None, :]
    w['repsel'] = (k == 2 * (jm // 128) + (jm % 128) // 64).astype(np.float32)
    w['ident'] = np.eye(128, dtype=np.float32)
    w['ones'] = np.ones((128, 1), np.float32)
    w['onesr'] = np.ones((1, 128), np.float32)
    w['consts'] = np.array([[-0.5, 0.0]], np.float32)
    return w


def kernel(L0_attn_hidden_states, L0_attn_Wq, L0_attn_Wk, L0_attn_Wv,
           L0_attn_q_to_latent, L0_attn_v_up, L0_attn_Wo,
           L0_ffn_U, L0_ffn_V, L0_ffn_W1, L0_ffn_b1, L0_ffn_W2, L0_ffn_b2):
    x = np.asarray(L0_attn_hidden_states, np.float32)
    w = _prep_weights(
        np.asarray(L0_attn_Wq, np.float32), np.asarray(L0_attn_Wk, np.float32),
        np.asarray(L0_attn_Wv, np.float32),
        np.asarray(L0_attn_q_to_latent, np.float32),
        np.asarray(L0_attn_v_up, np.float32), np.asarray(L0_attn_Wo, np.float32),
        np.asarray(L0_ffn_U, np.float32), np.asarray(L0_ffn_V, np.float32),
        np.asarray(L0_ffn_W1, np.float32), np.asarray(L0_ffn_b1, np.float32),
        np.asarray(L0_ffn_W2, np.float32), np.asarray(L0_ffn_b2, np.float32))

    ncf = _get_program()
    shared = {k: w[k] for k in ('wqq', 'woq', 'q2l', 'vup', 'w1q', 'w2',
                                'varr', 'uarr', 'b1', 'b2', 'wkv', 'repsel',
                                'ident', 'ones', 'onesr', 'consts')}
    in_maps = []
    for c in range(8):
        b, p = divmod(c, 4)
        bandm, fixm = make_masks(p)
        maskneg = make_maskneg(p)
        kwidx = np.zeros((32, 1), np.int32)
        for j, T in enumerate(owned_tiles(p)):
            s0 = slot_of(p, max(0, T - 1))
            s1 = slot_of(p, T)
            kwidx[4 * j:4 * j + 4, 0] = (2 * s0, 2 * s0 + 1,
                                         2 * s1, 2 * s1 + 1)
        xall = np.concatenate([arrange_x(x[b], r) for r in blockorder(p)],
                              axis=1)
        m = dict(shared)
        m.update(xall=xall, bandm=bandm, fixm=fixm, maskneg=maskneg,
                 kwidx=kwidx)
        in_maps.append(m)
    res = run_bass_kernel_spmd(ncf, in_maps, core_ids=list(range(8)))

    out = np.zeros_like(x)
    for c in range(8):
        b, p = divmod(c, 4)
        y = np.asarray(res.results[c]['yout'], np.float32)
        out[b][owned_rows(p)] = unarrange_x(y)
    return out


# version 33 (final)
# speedup vs baseline: 1.1110x; 1.0001x over previous
import contextlib
import numpy as np
import ml_dtypes
import concourse.bass as bass
import concourse.mybir as mybir
from concourse.bass import IndirectOffsetOnAxis
from concourse.bass_utils import run_bass_kernel_spmd

F32 = mybir.dt.float32
BF16 = mybir.dt.bfloat16
I32 = mybir.dt.int32
U8 = mybir.dt.uint8
AF = mybir.ActivationFunctionType
ALU = mybir.AluOpType
AX = mybir.AxisListType

E = 2048
NCH = 16
NTOK = 1024
H, D, L = 16, 128, 64
NT = 32
NEG = -1e30
EPS = 1e-5


DEBUG_DUMP = False

_UID = [0]


def uid():
    _UID[0] += 1
    return _UID[0]


class Sem:
    def __init__(self, h):
        self.h = h
        self.n = 0

    def inc(self, k=1):
        self.n += k
        return self.n



def arr_lhsT(w):
    kin, mout = w.shape
    return np.ascontiguousarray(
        w.reshape(kin // 128, 128, mout).transpose(1, 0, 2)
        .reshape(128, (kin // 128) * mout))


def arr_lhsT_q(w):
    kin, mout = w.shape
    nk = kin // 128
    mq = mout // 4
    a = w.reshape(nk, 128, 4, mq).transpose(1, 2, 0, 3)
    return np.ascontiguousarray(a.reshape(128, 4, nk * mq))


def owned_tiles(p):
    return [p + 4 * j for j in range(8)]


def owned_rows(p):
    return np.concatenate(
        [np.arange(128 * t, 128 * t + 128) for t in owned_tiles(p)])


def blockorder(p):
    return [r for r in range(4) if r != p] + [p]


def slot_of(p, g):
    return 4 * (g // 4) + blockorder(p).index(g % 4)


def make_masks(p):
    sp = np.arange(128)[:, None]
    qc = np.arange(128)[None, :]
    bandm = np.zeros((128, 4 * 256), np.uint8)
    fixm = np.zeros((128, 4 * 256), np.float32)
    for jj, r in enumerate(blockorder(p)):
        for w in range(2):
            dt = (p - r) if w == 0 else (p + 4 - r)
            diff = 128 * dt + qc - sp
            band = (diff >= 0) & (diff <= 64)
            cols = slice(256 * jj + 128 * w, 256 * jj + 128 * w + 128)
            bandm[:, cols] = band.astype(np.uint8)
            fixm[:, cols] = np.where(diff < 0, NEG, 0.0)
    return bandm, fixm.astype(ml_dtypes.bfloat16)


def make_maskneg(p):
    i = np.arange(128)[:, None]
    jc = np.arange(128)[None, :]
    mn = np.full((128, 2 * 256), NEG, np.float32)
    for slot in range(2):
        for w in range(2):
            if slot == 0 and p == 0 and w == 0:
                continue
            diff = (128 - 128 * w) + i - jc
            band = (diff >= 0) & (diff <= 64)
            cols = slice(256 * slot + 128 * w, 256 * slot + 128 * w + 128)
            mn[:, cols] = np.where(band, 0.0, NEG)
    return mn


def arrange_x(xb, p):
    xt = xb[owned_rows(p)].T
    return np.ascontiguousarray(
        xt.reshape(NCH, 128, NTOK).transpose(1, 0, 2).reshape(128, NCH * NTOK))


def unarrange_x(y):
    return np.ascontiguousarray(
        y.reshape(128, NCH, NTOK).transpose(1, 0, 2).reshape(E, NTOK).T)


def qlo_of(s):
    return 128 * (s // 4)


def pc_delta(i):
    return 1024 * (i % 2) - qlo_of(i)


def split_ranges(lo, hi, cuts):
    pts = sorted({lo, hi} | {c for c in cuts if lo < c < hi})
    return list(zip(pts[:-1], pts[1:]))



def load_fence(block, engines, sem, val):
    for name in engines:
        @getattr(block, name)
        def _(e):
            e.wait_ge(sem, val)


SEM_NAMES = ('ct', 'cs', 'cv', 'cg', 'dsy', 'dgp', 'dconst', 'dlat',
             'dw0', 'dw1', 'dvu0', 'dvu1', 'dx')


def make_sems(nc, e):
    hs = [e(nc.semaphore(n)) for n in SEM_NAMES]
    d = {n: Sem(h) for n, h in zip(SEM_NAMES, hs)}
    return d


def emit_ln(nc, block, bufs, S, src_chunks, dst, wait0=None, mrr=None):
    w0 = wait0 if isinstance(wait0, list) else \
        ([wait0] + [None] * (NCH - 1) if wait0 is not None else [None] * NCH)
    ct, cs, cv = S['ct'], S['cs'], S['cv']
    onesb = bufs['ones_bf']
    u = uid()
    with contextlib.ExitStack() as ps:
        p_sum = ps.enter_context(nc.psum_tensor(f"ln_sum{u}", [1, NTOK], F32))
        p_sq = ps.enter_context(nc.psum_tensor(f"ln_sq{u}", [1, NTOK], F32))
        p_rrep = ps.enter_context(nc.psum_tensor(f"ln_rrep{u}", [128, NTOK], F32))
        sq2 = [ps.enter_context(nc.sbuf_tensor(f"ln_sq2_{u}0", [128, NTOK],
                                               BF16))] * 2
        m_sb = ps.enter_context(nc.sbuf_tensor(f"ln_m{u}", [1, NTOK], F32))
        t2 = ps.enter_context(nc.sbuf_tensor(f"ln_t2{u}", [1, NTOK], F32))
        rstd = ps.enter_context(nc.sbuf_tensor(f"ln_r{u}", [1, NTOK], F32))
        sq_toks = []
        mm_toks = []
        for c in range(NCH):
            par = c % 2
            @block.scalar
            def _(s, c=c, par=par):
                if w0[c] is not None:
                    s.wait_ge(w0[c][0], w0[c][1])
                if c >= 1:
                    s.wait_ge(ct.h, mm_toks[c - 1])
                s.activation(sq2[par][:, :], src_chunks[c],
                             AF.Square).then_inc(cs.h, 1)
            sq_toks.append(cs.inc())
            @block.tensor
            def _(t, c=c, par=par):
                if w0[c] is not None:
                    t.wait_ge(w0[c][0], w0[c][1])
                t.matmul(p_sum[:, 0:512], bufs['ones_f'][:, 0:1],
                         src_chunks[c][:, 0:512],
                         start=(c == 0), stop=(c == NCH - 1))
                t.matmul(p_sum[:, 512:1024], bufs['ones_f'][:, 0:1],
                         src_chunks[c][:, 512:1024],
                         start=(c == 0), stop=(c == NCH - 1))
                t.wait_ge(cs.h, sq_toks[c])
                t.matmul(p_sq[:, 0:512], onesb[:, 0:1], sq2[par][:, 0:512],
                         start=(c == 0), stop=(c == NCH - 1))
                t.matmul(p_sq[:, 512:1024], onesb[:, 0:1],
                         sq2[par][:, 512:1024],
                         start=(c == 0), stop=(c == NCH - 1)).then_inc(ct.h, 1)
            mm_toks.append(ct.inc())
        @block.vector
        def _(v):
            v.wait_ge(ct.h, mm_toks[-1])
            v.tensor_scalar(m_sb[:, :], p_sum[:, :], 1.0 / E,
                            None, ALU.mult).then_inc(cv.h, 1)
            v.wait_ge(cv.h, cv.n + 1)
            v.tensor_tensor(t2[:, :], m_sb[:, :], m_sb[:, :],
                            ALU.mult).then_inc(cv.h, 1)
            v.wait_ge(cv.h, cv.n + 2)
            v.tensor_scalar(t2[:, :], t2[:, :], EPS, None,
                            ALU.subtract).then_inc(cv.h, 1)
            v.wait_ge(cv.h, cv.n + 3)
            v.scalar_tensor_tensor(t2[:, :], p_sq[:, :], 1.0 / E, t2[:, :],
                                   ALU.mult, ALU.subtract).then_inc(cv.h, 1)
        cv.inc(3)
        var_tok = cv.inc()
        @block.scalar
        def _(s):
            s.wait_ge(cv.h, var_tok)
            s.activation(rstd[:, :], t2[:, :], AF.Ln).then_inc(cs.h, 1)
            s.wait_ge(cs.h, cs.n + 1)
            s.activation(t2[:, :], rstd[:, :], AF.Exp,
                         scale=-0.5).then_inc(cs.h, 1)
        cs.inc(1)
        rstd_tok = cs.inc()
        @block.vector
        def _(v):
            v.wait_ge(cs.h, rstd_tok)
            v.scalar_tensor_tensor(mrr[:, :], m_sb[:, :], -1.0, t2[:, :],
                                   ALU.mult, ALU.mult).then_inc(cv.h, 1)
        mr_tok = cv.inc()
        @block.tensor
        def _(t):
            t.wait_ge(cs.h, rstd_tok)
            t.matmul(p_rrep[:, 0:512], bufs['onesr_f'][:, :], t2[:, 0:512],
                     start=True, stop=True)
            t.matmul(p_rrep[:, 512:1024], bufs['onesr_f'][:, :],
                     t2[:, 512:1024], start=True, stop=True).then_inc(ct.h, 1)
        rep_tok = ct.inc()
        last = None
        apply_toks = []
        for c in range(NCH):
            @block.vector
            def _(v, c=c):
                if c == 0:
                    v.wait_ge(ct.h, rep_tok)
                v.tensor_mul(dst[:, NTOK * c:NTOK * (c + 1)], src_chunks[c],
                             p_rrep[:, :]).then_inc(cv.h, 1)
            last = cv.inc()
            apply_toks.append(last)
        load_fence(block, ('tensor', 'scalar'), cv.h, last)
        return last, apply_toks, mr_tok


def emit_big_proj(nc, block, bufs, S, wq_dram, rhs_sb, out_cb,
                  extra_wait=None, rhs_ready=None, fixup=None,
                  colsum=None):
    ct, cs, cv = S['ct'], S['cs'], S['cv']
    dw = [S['dw0'], S['dw1']]
    wslot = bufs['wslot']
    slot_free = [None] * 4
    u = uid()
    with contextlib.ExitStack() as ps:
        P = [ps.enter_context(nc.psum_tensor(f"bp{u}_p{i}", [128, NTOK], F32))
             for i in range(2)]
        cs_ps = ps.enter_context(nc.psum_tensor(f"bp{u}_cs", [1, 512], F32)) \
            if colsum is not None else None
        qload = [None] * 4
        cons_toks = {}

        entry_ct = ct.n
        def emit_load(q):
            @block.gpsimd
            def _(g, q=q):
                if q >= 2:
                    g.wait_ge(ct.h, slot_free[q - 2])
                else:
                    g.wait_ge(ct.h, entry_ct)
                g.dma_start(out=wslot[q % 2][:, :],
                            in_=wq_dram[:, q, :]).then_inc(dw[q % 2].h, 16)
            qload[q] = (dw[q % 2].h, dw[q % 2].inc(16))

        emit_load(0)
        emit_load(1)
        for q in range(4):
            if colsum is not None:
                msum_sb, cs_out = colsum
                @block.tensor
                def _(t, q=q):
                    t.wait_ge(qload[q][0], qload[q][1])
                    wv = wslot[q % 2].rearrange("p (k m) -> p k m", k=NCH)
                    for k in range(NCH):
                        mi = t.matmul(cs_ps[:, :], bufs['ones_bf'][:, 0:1],
                                      wv[:, k, :], start=(k == 0),
                                      stop=(k == NCH - 1))
                    mi.then_inc(ct.h, 1)
                csq_tok = ct.inc()
                @block.vector
                def _(v, q=q, csq_tok=csq_tok, msum_sb=msum_sb):
                    v.wait_ge(ct.h, csq_tok)
                    v.tensor_copy(msum_sb[:, 512 * q:512 * q + 512],
                                  cs_ps[:, :]).then_inc(cv.h, 1)
                cs_out.append(cv.inc())
            for mc in range(4):
                m = 4 * q + mc
                par = m % 2
                @block.tensor
                def _(t, q=q, mc=mc, m=m, par=par):
                    if mc == 0:
                        t.wait_ge(qload[q][0], qload[q][1])
                        if extra_wait is not None:
                            t.wait_ge(extra_wait[0], extra_wait[1])
                        if m == 0 and fixup is not None:
                            t.wait_ge(cv.h, fixup[3])
                    if m >= 2:
                        t.wait_ge(cons_toks[m - 2][0], cons_toks[m - 2][1])
                    wv = wslot[q % 2].rearrange("p (k m) -> p k m", k=NCH)
                    for k in range(NCH):
                        if m == 0 and rhs_ready is not None:
                            t.wait_ge(rhs_ready[k][0], rhs_ready[k][1])
                        lhsT = wv[:, k, 128 * mc:128 * mc + 128]
                        for half in range(2):
                            mi = t.matmul(
                                P[par][:, 512 * half:512 * half + 512],
                                lhsT,
                                rhs_sb[:, NTOK * k + 512 * half:
                                       NTOK * k + 512 * half + 512],
                                start=(k == 0),
                                stop=(fixup is None and k == NCH - 1))
                    if fixup is not None:
                        msum_sb, ms_toks, mrr, _mt = fixup
                        if ms_toks is not None and mc == 0:
                            t.wait_ge(cv.h, ms_toks[q])
                        lhsT = msum_sb[:, 128 * m:128 * m + 128]
                        for half in range(2):
                            mi = t.matmul(
                                P[par][:, 512 * half:512 * half + 512],
                                lhsT,
                                mrr[:, 512 * half:512 * half + 512],
                                start=False, stop=True)
                    mi.then_inc(ct.h, 1)
                mm_tok = ct.inc()
                slot_free[q] = mm_tok
                cons_toks[m] = out_cb(m, P[par], mm_tok)
            if q + 2 <= 3:
                emit_load(q + 2)
        last = cons_toks[NCH - 1]
        load_fence(block, ('tensor', 'scalar'), last[0], last[1])
        return last, slot_free


def emit_ffn(nc, block, bufs, S, x_chunks, xhat, varr_dram, uarr_dram,
             rhs_ready=None, mrr=None, mr_tok=None, first=False, zps=None):
    ct, cs, cv = S['ct'], S['cs'], S['cv']
    dvu = [S['dvu0'], S['dvu1']]
    h_sb = bufs['hbuf']
    wslot = bufs['wslot']

    def h_cb(m, pap, mm_tok):
        @block.scalar
        def _(s, m=m, pap=pap, mm_tok=mm_tok):
            s.wait_ge(ct.h, mm_tok)
            s.activation(h_sb[:, NTOK * m:NTOK * (m + 1)], pap[:, :],
                         AF.Relu, bias=bufs['b1_sb'][:, m:m + 1]).then_inc(cs.h, 1)
        return (cs.h, cs.inc())

    ms_toks = [] if first else None
    wlast, slot_free = emit_big_proj(
        nc, block, bufs, S, bufs['w1q_dram'], xhat, h_cb,
        rhs_ready=rhs_ready,
        fixup=(bufs['msum_w1'], ms_toks, mrr, mr_tok),
        colsum=(bufs['msum_w1'], ms_toks) if first else None)
    @block.gpsimd
    def _(g):
        g.wait_ge(ct.h, slot_free[2])
        g.dma_start(out=wslot[0][:, :], in_=varr_dram[:, :]).then_inc(dvu[0].h, 16)
    v_tok = (dvu[0].h, dvu[0].inc(16))
    @block.gpsimd
    def _(g):
        g.wait_ge(ct.h, slot_free[3])
        g.dma_start(out=wslot[1][:, :], in_=uarr_dram[:, :]).then_inc(dvu[1].h, 16)
    u_tok = (dvu[1].h, dvu[1].inc(16))

    u = uid()
    with contextlib.ExitStack() as ps:
        P = [ps.enter_context(nc.psum_tensor(f"fe{u}_p{i}", [128, NTOK], F32))
             for i in range(2)]
        p_z = zps[0:8, :]
        p_ztr = zps[0:128, 0:64]
        p_a8 = ps.enter_context(nc.psum_tensor(f"ffn_a8{u}", [8, NTOK], BF16))
        z_sb = ps.enter_context(nc.sbuf_tensor(f"ffn_zsb{u}", [8, NTOK], F32))
        zt = ps.enter_context(nc.sbuf_tensor(f"ffn_zt{u}", [128, 64], F32))
        nmax = ps.enter_context(nc.sbuf_tensor(f"ffn_nm{u}", [128, 8], F32))
        ez = ps.enter_context(nc.sbuf_tensor(f"ffn_ez{u}", [128, 64], F32))
        ssum = ps.enter_context(nc.sbuf_tensor(f"ffn_ss{u}", [128, 8], F32))
        rcp = ps.enter_context(nc.sbuf_tensor(f"ffn_rc{u}", [128, 8], F32))
        alpha_tm = ps.enter_context(nc.sbuf_tensor(f"ffn_al{u}", [128, 64], BF16))
        a8 = ps.enter_context(nc.sbuf_tensor(f"ffn_a8s{u}", [8, NTOK], BF16))
        vx = ps.enter_context(nc.sbuf_tensor(f"ffn_vx{u}", [128, 4 * NTOK], BF16))
        @block.tensor
        def _(t, pzw=bufs.get('prev_zt', 0), pkv=bufs.get('prev_kvs', 0)):
            t.wait_ge(cv.h, pzw)
            t.wait_ge(cv.h, pkv)
            w2v = bufs['w2_sb'].rearrange("p (k m) -> p k m", k=NCH)
            for k in range(NCH):
                for half in range(2):
                    mi = t.matmul(p_z[:, 512 * half:512 * half + 512],
                                  w2v[:, k, :],
                                  h_sb[:, NTOK * k + 512 * half:
                                       NTOK * k + 512 * half + 512],
                                  start=(k == 0), stop=(k == NCH - 1))
            mi.then_inc(ct.h, 1)
        z_tok = ct.inc()
        @block.vector
        def _(v):
            v.wait_ge(ct.h, z_tok)
            v.tensor_scalar(z_sb[:, :], p_z[:, :], bufs['b2_sb'][:, 0:1],
                            None, ALU.add).then_inc(cv.h, 1)
        zsb_tok = cv.inc()
        zsb_sem = cv
        @block.tensor
        def _(t):
            t.wait_ge(cv.h, zsb_tok)
            for blk in range(8):
                mi = t.transpose(p_ztr[:, 8 * blk:8 * blk + 8],
                                 z_sb[:, 128 * blk:128 * blk + 128],
                                 bufs['ident_f'][0:8, 0:8])
            mi.then_inc(ct.h, 1)
        ztr_tok = ct.inc()
        @block.vector
        def _(v):
            v.wait_ge(ct.h, ztr_tok)
            v.tensor_copy(zt[:, :], p_ztr[:, :]).then_inc(cv.h, 1)
        zt_tok = cv.inc()
        bufs['prev_zt'] = zt_tok
        @block.vector
        def _(v):
            v.wait_ge(cv.h, zt_tok)
            v.wait_ge(cv.h, zt_tok)
            v.reduce_max(nmax.rearrange("p (b o) -> p b o", o=1)[:, :, :],
                         zt.rearrange("p (b k) -> p b k", k=8)[:, :, :],
                         axis=AX.X, negate=True).then_inc(cv.h, 1)
        nmax_tok = cv.inc()
        @block.scalar
        def _(s):
            s.wait_ge(cv.h, nmax_tok)
            for b in range(8):
                inst = s.activation(ez[:, 8 * b:8 * b + 8],
                                    zt[:, 8 * b:8 * b + 8],
                                    AF.Exp, bias=nmax[:, b:b + 1])
            inst.then_inc(cs.h, 1)
        ez_tok = cs.inc()
        @block.vector
        def _(v):
            v.wait_ge(cs.h, ez_tok)
            v.reduce_sum(ssum.rearrange("p (b o) -> p b o", o=1)[:, :, :],
                         ez.rearrange("p (b k) -> p b k", k=8)[:, :, :],
                         axis=AX.X).then_inc(cv.h, 1)
            v.wait_ge(cv.h, cv.n + 1)
            v.reciprocal(rcp[:, :], ssum[:, :]).then_inc(cv.h, 1)
            v.wait_ge(cv.h, cv.n + 2)
            v.tensor_scalar(rcp[:, :], rcp[:, :], 0.5, None,
                            ALU.mult).then_inc(cv.h, 1)
            v.wait_ge(cv.h, cv.n + 3)
            for b in range(8):
                inst = v.tensor_scalar(alpha_tm[:, 8 * b:8 * b + 8],
                                       ez[:, 8 * b:8 * b + 8], rcp[:, b:b + 1],
                                       None, ALU.mult)
            inst.then_inc(cv.h, 1)
        cv.inc(3)
        alpha_tok = cv.inc()
        @block.tensor
        def _(t):
            t.wait_ge(cv.h, alpha_tok)
            for blk in range(8):
                mi = t.transpose(p_a8[:, 128 * blk:128 * blk + 128],
                                 alpha_tm[:, 8 * blk:8 * blk + 8],
                                 bufs['ident_bf'][:, :])
            mi.then_inc(ct.h, 1)
        a8_tok = ct.inc()
        @block.vector
        def _(v):
            v.wait_ge(ct.h, a8_tok)
            v.tensor_copy(a8[:, :], p_a8[:, :]).then_inc(cv.h, 1)
        a8sb_tok = cv.inc()
        vx_toks = {}
        vav = wslot[0].rearrange("p (k m) -> p k m", k=NCH)
        vsum_tok = None
        if first:
            @block.tensor
            def _(t, wlast=wlast):
                t.wait_ge(v_tok[0], v_tok[1])
                t.wait_ge(wlast[0], wlast[1])
                for k in range(NCH):
                    mi = t.matmul(P[0][0:1, 0:512], bufs['ones_bf'][:, 0:1],
                                  vav[:, k, :], start=(k == 0),
                                  stop=(k == NCH - 1))
                mi.then_inc(ct.h, 1)
            vs_mm = ct.inc()
            @block.vector
            def _(v, vs_mm=vs_mm):
                v.wait_ge(ct.h, vs_mm)
                v.tensor_copy(bufs['vsum'][:, :],
                              P[0][0:1, 0:512]).then_inc(cv.h, 1)
            vsum_tok = cv.inc()
        for j in range(4):
            par = j % 2
            @block.tensor
            def _(t, j=j, par=par, vsum_tok=vsum_tok):
                if j == 0:
                    t.wait_ge(v_tok[0], v_tok[1])
                    t.wait_ge(cv.h, mr_tok)
                    if vsum_tok is not None:
                        t.wait_ge(cv.h, vsum_tok)
                if j >= 2:
                    t.wait_ge(cv.h, vx_toks[j - 2])
                for k in range(NCH):
                    lhsT = vav[:, k, 128 * j:128 * j + 128]
                    for half in range(2):
                        mi = t.matmul(
                            P[par][:, 512 * half:512 * half + 512], lhsT,
                            xhat[:, NTOK * k + 512 * half:
                                 NTOK * k + 512 * half + 512],
                            start=(k == 0), stop=False)
                lhsT = bufs['vsum'][:, 128 * j:128 * j + 128]
                for half in range(2):
                    mi = t.matmul(P[par][:, 512 * half:512 * half + 512],
                                  lhsT,
                                  mrr[:, 512 * half:512 * half + 512],
                                  start=False, stop=True)
                mi.then_inc(ct.h, 1)
            vmm_tok = ct.inc()
            @block.vector
            def _(v, j=j, par=par, vmm_tok=vmm_tok):
                v.wait_ge(ct.h, vmm_tok)
                v.tensor_copy(vx[:, NTOK * j:NTOK * (j + 1)],
                              P[par][:, :]).then_inc(cv.h, 1)
            vx_toks[j] = cv.inc()
        avx_toks = {}
        for j in range(4):
            par = j % 2
            @block.tensor
            def _(t, j=j, par=par):
                if j == 0:
                    t.wait_ge(cv.h, a8sb_tok)
                    t.wait_ge(cv.h, vx_toks[3])
                else:
                    t.wait_ge(cv.h, avx_toks[j - 1])
                for half in range(2):
                    mi = t.matmul(P[par][:, 512 * half:512 * half + 512],
                                  bufs['repsel_sb'][:, 128 * j:128 * j + 128],
                                  a8[:, 512 * half:512 * half + 512],
                                  start=True, stop=True)
                mi.then_inc(ct.h, 1)
            arep_tok = ct.inc()
            @block.vector
            def _(v, j=j, par=par, arep_tok=arep_tok):
                v.wait_ge(ct.h, arep_tok)
                v.wait_ge(cv.h, vx_toks[j])
                v.tensor_mul(vx[:, NTOK * j:NTOK * (j + 1)],
                             vx[:, NTOK * j:NTOK * (j + 1)],
                             P[par][:, :]).then_inc(cv.h, 1)
            avx_toks[j] = cv.inc()
        res_toks = {}
        uav = wslot[1].rearrange("p (j k m) -> p j k m", j=4, k=NCH)
        for m in range(NCH):
            par = m % 2
            @block.tensor
            def _(t, m=m, par=par):
                if m == 0:
                    t.wait_ge(cv.h, avx_toks[3])
                    t.wait_ge(u_tok[0], u_tok[1])
                if m >= 2:
                    t.wait_ge(cv.h, res_toks[m - 2])
                for j in range(4):
                    lhsT = uav[:, j, m, :]
                    for half in range(2):
                        mi = t.matmul(
                            P[par][:, 512 * half:512 * half + 512], lhsT,
                            vx[:, NTOK * j + 512 * half:
                               NTOK * j + 512 * half + 512],
                            start=(j == 0), stop=(j == 3))
                mi.then_inc(ct.h, 1)
            umm_tok = ct.inc()
            @block.vector
            def _(v, m=m, par=par, umm_tok=umm_tok):
                v.wait_ge(ct.h, umm_tok)
                v.tensor_add(x_chunks[m], x_chunks[m],
                             P[par][:, :]).then_inc(cv.h, 1)
            res_toks[m] = cv.inc()
        load_fence(block, ('tensor', 'scalar'), cv.h, res_toks[NCH - 1])
        return res_toks


def alloc_common(nc, e):
    b = {}
    b['w2_sb'] = e(nc.sbuf_tensor("w2_sb", [128, NCH * 8], BF16))
    b['b1_sb'] = e(nc.sbuf_tensor("b1_sb", [128, NCH], F32))
    b['b2_sb'] = e(nc.sbuf_tensor("b2_sb", [8, 1], F32))
    b['repsel_sb'] = e(nc.sbuf_tensor("repsel_sb", [8, 4 * 128], BF16))
    b['ident_f'] = e(nc.sbuf_tensor("ident_f", [128, 128], F32))
    b['ident_bf'] = e(nc.sbuf_tensor("ident_bf", [128, 128], BF16))
    b['ones_f'] = e(nc.sbuf_tensor("ones_f", [128, 1], F32))
    b['ones_bf'] = e(nc.sbuf_tensor("ones_bf", [128, 1], BF16))
    b['onesr_f'] = e(nc.sbuf_tensor("onesr_f", [1, 128], F32))
    b['onesr_bf'] = e(nc.sbuf_tensor("onesr_bf", [1, 128], BF16))
    b['consts'] = e(nc.sbuf_tensor("consts_sb", [1, 2], F32))
    return b


def emit_common_loads(nc, block, bufs, dr, S):
    dsy, dconst = S['dsy'], S['dconst']
    @block.sync
    def _(sy):
        sy.dma_start(out=bufs['b1_sb'][:, :], in_=dr['b1'][:, :]).then_inc(dsy.h, 16)
        sy.dma_start(out=bufs['b2_sb'][:, :], in_=dr['b2'][:, :]).then_inc(dsy.h, 16)
        sy.dma_start(out=bufs['ident_f'][:, :], in_=dr['ident'][:, :]).then_inc(dsy.h, 16)
        sy.dma_start(out=bufs['ones_f'][:, :], in_=dr['ones'][:, :]).then_inc(dsy.h, 16)
        sy.dma_start(out=bufs['onesr_f'][:, :], in_=dr['onesr'][:, :]).then_inc(dsy.h, 16)
        sy.dma_start(out=bufs['consts'][:, :], in_=dr['consts'][:, :]).then_inc(dsy.h, 16)
    dsy.inc(16 * 6)
    @block.gpsimd
    def _(g):
        g.dma_start(out=bufs['w2_sb'][:, :], in_=dr['w2'][:, :]).then_inc(dconst.h, 16)
        g.dma_start(out=bufs['repsel_sb'][:, :], in_=dr['repsel'][:, :]).then_inc(dconst.h, 16)
        g.dma_start(out=bufs['ident_bf'][:, :], in_=dr['ident'][:, :]).then_inc(dconst.h, 16)
        g.dma_start(out=bufs['ones_bf'][:, :], in_=dr['ones'][:, :]).then_inc(dconst.h, 16)
        g.dma_start(out=bufs['onesr_bf'][:, :], in_=dr['onesr'][:, :]).then_inc(dconst.h, 16)
    dconst.inc(16 * 5)



def preload_set6(block):
    @block.scalar
    def _(s):
        s.add_instruction(mybir.InstLoadActFuncSet(
            name=s.bass.get_next_instruction_name(),
            act_func_set_id=6, ins=[], outs=[]))



def build_fused():
    nc = bass.Bass("TRN2", target_bir_lowering=False)
    dr = {}
    def di(name, shape, dt=F32):
        dr[name] = nc.dram_tensor(name, shape, dt, kind="ExternalInput").ap()
    di('xall', [128, 4 * NCH * NTOK])
    di('wkv', [128, NCH * 128])
    di('wqq', [128, 4, NCH * 512])
    di('woq', [128, 4, NCH * 512])
    di('q2l', [128, H * 64])
    di('vup', [64, H * 128])
    di('w1q', [128, 4, NCH * 512])
    di('w2', [128, NCH * 8])
    di('varr', [128, NCH * 512])
    di('uarr', [128, 4 * NCH * 128])
    di('b1', [128, NCH])
    di('b2', [8, 1])
    di('repsel', [8, 4 * 128])
    di('ident', [128, 128])
    di('ones', [128, 1])
    di('onesr', [1, 128])
    di('consts', [1, 2])
    di('bandm', [128, 4 * 256], U8)
    di('fixm', [128, 4 * 256], BF16)
    di('maskneg', [128, 2 * 256])
    di('kwidx', [32, 1], I32)
    dr['yout'] = nc.dram_tensor("yout", [128, NCH * NTOK], F32,
                                kind="ExternalOutput").ap()
    if DEBUG_DUMP:
        dr['dbg1'] = nc.dram_tensor("dbg1", [128, NCH * NTOK], BF16,
                                    kind="ExternalOutput").ap()
        dr['dbg2'] = nc.dram_tensor("dbg2", [128, NT * 66 + NTOK], BF16,
                                    kind="ExternalOutput").ap()
        dr['dbg3'] = nc.dram_tensor("dbg3", [1, 2 * NTOK], F32,
                                    kind="ExternalOutput").ap()
    ktok = nc.dram_tensor("ktok", [NT, 66, 128], BF16, kind="Internal").ap()
    kwin_d = nc.dram_tensor("kwin_d", [32, 33 * 128], BF16, kind="Internal").ap()
    crow_d = nc.dram_tensor("crow_d", [H, NTOK], F32, kind="Internal").ap()

    with contextlib.ExitStack() as stx:
        e = stx.enter_context
        x_res = e(nc.sbuf_tensor("x_res", [128, NCH * NTOK], F32))
        bbuf = e(nc.sbuf_tensor("bbuf", [128, NCH * NTOK], BF16))
        kaug = e(nc.sbuf_tensor("kaug", [66, 4096], BF16))
        vt_sb = e(nc.sbuf_tensor("vt_sb", [64, 4096], BF16))
        bufs = alloc_common(nc, e)
        bufs['hbuf'] = bbuf
        bufs['w1q_dram'] = dr['w1q']

        S = make_sems(nc, e)
        block = e(nc.Block())
        preload_set6(block)
        ct, cs, cv = S['ct'], S['cs'], S['cv']
        cg, dsy, dgp = S['cg'], S['dsy'], S['dgp']
        dlat, dx = S['dlat'], S['dx']

        emit_common_loads(nc, block, bufs, dr, S)
        x_chunks = [x_res[:, NTOK * c:NTOK * (c + 1)] for c in range(NCH)]
        xall4 = dr['xall'].rearrange("p (j n) -> p j n", j=4)

        with contextlib.ExitStack() as ph1:
            xhat = ph1.enter_context(
                nc.sbuf_tensor("xhat", [128, NCH * NTOK], BF16))
            kvout = ph1.enter_context(
                nc.sbuf_tensor("kvout", [128, NTOK], BF16))
            wkv_sb = ph1.enter_context(
                nc.sbuf_tensor("wkv_sb", [128, NCH * 128], BF16))
            bufs['msum_w1'] = ph1.enter_context(
                nc.sbuf_tensor("msum_w1", [1, 2048], BF16))
            bufs['vsum'] = ph1.enter_context(
                nc.sbuf_tensor("vsum", [1, 512], BF16))
            wkvsum = ph1.enter_context(
                nc.sbuf_tensor("wkvsum", [1, 128], BF16))
            mrr1 = ph1.enter_context(nc.sbuf_tensor("mrr1", [1, NTOK], BF16))
            mrr2 = mrr1
            bufs['wslot'] = [ph1.enter_context(
                nc.sbuf_tensor(f"wslotP{i}", [128, NCH * 512], BF16))
                for i in range(2)]
            @block.gpsimd
            def _(g):
                g.dma_start(out=wkv_sb[:, :],
                            in_=dr['wkv'][:, :]).then_inc(S['dconst'].h, 16)
            S['dconst'].inc(16)
            load_fence(block, ('tensor', 'scalar', 'vector'), dsy.h, dsy.n)
            load_fence(block, ('tensor', 'scalar', 'vector'),
                       S['dconst'].h, S['dconst'].n)

            with nc.psum_tensor("wkvs_ps", [1, 128], F32) as wkvs_ps:
                @block.tensor
                def _(t):
                    wv = wkv_sb.rearrange("p (k m) -> p k m", k=NCH)
                    for k in range(NCH):
                        mi = t.matmul(wkvs_ps[:, :], bufs['ones_bf'][:, 0:1],
                                      wv[:, k, :], start=(k == 0),
                                      stop=(k == NCH - 1))
                    mi.then_inc(ct.h, 1)
                wks_mm = ct.inc()
                @block.vector
                def _(v, wks_mm=wks_mm):
                    v.wait_ge(ct.h, wks_mm)
                    v.tensor_copy(wkvsum[:, :],
                                  wkvs_ps[:, :]).then_inc(cv.h, 1)
                wkvsum_tok = cv.inc()

            kaug512 = kaug[0:64, :].rearrange("p (t c) -> p t c", c=512)
            vt512 = vt_sb.rearrange("p (t c) -> p t c", c=512)
            kv512 = kvout.rearrange("p (t c) -> p t c", c=128)
            kv_ps = ph1.enter_context(
                nc.psum_tensor("kv_ps", [128, NTOK], F32))
            ln2_q = None
            kvs_tok = None
            for j in range(4):
                xq_toks = []
                for qd in range(4):
                    if ln2_q is None:
                        rel = wkvsum_tok if qd == 0 else None
                    else:
                        rel = ln2_q[4 * qd + 3]
                    @block.sync
                    def _(sy, j=j, qd=qd, rel=rel, dxn=dx.n):
                        sy.wait_ge(dx.h, dxn)
                        if rel is not None:
                            sy.wait_ge(cv.h, rel)
                        sy.dma_start(
                            out=x_res[:, 4 * NTOK * qd:4 * NTOK * (qd + 1)],
                            in_=xall4[:, j, 4 * NTOK * qd:4 * NTOK * (qd + 1)]
                        ).then_inc(dx.h, 16)
                    xq_toks.append(dx.inc(16))
                w0 = [(dx.h, xq_toks[c // 4]) for c in range(NCH)]
                _, ln1_ap, mr1 = emit_ln(nc, block, bufs, S, x_chunks, xhat,
                                         wait0=w0, mrr=mrr1)
                res_toks = emit_ffn(nc, block, bufs, S, x_chunks, xhat,
                                    dr['varr'], dr['uarr'],
                                    rhs_ready=[(cv.h, t) for t in ln1_ap],
                                    mrr=mrr1, mr_tok=mr1,
                                    first=(j == 0), zps=kv_ps)
                w2l = [(cv.h, res_toks[c]) for c in range(NCH)]
                _, ln2_q, mr2 = emit_ln(nc, block, bufs, S, x_chunks, xhat,
                                        wait0=w2l, mrr=mrr2)
                if True:
                    @block.tensor
                    def _(t, mr2=mr2, ln2_q=ln2_q, pzw=bufs['prev_zt']):
                        t.wait_ge(cv.h, mr2)
                        t.wait_ge(cv.h, pzw)
                        wv = wkv_sb.rearrange("p (k m) -> p k m", k=NCH)
                        for k in range(NCH):
                            t.wait_ge(cv.h, ln2_q[k])
                            for half in range(2):
                                mi = t.matmul(
                                    kv_ps[:, 512 * half:512 * half + 512],
                                    wv[:, k, :],
                                    xhat[:, NTOK * k + 512 * half:
                                         NTOK * k + 512 * half + 512],
                                    start=(k == 0), stop=False)
                        for half in range(2):
                            mi = t.matmul(
                                kv_ps[:, 512 * half:512 * half + 512],
                                wkvsum[:, :],
                                mrr2[:, 512 * half:512 * half + 512],
                                start=False, stop=True)
                        mi.then_inc(ct.h, 1)
                    kv_tok = ct.inc()
                    @block.vector
                    def _(v, kv_tok=kv_tok, dn=dlat.n):
                        v.wait_ge(ct.h, kv_tok)
                        v.wait_ge(dlat.h, dn)
                        v.tensor_copy(kvout[:, :],
                                      kv_ps[:, :]).then_inc(cv.h, 1)
                    kvs_tok = cv.inc()
                    bufs['prev_kvs'] = kvs_tok
                @block.gpsimd
                def _(g, j=j, kvs_tok=kvs_tok):
                    g.wait_ge(cv.h, kvs_tok)
                    g.dma_start(out=kaug512[:, :, 128 * j:128 * j + 128],
                                in_=kv512[0:64, :, :]).then_inc(dlat.h, 16)
                    g.dma_start(out=vt512[:, :, 128 * j:128 * j + 128],
                                in_=kv512[64:128, :, :]).then_inc(dlat.h, 16)
                dlat.inc(32)

            def qt_cb(m, pap, mm_tok):
                @block.vector
                def _(v, m=m, pap=pap, mm_tok=mm_tok):
                    v.wait_ge(ct.h, mm_tok)
                    v.tensor_copy(bbuf[:, NTOK * m:NTOK * (m + 1)],
                                  pap[:, :]).then_inc(cv.h, 1)
                return (cv.h, cv.inc())

            ms_wq = []
            (_, wq_last), _sf = emit_big_proj(
                nc, block, bufs, S, dr['wqq'], xhat, qt_cb,
                rhs_ready=[(cv.h, ln2_q[k]) for k in range(NCH)],
                fixup=(bufs['msum_w1'], ms_wq, mrr2, mr2),
                colsum=(bufs['msum_w1'], ms_wq))

        @block.gpsimd
        def _(g):
            g.wait_ge(ct.h, ct.n)
            g.wait_ge(cs.h, cs.n)
        @block.sync
        def _(sy):
            sy.wait_ge(ct.h, ct.n)
            sy.wait_ge(cs.h, cs.n)

        with contextlib.ExitStack() as attn_stack:
            ea = attn_stack.enter_context
            kwin_sb = ea(nc.sbuf_tensor("kwin_sb", [66, 16 * 128], BF16))
            ksum_bf = ea(nc.sbuf_tensor("ksum_bf", [1, 4096], BF16))
            km2 = ea(nc.sbuf_tensor("km2", [1, 1], F32))
            kmt = ea(nc.sbuf_tensor("kmt", [1, 1], F32))
            hlk = ea(nc.sbuf_tensor("hlk", [128, 1], F32))
            idxt = ea(nc.sbuf_tensor("idxt", [32, 1], I32))
            bandm_sb = ea(nc.sbuf_tensor("bandm_sb", [128, 4 * 256], U8))
            fixm_sb = ea(nc.sbuf_tensor("fixm_sb", [128, 4 * 256], BF16))
            maskneg_sb = ea(nc.sbuf_tensor("maskneg_sb", [128, 2 * 256], F32))
            q2l_sb = ea(nc.sbuf_tensor("q2l_sb", [128, H * 64], BF16))
            vup_sb = ea(nc.sbuf_tensor("vup_sb", [64, H * 128], BF16))
            qsh_bf = ea(nc.sbuf_tensor("qsh_bf", [1, NTOK], BF16))
            ksumT = ea(nc.sbuf_tensor("ksumT", [128, NT], F32))
            ksT32 = ea(nc.sbuf_tensor("ksT32", [NT, 128], F32))
            ob_sb = [ea(nc.sbuf_tensor(f"ob{i}", [128, 256], BF16))
                     for i in range(2)]
            cb_sb = ea(nc.sbuf_tensor("cb_sb", [1, NTOK], F32))
            cfin = ea(nc.sbuf_tensor("cfin", [1, NTOK], F32))
            crow_bf = ea(nc.sbuf_tensor("crow_bf", [1, NTOK], BF16))
            bm_row = ea(nc.sbuf_tensor("bm_row", [1, NTOK], F32))
            sums_sb = ea(nc.sbuf_tensor("sums_sb", [1, NTOK], F32))
            rep_sb = ea(nc.sbuf_tensor("rep_sb", [128, NTOK], F32))
            rcp_row = ea(nc.sbuf_tensor("rcp_row", [1, NTOK], F32))
            bmaxT = ea(nc.sbuf_tensor("bmaxT", [128, 8], F32))
            bmax8 = ea(nc.sbuf_tensor("bmax8", [8, 128], F32))
            sqh = ea(nc.sbuf_tensor("sqh", [64, NTOK], BF16))
            qlat_b = [ea(nc.sbuf_tensor(f"qlat_h{i}", [66, NTOK], BF16))
                      for i in range(2)]
            qshrep_b = [ea(nc.sbuf_tensor(f"qshrepb{i}", [128, NTOK], BF16))
                        for i in range(2)]
            vtT = ea(nc.sbuf_tensor("vtT", [128, NT * 66], BF16))
            cv_sb = ea(nc.sbuf_tensor("cv_sb", [64, NTOK], BF16))
            probs = [ea(nc.sbuf_tensor(f"probs{i}", [128, NTOK], BF16))
                     for i in range(2)]
            mr_sb = [ea(nc.sbuf_tensor(f"mr{i}", [128, 256], BF16))
                     for i in range(4)]
            madd_sb = [ea(nc.sbuf_tensor(f"madd{i}", [128, 256], F32))
                       for i in range(2)]
            madd2_sb = [ea(nc.sbuf_tensor(f"madd2_{i}", [128, 256], F32))
                        for i in range(2)]

            @block.sync
            def _(sy):
                sy.dma_start(out=idxt[:, :], in_=dr['kwidx'][:, :]).then_inc(dsy.h, 16)
                sy.dma_start(out=fixm_sb[:, :], in_=dr['fixm'][:, :]).then_inc(dsy.h, 16)
                sy.dma_start(out=maskneg_sb[:, :], in_=dr['maskneg'][:, :]).then_inc(dsy.h, 16)
            dsy.inc(16 * 3)
            @block.gpsimd
            def _(g):
                g.dma_start(out=q2l_sb[:, :], in_=dr['q2l'][:, :]).then_inc(dlat.h, 16)
                g.dma_start(out=vup_sb[:, :], in_=dr['vup'][:, :]).then_inc(dlat.h, 16)
                g.dma_start(out=bandm_sb[:, :], in_=dr['bandm'][:, :]).then_inc(dlat.h, 16)
            lat_tok = dlat.inc(16 * 3)
            @block.gpsimd
            def _(g):
                g.wait_ge(dlat.h, lat_tok)
                g.memset(kaug[64:65, :], 1.0).then_inc(cg.h, 1)
            ones_row_tok = cg.inc()
            load_fence(block, ('tensor', 'scalar', 'vector'), dsy.h, dsy.n)
            load_fence(block, ('tensor', 'scalar', 'vector'), dlat.h, lat_tok)

            ksum_evac = {}
            knmax = {}
            with contextlib.ExitStack() as kst:
                ksq = kst.enter_context(
                    nc.sbuf_tensor("ksq", [64, 4096], BF16))
                ks_ps = kst.enter_context(nc.psum_tensor("ks_ps", [1, 512], F32))
                kn_ps = kst.enter_context(nc.psum_tensor("kn_ps", [1, 512], F32))
                hb_ps = kst.enter_context(nc.psum_tensor("hb_ps", [128, 1], F32))
                @block.scalar
                def _(s):
                    s.activation(ksq[:, :], kaug[0:64, :],
                                 AF.Square).then_inc(cs.h, 1)
                ksq_tok = cs.inc()
                for cch in range(8):
                    @block.tensor
                    def _(t, cch=cch):
                        if cch == 0:
                            t.wait_ge(cs.h, ksq_tok)
                        else:
                            t.wait_ge(cv.h, ksum_evac[cch - 1])
                            t.wait_ge(cv.h, knmax[cch - 1])
                        t.matmul(ks_ps[:, :], bufs['ones_bf'][0:64, 0:1],
                                 kaug[0:64, 512 * cch:512 * cch + 512],
                                 start=True, stop=True)
                        t.matmul(kn_ps[:, :], bufs['ones_bf'][0:64, 0:1],
                                 ksq[:, 512 * cch:512 * cch + 512],
                                 start=True, stop=True).then_inc(ct.h, 1)
                    kmm = ct.inc()
                    @block.vector
                    def _(v, cch=cch, kmm=kmm):
                        v.wait_ge(ct.h, kmm)
                        v.tensor_copy(ksum_bf[:, 512 * cch:512 * cch + 512],
                                      ks_ps[:, :]).then_inc(cv.h, 1)
                    ksum_evac[cch] = cv.inc()
                    @block.vector
                    def _(v, cch=cch, kmm=kmm):
                        v.wait_ge(ct.h, kmm)
                        v.reduce_max(kmt[:, :], kn_ps[:, :],
                                     axis=AX.X).then_inc(cv.h, 1)
                        v.wait_ge(cv.h, cv.n + 1)
                        if cch == 0:
                            inst = v.tensor_copy(km2[:, :], kmt[:, :])
                        else:
                            inst = v.tensor_max(km2[:, :], km2[:, :], kmt[:, :])
                        inst.then_inc(cv.h, 1)
                    cv.inc(1)
                    knmax[cch] = cv.inc()
                @block.scalar
                def _(s):
                    s.wait_ge(cv.h, knmax[7])
                    s.wait_ge(cv.h, ksum_evac[7])
                    s.activation(kmt[:, :], km2[:, :], AF.Ln).then_inc(cs.h, 1)
                lnk_tok = cs.inc()
                @block.tensor
                def _(t):
                    t.wait_ge(cs.h, lnk_tok)
                    t.matmul(hb_ps[:, :], bufs['onesr_f'][:, :], kmt[:, :],
                             start=True, stop=True).then_inc(ct.h, 1)
                hb_tok = ct.inc()
                @block.vector
                def _(v):
                    v.wait_ge(ct.h, hb_tok)
                    v.tensor_scalar(hlk[:, :], hb_ps[:, :], 0.5, None,
                                    ALU.mult).then_inc(cv.h, 1)
                hlk_tok = cv.inc()
            @block.sync
            def _(sy):
                sy.wait_ge(cs.h, lnk_tok)
                sy.wait_ge(cg.h, ones_row_tok)
                sy.dma_start(out=kaug[65:66, :],
                             in_=ksum_bf[:, :]).then_inc(dsy.h, 16)
            r65_tok = dsy.inc(16)
            @block.sync
            def _(sy):
                sy.wait_ge(dsy.h, r65_tok)
                sy.dma_start(out=ktok.rearrange("a b c -> b a c"),
                             in_=kaug.rearrange("p (a c) -> p a c", c=128)
                             ).then_inc(dsy.h, 16)
            ktok_tok = dsy.inc(16)
            @block.gpsimd
            def _(g):
                g.wait_ge(dsy.h, ktok_tok)
                g.dma_start(out=ksT32[:, :],
                            in_=ktok[:, 65, :]).then_inc(dlat.h, 16)
            kst_tok = dlat.inc(16)
            with nc.sbuf_tensor("kwtmp", [32, 33 * 128], BF16) as kwtmp:
                @block.gpsimd
                def _(g):
                    g.wait_ge(dsy.h, ktok_tok)
                    g.indirect_dma_start(
                        kwtmp[:, :], None,
                        ktok.rearrange("a (h bb) c -> (a h) (bb c)", h=2),
                        IndirectOffsetOnAxis(ap=idxt[:, :], axis=0)
                    ).then_inc(dgp.h, 16)
                gath_tok = dgp.inc(16)
                @block.sync
                def _(sy):
                    sy.wait_ge(dgp.h, gath_tok)
                    sy.dma_start(out=kwin_d[:, :],
                                 in_=kwtmp[:, :]).then_inc(dsy.h, 16)
            kwd_tok = dsy.inc(16)
            @block.sync
            def _(sy):
                sy.wait_ge(dsy.h, kwd_tok)
                sy.dma_start(
                    out=kwin_sb.rearrange("p (a c) -> p a c", c=128),
                    in_=kwin_d.rearrange("(a h) (bb c) -> (h bb) a c",
                                         h=2, c=128)
                ).then_inc(dsy.h, 16)
            kwin_tok = dsy.inc(16)

            vt_tok = lat_tok

            with contextlib.ExitStack() as ps2:
                PA = ps2.enter_context(nc.psum_tensor("PA", [128, NTOK], F32))
                CV = ps2.enter_context(nc.psum_tensor("CV", [128, NTOK], F32))
                @block.tensor
                def _(t):
                    t.wait_ge(dlat.h, kst_tok)
                    t.transpose(PA[:, 0:32], ksT32[:, :],
                                bufs['ident_f'][0:32, 0:32]).then_inc(ct.h, 1)
                kstr_tok = ct.inc()
                @block.vector
                def _(v, kstr_tok=kstr_tok):
                    v.wait_ge(ct.h, kstr_tok)
                    v.tensor_copy(ksumT[:, :], PA[:, 0:32]).then_inc(cv.h, 1)
                ksumT_cv = cv.inc()
                @block.vector
                def _(v):
                    for r in range(4):
                        v.tensor_copy(mr_sb[r][:, :],
                                      fixm_sb[:, 256 * r:256 * r + 256]
                                      ).then_inc(cv.h, 1)
                cv.inc(4)
                with nc.psum_tensor("vtp", [128, NT * 64], BF16) as vtp:
                    @block.tensor
                    def _(t):
                        t.wait_ge(dlat.h, vt_tok)
                        for s in range(NT):
                            mi = t.transpose(vtp[:, 64 * s:64 * s + 64],
                                             vt_sb[:, 128 * s:128 * s + 128],
                                             bufs['ident_bf'][0:64, 0:64])
                        mi.then_inc(ct.h, 1)
                    vtp_tok = ct.inc()
                    @block.vector
                    def _(v, vtp_tok=vtp_tok):
                        v.wait_ge(ct.h, vtp_tok)
                        for s in range(NT):
                            v.tensor_copy(vtT[:, 66 * s:66 * s + 64],
                                          vtp[:, 64 * s:64 * s + 64])
                            mi = v.tensor_copy(
                                vtT[:, 66 * s + 64:66 * s + 65],
                                bufs['ones_bf'][:, 0:1])
                        mi.then_inc(cv.h, 1)
                    vtT_tok = cv.inc()
                PC = ps2.enter_context(
                    nc.psum_tensor("PC", [128, 2 * NTOK], F32))
                PF = {'pa': (cv.h, ksumT_cv), 'pb': (cv.h, 0),
                      'pc': (cv.h, 0), 'pdfree': 0, 'ctx_last': 0}
                sco_last = {-2: 0, -1: 0}
                A = {}
                ctxout_tok = 0
                ob64 = bufs['ones_bf'][0:64, 0:1]

                def emit_A_stage(h, k):
                    st = A.setdefault(h, {})
                    par = h % 2
                    ql = qlat_b[par]
                    pv = A.get(h - 1)
                    if k == 0:
                        ps = sco_last[h - 2]
                        pn = pv['nsq'] if pv else 0
                        st['ps'] = ps
                        @block.tensor
                        def _(t, h=h, paf=PF['pa']):
                            t.wait_ge(paf[0], paf[1])
                            if h == 0:
                                t.wait_ge(dsy.h, kwin_tok)
                            lhsT = q2l_sb[:, 64 * h:64 * h + 64]
                            t.matmul(PA[0:64, 0:512], lhsT,
                                     bbuf[:, NTOK * h:NTOK * h + 512],
                                     start=True, stop=True)
                            t.matmul(PA[0:64, 512:1024], lhsT,
                                     bbuf[:, NTOK * h + 512:NTOK * h + 1024],
                                     start=True, stop=True).then_inc(ct.h, 1)
                        qmm = ct.inc()
                        @block.vector
                        def _(v, qmm=qmm, ps=ps, pn=pn, ql=ql):
                            v.wait_ge(ct.h, qmm)
                            v.wait_ge(ct.h, ps)
                            v.wait_ge(ct.h, pn)
                            v.tensor_scalar(ql[0:64, :], PA[0:64, :], 0.125,
                                            None, ALU.mult).then_inc(cv.h, 1)
                            v.wait_ge(cv.h, cv.n + 1)
                            v.tensor_mul(sqh[:, :], ql[0:64, :],
                                         ql[0:64, :]).then_inc(cv.h, 1)
                        cv.inc(1)
                        st['qevac'] = cv.inc()
                        @block.gpsimd
                        def _(g, ps=ps, ql=ql):
                            g.wait_ge(ct.h, ps)
                            g.memset(ql[64:65, :], 0.0).then_inc(cg.h, 1)
                        st['z64'] = cg.inc()
                    elif k == 1:
                        @block.tensor
                        def _(t, qevac=st['qevac'], ql=ql):
                            t.wait_ge(cv.h, qevac)
                            t.matmul(PA[0:1, 0:512], ob64, ql[0:64, 0:512],
                                     start=True, stop=True)
                            t.matmul(PA[0:1, 512:1024], ob64,
                                     ql[0:64, 512:1024],
                                     start=True, stop=True).then_inc(ct.h, 1)
                        qs = ct.inc()
                        @block.vector
                        def _(v, qs=qs, pv=pv):
                            v.wait_ge(ct.h, qs)
                            if pv:
                                v.wait_ge(dsy.h, pv['r65'])
                            v.tensor_scalar(qsh_bf[:, :], PA[0:1, :], 0.5,
                                            None, ALU.mult).then_inc(cv.h, 1)
                        st['qshv'] = cv.inc()
                    elif k == 2:
                        @block.tensor
                        def _(t, qshv=st['qshv']):
                            t.wait_ge(cv.h, qshv)
                            t.matmul(PA[0:1, 0:512], ob64, sqh[:, 0:512],
                                     start=True, stop=True)
                            t.matmul(PA[0:1, 512:1024], ob64,
                                     sqh[:, 512:1024],
                                     start=True, stop=True).then_inc(ct.h, 1)
                        st['nsq'] = ct.inc()
                        @block.scalar
                        def _(s, h=h, nsq=st['nsq'], pv=pv):
                            s.wait_ge(ct.h, nsq)
                            if h == 0:
                                s.wait_ge(cv.h, hlk_tok)
                            if pv:
                                s.wait_ge(cv.h, pv['negc'])
                            s.activation(cfin[:, :], PA[0:1, :],
                                         AF.Ln).then_inc(cs.h, 1)
                            s.wait_ge(cs.h, cs.n + 1)
                            s.activation(cb_sb[:, :], cfin[:, :], AF.Exp,
                                         scale=0.5,
                                         bias=hlk[0:1, 0:1]).then_inc(cs.h, 1)
                        cs.inc(1)
                        st['cb'] = cs.inc()
                        @block.sync
                        def _(sy, qshv=st['qshv'], cbt=st['cb'], ql=ql):
                            sy.wait_ge(cv.h, qshv)
                            sy.wait_ge(cs.h, cbt)
                            sy.dma_start(out=ql[65:66, :],
                                         in_=qsh_bf[:, :]).then_inc(dsy.h, 16)
                        st['r65'] = dsy.inc(16)
                    elif k == 3:
                        @block.tensor
                        def _(t, cbt=st['cb']):
                            t.wait_ge(cs.h, cbt)
                            obr = bufs['onesr_bf'][:, :]
                            t.matmul(PA[:, 0:512], obr, qsh_bf[:, 0:512],
                                     start=True, stop=True)
                            t.matmul(PA[:, 512:1024], obr,
                                     qsh_bf[:, 512:1024],
                                     start=True, stop=True).then_inc(ct.h, 1)
                        qrep = ct.inc()
                        @block.vector
                        def _(v, qrep=qrep, par=par):
                            v.wait_ge(ct.h, qrep)
                            v.tensor_copy(qshrep_b[par][:, :],
                                          PA[:, :]).then_inc(cv.h, 1)
                        st['qrepv'] = cv.inc()
                        st['pp_evac'] = {}
                        st['ma'] = {}
                        st['bm'] = {}
                    elif 4 <= k <= 11:
                        j = k - 4
                        ppar = j % 2
                        pp_evac, ma_toks, bm_toks = \
                            st['pp_evac'], st['ma'], st['bm']
                        @block.tensor
                        def _(t, j=j, ppar=ppar, st=st, ql=ql):
                            if j == 0:
                                t.wait_ge(dsy.h, st['r65'])
                                t.wait_ge(cg.h, st['z64'])
                                t.wait_ge(cv.h, st['qrepv'])
                            if j >= 2:
                                t.wait_ge(cv.h, pp_evac[j - 2])
                            t.matmul(PA[:, 512 * ppar:512 * ppar + 256],
                                     ql[0:66, 128 * j:128 * j + 128],
                                     kwin_sb[0:66, 256 * j:256 * j + 256],
                                     start=True, stop=True).then_inc(ct.h, 1)
                        pp_tok = ct.inc()
                        slot = 0 if j == 0 else 1
                        @block.vector
                        def _(v, j=j, ppar=ppar, pp_tok=pp_tok, pv=pv,
                              slot=slot):
                            v.wait_ge(ct.h, pp_tok)
                            v.tensor_tensor(madd_sb[ppar][:, :],
                                            PA[:, 512 * ppar:512 * ppar + 256],
                                            maskneg_sb[:, 256 * slot:
                                                       256 * slot + 256],
                                            ALU.add).then_inc(cv.h, 1)
                        pp_evac[j] = cv.inc()
                        @block.vector
                        def _(v, j=j, ppar=ppar, pv=pv, pe=pp_evac[j]):
                            v.wait_ge(cv.h, pe)
                            if j < 2 and pv:
                                v.wait_ge(ct.h, pv['btr'])
                            v.reduce_max(bmaxT[:, j:j + 1],
                                         madd_sb[ppar][:, :],
                                         axis=AX.X).then_inc(cv.h, 1)
                        bm_toks[j] = cv.inc()
                    elif k == 12:
                        @block.tensor
                        def _(t, bm7=st['bm'][7]):
                            t.wait_ge(cv.h, bm7)
                            t.transpose(PA[0:8, 0:128], bmaxT[:, :],
                                        bufs['ident_f'][:, :]
                                        ).then_inc(ct.h, 1)
                        st['btr'] = ct.inc()
                        @block.vector
                        def _(v, btr=st['btr'], pv=pv):
                            v.wait_ge(ct.h, btr)
                            if pv:
                                v.wait_ge(dsy.h, pv['crow_out'])
                            v.tensor_copy(bmax8[:, :],
                                          PA[0:8, 0:128]).then_inc(cv.h, 1)
                        st['b8'] = cv.inc()
                        @block.sync
                        def _(sy, h=h, b8=st['b8']):
                            sy.wait_ge(cv.h, b8)
                            sy.dma_start(
                                out=crow_d[h:h + 1, :].rearrange(
                                    "o (a c) -> (o a) c", a=8),
                                in_=bmax8[:, :]).then_inc(dsy.h, 16)
                        st['crow_out'] = dsy.inc(16)
                        @block.sync
                        def _(sy, co=st['crow_out'], pv=pv):
                            sy.wait_ge(dsy.h, co)
                            if pv:
                                sy.wait_ge(cv.h, pv['negc'])
                            sy.dma_start(out=bm_row[:, :],
                                         in_=crow_d[h:h + 1, :]
                                         ).then_inc(dsy.h, 16)
                        st['bmrow'] = dsy.inc(16)
                    elif k == 13:
                        @block.vector
                        def _(v, bmrow=st['bmrow'], cbt=st['cb'], pv=pv):
                            v.wait_ge(dsy.h, bmrow)
                            v.wait_ge(cs.h, cbt)
                            if pv:
                                v.wait_ge(dsy.h, pv['r64'])
                            v.tensor_max(cfin[:, :], cb_sb[:, :],
                                         bm_row[:, :]).then_inc(cv.h, 1)
                            v.wait_ge(cv.h, cv.n + 1)
                            v.tensor_scalar(crow_bf[:, :], cfin[:, :],
                                            -1.004, -0.5, ALU.mult,
                                            ALU.add).then_inc(cv.h, 1)
                        cv.inc(1)
                        st['negc'] = cv.inc()
                        @block.sync
                        def _(sy, negc=st['negc'], ql=ql):
                            sy.wait_ge(cv.h, negc)
                            sy.dma_start(out=ql[64:65, :],
                                         in_=crow_bf[:, :]).then_inc(dsy.h, 16)
                        st['r64'] = dsy.inc(16)

                def emit_epi_sums(h, half, sc_tok):
                    lo, hi = 512 * half, 512 * half + 512
                    @block.vector
                    def _(v, sc_tok=sc_tok, lo=lo, hi=hi,
                          cl=PF['ctx_last']):
                        v.wait_ge(ct.h, sc_tok)
                        v.wait_ge(ct.h, cl)
                        v.tensor_copy(cv_sb[:, lo:hi],
                                      CV[0:64, lo:hi]).then_inc(cv.h, 1)
                        v.tensor_copy(sums_sb[:, lo:hi],
                                      CV[64:65, lo:hi]).then_inc(cv.h, 1)
                    cv.inc(1)
                    s_tok = cv.inc()
                    @block.scalar
                    def _(s, s_tok=s_tok, lo=lo, hi=hi):
                        s.wait_ge(cv.h, s_tok)
                        s.activation(rcp_row[:, lo:hi], sums_sb[:, lo:hi],
                                     AF.Ln).then_inc(cs.h, 1)
                        s.wait_ge(cs.h, cs.n + 1)
                        s.activation(rcp_row[:, lo:hi], rcp_row[:, lo:hi],
                                     AF.Exp, scale=-1.0).then_inc(cs.h, 1)
                    cs.inc(1)
                    return (s_tok, cs.inc())

                def emit_epi_rep(h, half, rcp_tok):
                    lo, hi = 512 * half, 512 * half + 512
                    nxt = A.get(h + 1)
                    ve_war = nxt['b8'] if nxt else 0
                    @block.tensor
                    def _(t, rcp_tok=rcp_tok, ve_war=ve_war, lo=lo, hi=hi,
                          pdf=PF['pdfree']):
                        t.wait_ge(cs.h, rcp_tok)
                        t.wait_ge(cv.h, ve_war)
                        t.wait_ge(cv.h, pdf)
                        t.matmul(PA[:, lo:hi], bufs['onesr_f'][:, :],
                                 rcp_row[:, lo:hi], start=True,
                                 stop=True).then_inc(ct.h, 1)
                    rep_tok = ct.inc()
                    @block.vector
                    def _(v, rep_tok=rep_tok, lo=lo, hi=hi):
                        v.wait_ge(ct.h, rep_tok)
                        v.tensor_copy(rep_sb[:, lo:hi],
                                      PA[:, lo:hi]).then_inc(cv.h, 1)
                    return (rep_tok, cv.inc())

                def emit_epi_out(h, half, rep, cvsb_tok, pd_free):
                    lo, hi = 512 * half, 512 * half + 512
                    lo2 = 512 * (1 - half)
                    nxt = A.get(h + 1)
                    b8w = nxt['b8'] if (nxt and half == 0) else 0
                    @block.tensor
                    def _(t, h=h, cvsb_tok=cvsb_tok, pd_free=pd_free,
                          lo=lo, hi=hi, lo2=lo2, b8w=b8w):
                        t.wait_ge(cv.h, cvsb_tok)
                        t.wait_ge(cv.h, pd_free)
                        t.wait_ge(cv.h, b8w)
                        t.matmul(PA[:, lo2:lo2 + 512],
                                 vup_sb[:, 128 * h:128 * h + 128],
                                 cv_sb[:, lo:hi], start=True,
                                 stop=True).then_inc(ct.h, 1)
                    ctx_tok = ct.inc()
                    PF['ctx_last'] = ctx_tok
                    @block.vector
                    def _(v, h=h, rep=rep, ctx_tok=ctx_tok, lo=lo, hi=hi,
                          lo2=lo2):
                        v.wait_ge(cv.h, rep[1])
                        v.wait_ge(ct.h, ctx_tok)
                        v.tensor_mul(bbuf[:, NTOK * h + lo:NTOK * h + hi],
                                     rep_sb[:, lo:hi],
                                     PA[:, lo2:lo2 + 512]).then_inc(cv.h, 1)
                    return cv.inc()

                epi = {}
                for k in range(14):
                    emit_A_stage(0, k)
                PF['pa'] = (cv.h, A[0]['b8'])
                for h in range(H):
                    st = A[h]
                    par = h % 2
                    ql = qlat_b[par]
                    qrep_sb = qshrep_b[par]
                    os_toks = {}
                    sco_toks = {}
                    mr_toks = {}
                    madd_toks = {}
                    exp_toks = {}
                    sc_toks = {}

                    def emit_outer(i):
                        qlo = qlo_of(i)
                        lead = min(256, NTOK - qlo)
                        @block.vector
                        def _(v, i=i, qlo=qlo, lead=lead, qrep_sb=qrep_sb):
                            v.tensor_scalar(ob_sb[i % 2][:, 0:lead],
                                            qrep_sb[:, qlo:qlo + lead],
                                            ksumT[:, i:i + 1], None,
                                            ALU.mult).then_inc(cv.h, 1)
                        os_toks[i] = cv.inc()


                    def emit_select(i):
                        r = i % 4
                        qlo = qlo_of(i)
                        lead = min(256, NTOK - qlo)
                        @block.vector
                        def _(v, i=i, r=r, lead=lead):
                            v.wait_ge(cv.h, os_toks[i])
                            v.copy_predicated(
                                mr_sb[r][:, 0:lead],
                                bandm_sb[:, 256 * r:256 * r + lead],
                                ob_sb[i % 2][:, 0:lead]
                            ).then_inc(cv.h, 1)
                        mr_toks[i] = cv.inc()

                    def emit_madd(i):
                        qlo = qlo_of(i)
                        lead = min(256, NTOK - qlo)
                        dlt = pc_delta(i)
                        @block.scalar
                        def _(s, i=i, qlo=qlo, lead=lead, dlt=dlt):
                            s.wait_ge(ct.h, sco_toks[i])
                            s.activation(madd2_sb[i % 2][:, 0:lead],
                                         PC[:, qlo + dlt:qlo + dlt + lead],
                                         AF.Copy).then_inc(cs.h, 1)
                        cp_tok = cs.inc()
                        @block.vector
                        def _(v, i=i, lead=lead, cp_tok=cp_tok):
                            v.wait_ge(cs.h, cp_tok)
                            v.wait_ge(cv.h, mr_toks[i])
                            v.tensor_tensor(
                                madd2_sb[i % 2][:, 0:lead],
                                madd2_sb[i % 2][:, 0:lead],
                                mr_sb[i % 4][:, 0:lead],
                                ALU.add).then_inc(cv.h, 1)
                        madd_toks[i] = cv.inc()

                    def emit_exp(i):
                        qlo = qlo_of(i)
                        lead = min(256, NTOK - qlo)
                        dlt = pc_delta(i)
                        toks = {}
                        def reg(a, b, tok):
                            for bank in range((a + dlt) // 512,
                                              (b - 1 + dlt) // 512 + 1):
                                toks[bank] = tok
                        for (a, b) in split_ranges(qlo + lead, NTOK, ()):
                            @block.scalar
                            def _(s, i=i, a=a, b=b, dlt=dlt):
                                s.wait_ge(ct.h, sco_toks[i])
                                s.activation(probs[i % 2][:, a:b],
                                             PC[:, a + dlt:b + dlt],
                                             AF.Exp).then_inc(cs.h, 1)
                            reg(a, b, cs.inc())
                        @block.scalar
                        def _(s, i=i, qlo=qlo, lead=lead):
                            s.wait_ge(cv.h, madd_toks[i])
                            s.activation(probs[i % 2][:, qlo:qlo + lead],
                                         madd2_sb[i % 2][:, 0:lead],
                                         AF.Exp).then_inc(cs.h, 1)
                        reg(qlo, qlo + lead, cs.inc())
                        exp_toks[i] = toks

                    def emit_iter(i, first_waits):
                        fw = tuple(first_waits)
                        @block.tensor
                        def _(t, i=i, fw=fw, st=st, ql=ql, h=h,
                              pbf=PF['pb']):
                            if i >= 2:
                                j = i - 2
                                qlo = qlo_of(j)
                                dlt = pc_delta(j)
                                if j == 0:
                                    t.wait_ge(pbf[0], pbf[1])
                                    if h == 0:
                                        t.wait_ge(cv.h, vtT_tok)
                                for (a, b) in split_ranges(qlo, NTOK, (512,)):
                                    t.wait_ge(cs.h,
                                              exp_toks[j][(a + dlt) // 512])
                                    last = (j == NT - 1
                                            or qlo_of(j + 1) >= b)
                                    mi = t.matmul(
                                        CV[0:65, a:b],
                                        vtT[:, 66 * j:66 * j + 65],
                                        probs[j % 2][:, a:b],
                                        start=(j == 0), stop=last)
                                mi.then_inc(ct.h, 1)
                                sc_toks[j] = ct.inc()
                            if i < NT:
                                qlo = qlo_of(i)
                                dlt = pc_delta(i)
                                for sem, val in fw:
                                    t.wait_ge(sem, val)
                                pieces = split_ranges(qlo, NTOK,
                                                      (qlo + 512,))
                                for (a, b) in pieces:
                                    bank = (a + dlt) // 512
                                    for back in (1, 2):
                                        pe = exp_toks.get(i - back)
                                        if pe is not None and bank in pe:
                                            t.wait_ge(cs.h, pe[bank])
                                            break
                                    mi = t.matmul(
                                        PC[:, a + dlt:b + dlt],
                                        kaug[0:65, 128 * i:128 * i + 128],
                                        ql[0:65, a:b],
                                        start=True, stop=True)
                                mi.then_inc(ct.h, 1)
                                sco_toks[i] = ct.inc()

                    for i in range(NT + 2):
                        if i == 1 and epi.get('pend') is not None:
                            ph, psc = epi.pop('pend')
                            s1 = emit_epi_sums(ph, 1, psc)
                            r1 = emit_epi_rep(ph, 1, s1[1])
                            co1 = emit_epi_out(ph, 1, r1, s1[0],
                                               PF['pdfree'])
                            PF['pdfree'] = co1
                            PF['pb'] = (cv.h, s1[0])
                            PF['pa'] = (cv.h, co1)
                        if h + 1 < H and 2 <= i <= 15:
                            emit_A_stage(h + 1, i - 2)
                        if i == 22:
                            epi['sums0'] = emit_epi_sums(h, 0, sc_toks[19])
                        elif i == 24:
                            epi['rep0'] = emit_epi_rep(h, 0,
                                                       epi['sums0'][1])
                        elif i == 25:
                            out0 = emit_epi_out(h, 0, epi['rep0'],
                                                epi['sums0'][0],
                                                PF['pdfree'])
                            PF['pdfree'] = out0
                        if i < NT:
                            emit_outer(i)
                            emit_select(i)
                        fw = ([(dsy.h, st['r64']), PF['pc']]
                              + ([(cv.h, vtT_tok)] if h == 0 else [])
                              ) if i == 0 else []
                        emit_iter(i, fw)
                        if i < NT:
                            emit_madd(i)
                            emit_exp(i)
                    sco_last[h] = sco_toks[NT - 1]
                    PF['pc'] = (cs.h, max(*exp_toks[NT - 2].values(),
                                          *exp_toks[NT - 1].values()))
                    if h + 1 < H:
                        epi['pend'] = (h, sc_toks[NT - 1])
                    else:
                        s1 = emit_epi_sums(h, 1, sc_toks[NT - 1])
                        r1 = emit_epi_rep(h, 1, s1[1])
                        ctxout_tok = emit_epi_out(h, 1, r1, s1[0],
                                                  PF['pdfree'])
                if DEBUG_DUMP:
                    D2 = NT * 66
                    @block.sync
                    def _(sy):
                        sy.wait_ge(cv.h, ctxout_tok)
                        sy.dma_start(out=dr['dbg1'][:, :],
                                     in_=bbuf[:, :]).then_inc(dsy.h, 16)
                        sy.dma_start(out=dr['dbg2'][:, 0:D2],
                                     in_=vtT[:, :]).then_inc(dsy.h, 16)
                        sy.dma_start(out=dr['dbg2'][0:64, D2:D2 + NTOK],
                                     in_=cv_sb[:, :]).then_inc(dsy.h, 16)
                        sy.dma_start(out=dr['dbg3'][:, 0:NTOK],
                                     in_=sums_sb[:, :]).then_inc(dsy.h, 16)
                        sy.dma_start(out=dr['dbg3'][:, NTOK:2 * NTOK],
                                     in_=rcp_row[:, :]).then_inc(dsy.h, 16)
                        sy.wait_ge(dsy.h, dsy.n + 80)
                    dsy.inc(80)
                load_fence(block, ('tensor', 'scalar'), cv.h, ctxout_tok)

        @block.gpsimd
        def _(g):
            g.wait_ge(ct.h, ct.n)
            g.wait_ge(cs.h, cs.n)
            g.wait_ge(cv.h, cv.n)
        @block.sync
        def _(sy):
            sy.wait_ge(ct.h, ct.n)
            sy.wait_ge(cv.h, cv.n)

        with contextlib.ExitStack() as f2:
            bufs['wslot'] = [f2.enter_context(
                nc.sbuf_tensor(f"wslotF{i}", [128, NCH * 512], BF16))
                for i in range(2)]
            xhat3 = f2.enter_context(
                nc.sbuf_tensor("xhat3", [128, NCH * NTOK], BF16))
            bufs['msum_w1'] = f2.enter_context(
                nc.sbuf_tensor("msum_w1f", [1, 2048], BF16))
            bufs['vsum'] = f2.enter_context(
                nc.sbuf_tensor("vsumf", [1, 512], BF16))
            mrr3 = f2.enter_context(nc.sbuf_tensor("mrr3", [1, NTOK], BF16))

            wo_toks = {}

            def wo_cb(m, pap, mm_tok):
                @block.vector
                def _(v, m=m, pap=pap, mm_tok=mm_tok):
                    v.wait_ge(ct.h, mm_tok)
                    v.tensor_add(x_chunks[m], x_chunks[m],
                                 pap[:, :]).then_inc(cv.h, 1)
                wo_toks[m] = cv.inc()
                return (cv.h, wo_toks[m])

            (_, wo_last), _sf = emit_big_proj(
                nc, block, bufs, S, dr['woq'], bbuf, wo_cb,
                extra_wait=(cv.h, ctxout_tok))

            zps2 = f2.enter_context(
                nc.psum_tensor("zps2", [128, NTOK], F32))
            _, ln3_ap, mr3 = emit_ln(
                nc, block, bufs, S, x_chunks, xhat3,
                wait0=[(cv.h, wo_toks[c]) for c in range(NCH)], mrr=mrr3)
            ffn2_res = emit_ffn(nc, block, bufs, S, x_chunks, xhat3,
                                dr['varr'], dr['uarr'],
                                rhs_ready=[(cv.h, t) for t in ln3_ap],
                                mrr=mrr3, mr_tok=mr3, first=True, zps=zps2)
            for qd in range(4):
                @block.sync
                def _(sy, qd=qd):
                    sy.wait_ge(cv.h, ffn2_res[4 * qd + 3])
                    sy.dma_start(
                        out=dr['yout'][:, 4 * NTOK * qd:4 * NTOK * (qd + 1)],
                        in_=x_res[:, 4 * NTOK * qd:4 * NTOK * (qd + 1)]
                    ).then_inc(dsy.h, 16)
                dsy.inc(16)
            @block.sync
            def _(sy):
                sy.wait_ge(dsy.h, dsy.n)
    return nc



_CACHE = {}


def _get_program():
    if 'f' not in _CACHE:
        _CACHE['f'] = build_fused()
    return _CACHE['f']


def _prep_weights(Wq, Wk, Wv, q2l, vup, Wo, U, V, W1, b1, W2, b2):
    w = {}
    w['w1q'] = arr_lhsT_q(W1).reshape(128, 4, NCH * 512)
    w['w2'] = arr_lhsT(W2)
    w['varr'] = arr_lhsT(np.ascontiguousarray(
        V.transpose(2, 0, 1).reshape(E, 8 * 64)))
    w['uarr'] = np.ascontiguousarray(
        U.reshape(4, 2, NCH, 128, 64).transpose(1, 4, 0, 2, 3)
        .reshape(128, 4 * NCH * 128))
    w['b1'] = np.ascontiguousarray(b1.reshape(NCH, 128).T)
    w['b2'] = np.ascontiguousarray(b2.reshape(8, 1))
    w['wkv'] = arr_lhsT(np.ascontiguousarray(
        np.concatenate([Wk.T, Wv.T], axis=1)))
    w['wqq'] = arr_lhsT_q(np.ascontiguousarray(Wq.T)).reshape(128, 4, NCH * 512)
    w['woq'] = arr_lhsT_q(np.ascontiguousarray(Wo.T)).reshape(128, 4, NCH * 512)
    w['q2l'] = np.ascontiguousarray(q2l.transpose(1, 0, 2).reshape(128, H * 64))
    w['vup'] = np.ascontiguousarray(vup.transpose(1, 0, 2).reshape(64, H * 128))
    k = np.arange(8)[:, None]
    jm = np.arange(4 * 128)[None, :]
    w['repsel'] = (k == 2 * (jm // 128) + (jm % 128) // 64).astype(np.float32)
    w['ident'] = np.eye(128, dtype=np.float32)
    w['ones'] = np.ones((128, 1), np.float32)
    w['onesr'] = np.ones((1, 128), np.float32)
    w['consts'] = np.array([[-0.5, 0.0]], np.float32)
    return w


def kernel(L0_attn_hidden_states, L0_attn_Wq, L0_attn_Wk, L0_attn_Wv,
           L0_attn_q_to_latent, L0_attn_v_up, L0_attn_Wo,
           L0_ffn_U, L0_ffn_V, L0_ffn_W1, L0_ffn_b1, L0_ffn_W2, L0_ffn_b2):
    x = np.asarray(L0_attn_hidden_states, np.float32)
    w = _prep_weights(
        np.asarray(L0_attn_Wq, np.float32), np.asarray(L0_attn_Wk, np.float32),
        np.asarray(L0_attn_Wv, np.float32),
        np.asarray(L0_attn_q_to_latent, np.float32),
        np.asarray(L0_attn_v_up, np.float32), np.asarray(L0_attn_Wo, np.float32),
        np.asarray(L0_ffn_U, np.float32), np.asarray(L0_ffn_V, np.float32),
        np.asarray(L0_ffn_W1, np.float32), np.asarray(L0_ffn_b1, np.float32),
        np.asarray(L0_ffn_W2, np.float32), np.asarray(L0_ffn_b2, np.float32))

    ncf = _get_program()
    shared = {k: w[k] for k in ('wqq', 'woq', 'q2l', 'vup', 'w1q', 'w2',
                                'varr', 'uarr', 'b1', 'b2', 'wkv', 'repsel',
                                'ident', 'ones', 'onesr', 'consts')}
    in_maps = []
    for c in range(8):
        b, p = divmod(c, 4)
        bandm, fixm = make_masks(p)
        maskneg = make_maskneg(p)
        kwidx = np.zeros((32, 1), np.int32)
        for j, T in enumerate(owned_tiles(p)):
            s0 = slot_of(p, max(0, T - 1))
            s1 = slot_of(p, T)
            kwidx[4 * j:4 * j + 4, 0] = (2 * s0, 2 * s0 + 1,
                                         2 * s1, 2 * s1 + 1)
        xall = np.concatenate([arrange_x(x[b], r) for r in blockorder(p)],
                              axis=1)
        m = dict(shared)
        m.update(xall=xall, bandm=bandm, fixm=fixm, maskneg=maskneg,
                 kwidx=kwidx)
        in_maps.append(m)
    res = run_bass_kernel_spmd(ncf, in_maps, core_ids=list(range(8)))

    out = np.zeros_like(x)
    for c in range(8):
        b, p = divmod(c, 4)
        y = np.asarray(res.results[c]['yout'], np.float32)
        out[b][owned_rows(p)] = unarrange_x(y)
    return out
